# version 20
# baseline (speedup 1.0000x reference)
import numpy as np
from contextlib import ExitStack

import concourse.bass as bass
import concourse.mybir as mybir
from concourse.bass_utils import run_bass_kernel_spmd

F32 = mybir.dt.float32
F16 = mybir.dt.float16
ALU = mybir.AluOpType
ACT = mybir.ActivationFunctionType
AX = mybir.AxisListType

CFG = dict(S=2048, NPROBE=5)


def _config(S=2048, NPROBE=5):
    CFG["S"] = S
    CFG["NPROBE"] = NPROBE


class Buf:
    __slots__ = ("w", "r")

    def __init__(self):
        self.w = None
        self.r = {}


class Tile:
    __slots__ = ("h", "b")

    def __init__(self, h):
        self.h = h
        self.b = Buf()


class Prog:
    ENGS = ("t", "v", "s", "y", "g")
    KDMA = 4

    def __init__(self, nc, es):
        self.nc = nc
        self.semids = ["t", "v", "s"] + [f"y{i}" for i in range(self.KDMA)] \
            + [f"g{i}" for i in range(self.KDMA)]
        self.sem = {sid: es.enter_context(nc.semaphore("sem_" + sid))
                    for sid in self.semids}
        self.cnt = {sid: 0 for sid in self.semids}
        self.dman = {"y": 0, "g": 0}
        self.seen = {e: {sid: 0 for sid in self.semids} for e in self.ENGS}
        self.q = {e: [] for e in self.ENGS}
        self.engobj = dict(t=nc.tensor, v=nc.vector, s=nc.scalar, y=nc.sync,
                           g=nc.gpsimd)

    def do(self, e, fn, r=(), w=(), dma=False):
        need = {}

        def add(src):
            if src is None:
                return
            f, c = src
            if e == "t" and f == "t":
                return
            if c > need.get(f, 0):
                need[f] = c

        for b in r:
            add(b.w)
        for b in w:
            add(b.w)
            for f, c in b.r.items():
                add((f, c))
        if dma:
            n = self.dman[e]
            sid = f"{e}{n % self.KDMA}"
            self.dman[e] = n + 1
            if 16 * (n // self.KDMA) > need.get(sid, 0):
                need[sid] = 16 * (n // self.KDMA)
            inc = 16
        else:
            sid = e
            inc = 1
        waits = []
        for f, c in need.items():
            if self.seen[e][f] < c:
                self.seen[e][f] = c
                waits.append((f, c))
        self.cnt[sid] += inc
        mycnt = self.cnt[sid]

        def clo(eo=self.engobj[e], waits=tuple(waits), fn=fn, sem=self.sem[sid],
                inc=inc, sems=self.sem):
            for f, c in waits:
                eo.wait_ge(sems[f], c)
            fn(eo).then_inc(sem, inc)

        self.q[e].append(clo)
        src = (sid, mycnt)
        for b in r:
            if b.r.get(sid, 0) < mycnt:
                b.r[sid] = mycnt
        for b in w:
            b.w = src
            b.r = {}
        return src

    def raw(self, e, fn):
        self.q[e].append(lambda eo=self.engobj[e], fn=fn: fn(eo))

    def barrier(self):
        snap = dict(self.cnt)
        sems = self.sem
        for e in self.ENGS:
            waits = []
            for sid in self.semids:
                if sid != e and self.seen[e][sid] < snap[sid]:
                    self.seen[e][sid] = snap[sid]
                    waits.append((sid, snap[sid]))

            def clo(eo=self.engobj[e], waits=tuple(waits)):
                for f, c in waits:
                    eo.wait_ge(sems[f], c)

            self.q[e].append(clo)


def build():
    S = CFG["S"]
    NPROBE = CFG["NPROBE"]
    D = 1024
    NCH = 8
    H = 16
    DH = 64
    KK = S // 4
    NQT = S // 128
    NG = S // 512
    QTH = NQT // 2
    HF = 8
    INNER = 512
    NW = S // 512
    HIDP = 2816
    NM1 = 2 * HIDP // 128
    NK2 = HIDP // 128
    SD = 2
    HLV = max(1, S // 1024)

    nc = bass.Bass("TRN2", target_bir_lowering=False)
    es = ExitStack()
    P = Prog(nc, es)

    def dram(name, shape, dt, kind):
        return Tile(nc.dram_tensor(name, shape, dt, kind=kind).ap())

    def sb(name, shape, dt):
        return Tile(es.enter_context(nc.sbuf_tensor(name, shape, dt)))

    def psum(name, shape, dt=F32):
        return Tile(es.enter_context(nc.psum_tensor(name, shape, dt)))

    xT = dram("xT", [D, S], F32, "ExternalInput")
    Wq = dram("Wq", [NCH * 128, D], F32, "ExternalInput")
    Wk = dram("Wk", [NCH * 128, D], F32, "ExternalInput")
    Wv = dram("Wv", [NCH * 128, D], F32, "ExternalInput")
    Wo = dram("Wo", [NCH * 128, D], F32, "ExternalInput")
    Wg = dram("Wg", [NCH * 128, D], F32, "ExternalInput")
    bqk_c = dram("bqk_c", [2, 128, NCH], F32, "ExternalInput")
    bg_c = dram("bg_c", [128, NCH], F32, "ExternalInput")
    bo_r = dram("bo_r", [1, D], F32, "ExternalInput")
    bv_r = dram("bv_r", [1, D], F32, "ExternalInput")
    Wt_r = dram("Wt_r", [128, NCH], F32, "ExternalInput")
    bt_s = dram("bt_s", [1, 1], F32, "ExternalInput")
    wqkv_qk = dram("wqkv_qk", [2 * 8 * 128, 1024], F32, "ExternalInput")
    wqkv_v = dram("wqkv_v", [2 * HF * 128, 512], F32, "ExternalInput")
    fwo = dram("fwo", [2 * 8 * 128, 512], F32, "ExternalInput")
    w1p = dram("w1p", [2 * NM1 * 128, 1024], F32, "ExternalInput")
    b1p_c = dram("b1p_c", [2, 128, NM1], F32, "ExternalInput")
    w2p = dram("w2p", [2, HIDP, D], F32, "ExternalInput")
    b2_r = dram("b2_r", [2, 1, D], F32, "ExternalInput")
    lng_c = dram("lng_c", [2, 128, NCH], F32, "ExternalInput")
    lnb_c = dram("lnb_c", [2, 128, NCH], F32, "ExternalInput")
    qsc_c = dram("qsc_c", [2, 128, 1], F32, "ExternalInput")
    ksc_c = dram("ksc_c", [2, 128, 1], F32, "ExternalInput")
    rotT = dram("rotT", [4, 128, S], F16, "ExternalInput")
    E2_d = dram("E2_d", [128, 33], mybir.dt.float8e4, "ExternalInput")
    madd_d = dram("madd_d", [128, 8 * 512], mybir.dt.float8e5, "ExternalInput")

    out_T = dram("out_T", [D, S], F32, "ExternalOutput")
    s_scr = dram("s_scr", [D, S], F16, "Internal")
    sp_scr = dram("sp_scr", [D, S], F16, "Internal")

    F8 = mybir.dt.float8e4

    def view(pack, p0, p1):
        t = Tile.__new__(Tile)
        t.h = pack.h[p0:p1, :]
        t.b = pack.b
        return t

    def cview(pack, c0, w):
        t = Tile.__new__(Tile)
        t.h = pack.h[:, c0:c0 + w]
        t.b = Buf()
        return t

    xn16 = [sb(f"xn16_{c}", [128, S + SD], F16) for c in range(NCH)]
    P8 = [sb(f"P8_{i}", [128, S + SD], F16) for i in range(8)]
    q16p = sb("q16p", [128, S + SD], F16)
    k16p = sb("k16p", [128, S + SD], F16)
    q16pB = sb("q16pB", [128, S + SD], F16)
    k16pB = sb("k16pB", [128, S + SD], F16)
    sq16 = sb("sq16", [128, S + SD], F16)
    dump8 = sb("dump8", [128, S], F8)
    s16 = P8
    h2t = [sb(f"h2t_{k}", [128, 1024], F16) for k in range(NK2)]
    wbig = [sb(f"wbig_{i}", [128, D], F16) for i in range(11)]
    wq16 = wbig[0:2]
    wk16 = wbig[2:4]
    wv16 = wbig[4:6]
    wo16 = wbig[6:8]
    wg16 = wbig[8:10]
    wqkv16 = [wbig[10], wbig[0]]
    w1ab = sb("w1ab", [128, D], F16)
    w1gb = sb("w1gb", [128, D], F16)
    fwo16 = [sb(f"fwo16_{i}", [128, INNER], F16) for i in range(2)]
    rot2 = [sb(f"rot2_{i}", [128, 512], F16) for i in range(2)]
    yT = [sb(f"yT_{k}", [128, 512], F16) for k in range(3)]
    vhat = [[sb(f"vh_{hh}_{t}", [128, DH + 1], F16) for t in range(NQT)]
            for hh in range(2)]
    E2 = sb("E2", [128, 33], mybir.dt.float8e4)
    madd = sb("madd", [128, 8 * 512], mybir.dt.float8e5)
    onesP = sb("onesP", [128, 512], F16)
    ones_c = sb("ones_c", [128, 1], F16)
    epsc = sb("epsc", [128, 1], F32)
    clampc = sb("clampc", [128, 1], F32)
    ln8c = sb("ln8c", [128, 1], F32)
    bias_q = sb("bias_q", [128, NCH], F32)
    bias_k = sb("bias_k", [128, NCH], F32)
    bias_g = sb("bias_g", [128, NCH], F32)
    b1c = sb("b1c", [128, NM1], F32)
    lngc = sb("lngc", [128, NCH], F32)
    lnbc = sb("lnbc", [128, NCH], F32)
    qscol = sb("qscol", [128, 1], F32)
    kscol = sb("kscol", [128, 1], F32)
    RW = [sb(f"RW_{i}", [128, S], F32) for i in range(2)]
    rrowA = view(RW[0], 0, 33)
    srow = view(RW[0], 0, 1)
    s2row = view(RW[1], 0, 1)
    murow = srow
    rstdrow = s2row
    R16 = [sb(f"R16_{i}", [128, S], F16) for i in range(2)]
    rrow16A = view(R16[0], 0, 33)
    rrow16 = [view(R16[0], 0, 1), view(R16[0], 32, 33)]
    murow16 = view(R16[0], 0, 1)
    rstdrow16 = view(R16[1], 0, 1)
    throw = view(R16[1], 64, 65)

    brows = sb("brows", [128, D], F16)
    borow = view(brows, 0, 1)
    bvrow = view(brows, 32, 33)
    b2row = view(brows, 64, 65)
    thb = sb("thb", [128, 512], F16)
    tcol16 = cview(thb, 0, 1)
    rd16 = view(thb, 64, 65)
    th16t = cview(thb, 8, QTH)
    wtmp2 = sb("wtmp2", [128, 512], F32)
    tmp32a = sb("tmp32a", [128, 512], F32)
    rd32 = view(tmp32a, 64, 65)
    tmp16a = sb("tmp16a", [128, 512], F16)
    tmp16b = sb("tmp16b", [128, 512], F16)
    gel16 = thb
    rdb16 = view(tmp16b, 64, 128)
    itemp = sb("itemp", [1, 1], F32)
    lnit = sb("lnit", [128, 1], F32)
    scal1 = cview(wtmp2, 0, 1)
    scal16 = cview(thb, 16, 1)
    tau0c = sb("tau0c", [128, 1], F32)
    irho0c = sb("irho0c", [128, 1], F32)
    _stnames = ("tauA", "tauB", "cA", "cB", "lo", "hi", "bt", "bd", "d1",
                "t1", "t2", "t3", "tnA", "tnB")
    _stpack = sb("stpack", [128, 14 * QTH], F32)

    st = {n: cview(_stpack, i * QTH, QTH) for i, n in enumerate(_stnames)}
    stc = {n: (cview(_stpack, _stnames.index(n) * QTH, QTH // 2),
               cview(_stpack, _stnames.index(n) * QTH + QTH // 2, QTH - QTH // 2))
           for n in ("cA", "cB")}
    _mpack = sb("mpack", [128, 2 * QTH], mybir.dt.uint8)
    msk = cview(_mpack, 0, QTH)
    mskn = cview(_mpack, QTH, QTH)
    meanT = cview(_stpack, 0, NCH)
    wtT = cview(_stpack, NCH, NCH)
    tcol = cview(_stpack, 2 * NCH, 1)

    PP1 = psum("PP1", [128, 1024])
    PP2 = psum("PP2", [128, 1024])
    PQ1 = psum("PQ1", [128, 512])
    PQ2 = psum("PQ2", [128, 512])
    PQ3 = psum("PQ3", [128, 512])
    PP1s = [Buf(), Buf()]
    PP2s = [Buf(), Buf()]

    DATA = slice(SD, SD + S)
    SH = slice(SD - 1, SD - 1 + S)

    def dslc(a, b2):
        return slice(SD + a, SD + b2)

    x16 = s16
    for c in range(NCH):
        P.do("g", lambda eo, c=c: eo.dma_start(
            out=x16[c].h[:, DATA], in_=xT.h[c * 128:(c + 1) * 128, :]),
            r=[xT.b], w=[x16[c].b], dma=True)
    for src_d, dst in ((E2_d, E2), (madd_d, madd), (bg_c, bias_g), (Wt_r, wtT)):
        P.do("y", lambda eo, src_d=src_d, dst=dst: eo.dma_start(
            out=dst.h[:, :], in_=src_d.h), r=[src_d.b], w=[dst.b], dma=True)
    for i, dst in ((0, bias_q), (1, bias_k)):
        P.do("y", lambda eo, i=i, dst=dst: eo.dma_start(
            out=dst.h[:, :], in_=bqk_c.h[i]), r=[bqk_c.b], w=[dst.b], dma=True)
    for src_d, dst in ((bo_r, borow), (bv_r, bvrow)):
        P.do("g", lambda eo, src_d=src_d, dst=dst: eo.dma_start(
            out=dst.h[:, :], in_=src_d.h), r=[src_d.b], w=[dst.b], dma=True)
    P.do("v", lambda eo: eo.memset(ones_c.h[:, :], 1.0), w=[ones_c.b])
    P.do("v", lambda eo: eo.memset(onesP.h[:, :], 1.0), w=[onesP.b])
    P.do("v", lambda eo: eo.memset(epsc.h[:, :], 1e-5), w=[epsc.b])
    P.do("v", lambda eo: eo.memset(clampc.h[:, :], 1e-18), w=[clampc.b])
    P.do("v", lambda eo: eo.memset(ln8c.h[:, :], float(np.log(8.0))), w=[ln8c.b])
    P.do("v", lambda eo: eo.memset(st["t1"].h[:, :], 0.0), w=[st["t1"].b])
    for hh in range(2):
        for t in range(NQT):
            P.do("v", lambda eo, hh=hh, t=t: eo.memset(
                vhat[hh][t].h[:, DH:DH + 1], 1.0), w=[vhat[hh][t].b])
    for c in range(NCH):
        P.do("v", lambda eo, c=c: eo.memset(xn16[c].h[:, 0:SD], 0.0), w=[xn16[c].b])

    def ln_stats(slices):
        for piece in range(NG):
            cols = [slice(a.start + piece * 512, a.start + (piece + 1) * 512)
                    for _, a in slices]
            for c in range(NCH):
                t, _ = slices[c]
                P.do("t", lambda eo, t=t, c=c, cc=cols[c]: eo.matmul(
                    PQ1.h[0:1, :], ones_c.h[:, :], t.h[:, cc],
                    start=(c == 0), stop=(c == NCH - 1)), r=[t.b, ones_c.b], w=[PQ1.b])
            P.do("s", lambda eo, piece=piece: eo.activation(
                srow.h[0:1, piece * 512:(piece + 1) * 512], PQ1.h[0:1, :],
                ACT.Identity), r=[PQ1.b], w=[srow.b])
            for c in range(NCH):
                t, _ = slices[c]
                P.do("v", lambda eo, t=t, cc=cols[c]: eo.tensor_tensor(
                    out=sq16.h[:, 0:512], in0=t.h[:, cc], in1=t.h[:, cc],
                    op=ALU.mult), r=[t.b], w=[sq16.b])
                P.do("t", lambda eo, c=c: eo.matmul(
                    PQ2.h[0:1, :], ones_c.h[:, :], sq16.h[:, 0:512],
                    start=(c == 0), stop=(c == NCH - 1)),
                    r=[sq16.b, ones_c.b], w=[PQ2.b])
            P.do("s", lambda eo, piece=piece: eo.activation(
                s2row.h[0:1, piece * 512:(piece + 1) * 512], PQ2.h[0:1, :],
                ACT.Identity), r=[PQ2.b], w=[s2row.b])

    def ln_rows():
        P.do("v", lambda eo: eo.tensor_scalar(out=murow.h[0:1, :], in0=srow.h[0:1, :],
             scalar1=1.0 / D, scalar2=None, op0=ALU.mult), r=[srow.b], w=[murow.b])
        for jh in range(S // 1024):
            hs = slice(jh * 1024, (jh + 1) * 1024)
            P.do("v", lambda eo, hs=hs: eo.tensor_tensor(
                out=PP1.h[0:1, :], in0=murow.h[0:1, hs],
                in1=murow.h[0:1, hs], op=ALU.mult), r=[murow.b], w=[PP1.b])
            P.do("v", lambda eo, hs=hs: eo.tensor_scalar(
                out=rstdrow.h[0:1, hs], in0=s2row.h[0:1, hs],
                scalar1=1.0 / D, scalar2=None, op0=ALU.mult),
                r=[s2row.b], w=[rstdrow.b])
            P.do("v", lambda eo, hs=hs: eo.tensor_tensor(
                out=rstdrow.h[0:1, hs], in0=rstdrow.h[0:1, hs],
                in1=PP1.h[0:1, :], op=ALU.subtract),
                r=[PP1.b, rstdrow.b], w=[rstdrow.b])
        P.do("s", lambda eo: eo.activation(rstdrow.h[0:1, :], rstdrow.h[0:1, :],
             ACT.Ln, bias=epsc.h[0:1, :]), r=[rstdrow.b, epsc.b], w=[rstdrow.b])
        P.do("s", lambda eo: eo.activation(rstdrow.h[0:1, :], rstdrow.h[0:1, :],
             ACT.Exp, scale=-0.5), r=[rstdrow.b], w=[rstdrow.b])
        P.do("v", lambda eo: eo.tensor_tensor(out=murow16.h[0:1, :],
             in0=murow.h[0:1, :], in1=rstdrow.h[0:1, :], op=ALU.mult),
             r=[murow.b, rstdrow.b], w=[murow16.b])
        P.do("v", lambda eo: eo.tensor_copy(rstdrow16.h[0:1, :], rstdrow.h[0:1, :]),
             r=[rstdrow.b], w=[rstdrow16.b])

    def bcast_to(pp, row16, half):
        c0 = half * 1024
        for j in range(2):
            P.do("t", lambda eo, j=j, c0=c0, pp=pp, row16=row16: eo.matmul(
                pp.h[:, j * 512:(j + 1) * 512], onesP.h[0:1, 0:128],
                row16.h[0:1, c0 + j * 512:c0 + (j + 1) * 512], start=True, stop=True),
                r=[onesP.b, row16.b], w=[pp.b])

    def ssq_rows(sqtile):
        for piece in range(NG):
            PQ = PQ2 if piece % 2 == 0 else PQ3
            P.do("t", lambda eo, piece=piece, PQ=PQ, sqtile=sqtile:
                 eo.matmul(PQ.h[0:33, :], E2.h[:, :],
                           sqtile.h[:, piece * 512:(piece + 1) * 512],
                           start=True, stop=True), r=[sqtile.b, E2.b], w=[PQ.b])
            eng = "s" if piece % 2 == 0 else "v"
            if eng == "s":
                P.do("s", lambda eo, piece=piece, PQ=PQ: eo.activation(
                    rrowA.h[:, piece * 512:(piece + 1) * 512], PQ.h[0:33, :],
                    ACT.Identity), r=[PQ.b], w=[rrowA.b])
            else:
                P.do("v", lambda eo, piece=piece, PQ=PQ: eo.tensor_copy(
                    rrowA.h[:, piece * 512:(piece + 1) * 512], PQ.h[0:33, :]),
                    r=[PQ.b], w=[rrowA.b])

    def finish_rrows(with_itemp):
        P.do("v", lambda eo: eo.tensor_scalar(
            out=rrowA.h[:, :], in0=rrowA.h[:, :], scalar1=clampc.h[0:33, :],
            scalar2=None, op0=ALU.max), r=[rrowA.b, clampc.b], w=[rrowA.b])
        P.do("s", lambda eo: eo.activation(rrowA.h[:, :], rrowA.h[:, :], ACT.Ln),
             r=[rrowA.b], w=[rrowA.b])
        if with_itemp:
            P.do("s", lambda eo: eo.activation(rrow16A.h[:, :], rrowA.h[:, :],
                 ACT.Exp, scale=-0.5, bias=lnit.h[0:33, :]),
                 r=[rrowA.b, lnit.b], w=[rrow16A.b])
        else:
            P.do("s", lambda eo: eo.activation(rrow16A.h[:, :], rrowA.h[:, :],
                 ACT.Exp, scale=-0.5), r=[rrowA.b], w=[rrow16A.b])

    def bcast_row16(pp, hh, half):
        c0 = half * 1024
        b0 = hh * 32
        for j in range(2):
            P.do("t", lambda eo, j=j, c0=c0, pp=pp, b0=b0, hh=hh: eo.matmul(
                pp.h[:, j * 512:(j + 1) * 512], onesP.h[b0:b0 + 1, 0:128],
                rrow16[hh].h[0:1, c0 + j * 512:c0 + (j + 1) * 512],
                start=True, stop=True), r=[onesP.b, rrow16A.b], w=[pp.b])

    def av_divide(dst_fn):
        P.do("s", lambda eo: eo.activation(rd32.h[0:1, :], PQ3.h[DH:DH + 1, :],
             ACT.Ln), r=[PQ3.b], w=[rd32.b])
        P.do("s", lambda eo: eo.activation(rd16.h[0:1, :], rd32.h[0:1, :],
             ACT.Exp, scale=-1.0), r=[rd32.b], w=[rd16.b])
        P.do("t", lambda eo: eo.matmul(PQ1.h[0:64, :], onesP.h[64:65, 0:64],
             rd16.h[0:1, :], start=True, stop=True), r=[onesP.b, rd16.b], w=[PQ1.b])
        P.do("s", lambda eo: eo.activation(rdb16.h[:, :], PQ1.h[0:64, :],
             ACT.Identity), r=[PQ1.b], w=[rdb16.b])
        P.do("v", lambda eo: eo.tensor_tensor(out=tmp16a.h[0:64, :],
             in0=PQ3.h[0:DH, :], in1=rdb16.h[:, :], op=ALU.mult),
             r=[PQ3.b, rdb16.b], w=[tmp16a.b])
        dst_fn()

    ln_stats([(x16[c], DATA) for c in range(NCH)])
    ln_rows()
    for half in range(HLV):
        bcast_to(PP1, rstdrow16, half)
        bcast_to(PP2, murow16, half)
        for c in range(NCH):
            for j in range(2):
                cs = dslc(half * 1024 + j * 512, half * 1024 + (j + 1) * 512)
                ps = slice(j * 512, (j + 1) * 512)
                P.do("v", lambda eo, c=c, cs=cs, ps=ps: eo.tensor_tensor(
                    out=wtmp2.h[:, :], in0=x16[c].h[:, cs], in1=PP1.h[:, ps],
                    op=ALU.mult), r=[x16[c].b, PP1.b], w=[wtmp2.b])
                P.do("v", lambda eo, c=c, cs=cs, ps=ps: eo.tensor_tensor(
                    out=xn16[c].h[:, cs], in0=wtmp2.h[:, :], in1=PP2.h[:, ps],
                    op=ALU.subtract), r=[wtmp2.b, PP2.b], w=[xn16[c].b])
    P.barrier()

    for c in range(NCH):
        P.do("s", lambda eo, c=c: eo.activation(
            sq16.h[:, 0:S], xn16[c].h[:, DATA], ACT.Identity,
            accum_out=meanT.h[:, c:c + 1]), r=[xn16[c].b], w=[sq16.b, meanT.b])
    P.do("v", lambda eo: eo.tensor_tensor(out=meanT.h[:, :], in0=meanT.h[:, :],
         in1=wtT.h[:, :], op=ALU.mult), r=[meanT.b, wtT.b], w=[meanT.b])
    P.do("v", lambda eo: eo.tensor_reduce(out=tcol.h[:, :], in_=meanT.h[:, :],
         axis=AX.X, op=ALU.add), r=[meanT.b], w=[tcol.b])
    P.do("v", lambda eo: eo.tensor_scalar(out=tcol16.h[:, :], in0=tcol.h[:, :],
         scalar1=1.0 / S, scalar2=None, op0=ALU.mult), r=[tcol.b], w=[tcol16.b])
    P.do("t", lambda eo: eo.matmul(PQ1.h[0:1, 0:1], tcol16.h[:, :], ones_c.h[:, :],
         start=True, stop=True), r=[tcol16.b, ones_c.b], w=[PQ1.b])
    P.do("y", lambda eo: eo.dma_start(out=scal1.h[0:1, :], in_=bt_s.h),
         r=[bt_s.b], w=[scal1.b], dma=True)
    P.do("s", lambda eo: eo.activation(itemp.h[0:1, :], PQ1.h[0:1, 0:1], ACT.Sigmoid,
         bias=scal1.h[0:1, :]), r=[PQ1.b, scal1.b], w=[itemp.b])
    P.do("v", lambda eo: eo.tensor_scalar(out=itemp.h[0:1, :], in0=itemp.h[0:1, :],
         scalar1=0.5, scalar2=None, op0=ALU.add), r=[itemp.b], w=[itemp.b])
    P.do("v", lambda eo: eo.reciprocal(itemp.h[0:1, :], itemp.h[0:1, :]),
         r=[itemp.b], w=[itemp.b])
    P.do("s", lambda eo: eo.activation(scal1.h[0:1, :], itemp.h[0:1, :], ACT.Ln),
         r=[itemp.b], w=[scal1.b])
    P.do("v", lambda eo: eo.tensor_copy(scal16.h[0:1, :], scal1.h[0:1, :]),
         r=[scal1.b], w=[scal16.b])
    P.do("t", lambda eo: eo.matmul(PQ1.h[:, 0:1], onesP.h[0:1, 0:128],
         scal16.h[0:1, :], start=True, stop=True), r=[onesP.b, scal16.b], w=[PQ1.b])
    P.do("v", lambda eo: eo.tensor_copy(lnit.h[:, :], PQ1.h[:, 0:1]),
         r=[PQ1.b], w=[lnit.b])
    for mul, dst in ((0.6744898 * 0.125, tau0c), (1.0 / 5207.5, irho0c)):
        P.do("v", lambda eo, mul=mul: eo.tensor_scalar(out=scal1.h[0:1, :],
             in0=itemp.h[0:1, :], scalar1=mul, scalar2=None, op0=ALU.mult),
             r=[itemp.b], w=[scal1.b])
        P.do("v", lambda eo: eo.tensor_copy(scal16.h[0:1, :], scal1.h[0:1, :]),
             r=[scal1.b], w=[scal16.b])
        P.do("t", lambda eo: eo.matmul(PQ1.h[:, 0:1], onesP.h[0:1, 0:128],
             scal16.h[0:1, :], start=True, stop=True),
             r=[onesP.b, scal16.b], w=[PQ1.b])
        P.do("v", lambda eo, dst=dst: eo.tensor_copy(dst.h[:, :], PQ1.h[:, 0:1]),
             r=[PQ1.b], w=[dst.b])
    P.barrier()

    def load_cols(Wd, dst, col0, ncol=128):
        t_ = col0 // 128
        src = Wd.h[t_ * 128:(t_ + 1) * 128, :]
        P.do("g", lambda eo, src=src, dst=dst: eo.dma_start(
            out=dst.h[:, 0:(ncol * NCH)], in_=src), r=[Wd.b], w=[dst.b], dma=True)

    def load_wpair(p, slot):
        load_cols(Wq, wq16[slot], p * 128)
        load_cols(Wk, wk16[slot], p * 128)
        load_cols(Wv, wv16[slot], p * 128)

    def vT_make(wvt, hh, bias_row, bias_off, src_tiles, src_off):
        for t in range(NQT):
            PQ = PQ1 if t % 2 == 0 else PQ2
            for c in range(NCH):
                tt, off = src_tiles[c], src_off[c]
                P.do("t", lambda eo, t=t, c=c, PQ=PQ, tt=tt, off=off, wvt=wvt, hh=hh:
                     eo.matmul(PQ.h[:, 0:64],
                               tt.h[:, off + t * 128:off + (t + 1) * 128],
                               wvt.h[:, c * 128 + hh * 64:c * 128 + hh * 64 + 64],
                               start=(c == 0), stop=False),
                     r=[tt.b, wvt.b], w=[PQ.b])
            P.do("t", lambda eo, t=t, PQ=PQ, bias_row=bias_row, bias_off=bias_off,
                 hh=hh: eo.matmul(
                     PQ.h[:, 0:64], onesP.h[32:33, 0:128],
                     bias_row.h[0:1, bias_off + hh * 64:bias_off + hh * 64 + 64],
                     start=False, stop=True),
                 r=[onesP.b, bias_row.b], w=[PQ.b])
            eng = "v" if t % 2 == 0 else "s"
            if eng == "v":
                P.do("v", lambda eo, hh=hh, t=t, PQ=PQ: eo.tensor_copy(
                    vhat[hh][t].h[:, 0:DH], PQ.h[:, 0:64]),
                    r=[PQ.b], w=[vhat[hh][t].b])
            else:
                P.do("s", lambda eo, hh=hh, t=t, PQ=PQ: eo.activation(
                    vhat[hh][t].h[:, 0:DH], PQ.h[:, 0:64],
                    ACT.Identity), r=[PQ.b], w=[vhat[hh][t].b])

    pending = [None]

    def emit_av(ctx):
        if ctx is None:
            return
        hh, hb, hd, halfb, qt, kt_ = ctx
        for gi in range(NG // 2):
            gidx = halfb * (NG // 2) + gi
            qs = dslc(gidx * 512, (gidx + 1) * 512)
            P.do("t", lambda eo, gidx=gidx: eo.matmul(PQ2.h[:, :],
                 onesP.h[64:65, 0:128],
                 throw.h[0:1, gidx * 512:(gidx + 1) * 512],
                 start=True, stop=True), r=[onesP.b, throw.b], w=[PQ2.b])
            P.do("s", lambda eo: eo.activation(thb.h[:, :], PQ2.h[:, :],
                 ACT.Identity), r=[PQ2.b], w=[thb.b])
            def qkb(kc, qs=qs, hb=hb):
                PQ = PQ1 if kc % 2 == 0 else PQ2
                P.do("t", lambda eo, kc=kc, qs=qs, PQ=PQ, hb=hb: eo.matmul(
                    PQ.h[:, :],
                    kt_.h[hb:hb + 64, SD + kc * 128:SD + (kc + 1) * 128],
                    qt.h[hb:hb + 64, qs], start=True, stop=True),
                    r=[kt_.b, qt.b], w=[PQ.b])
            qkb(0)
            for kc in range(NQT):
                PQ = PQ1 if kc % 2 == 0 else PQ2
                yt = yT[kc % 3]
                if kc + 1 < NQT:
                    qkb(kc + 1)
                P.do("s", lambda eo, PQ=PQ, yt=yt: eo.activation(
                    yt.h[:, :], PQ.h[:, :], ACT.Exp), r=[PQ.b], w=[yt.b])
                P.do("v", lambda eo, yt=yt: eo.tensor_tensor(
                    out=tmp16a.h[:, :], in0=yt.h[:, :], in1=thb.h[:, :],
                    op=ALU.is_gt), r=[yt.b, thb.b], w=[tmp16a.b])
                P.do("v", lambda eo, yt=yt: eo.tensor_tensor(
                    out=yt.h[:, :], in0=yt.h[:, :], in1=tmp16a.h[:, :],
                    op=ALU.mult), r=[yt.b, tmp16a.b], w=[yt.b])
                P.do("t", lambda eo, kc=kc, hh=hh, yt=yt: eo.matmul(
                    PQ3.h[0:DH + 1, :], vhat[hh][kc].h[:, :], yt.h[:, :],
                    start=(kc == 0), stop=(kc == NQT - 1)),
                    r=[vhat[hh][kc].b, yt.b], w=[PQ3.b])

            def store(hd=hd, gidx=gidx):
                P.do("y", lambda eo: eo.dma_start(
                    out=sp_scr.h[hd * 64:(hd + 1) * 64,
                                 gidx * 512:(gidx + 1) * 512],
                    in_=tmp16a.h[0:64, :]), r=[tmp16a.b], w=[sp_scr.b],
                    dma=True)
            av_divide(store)


    load_wpair(0, 0)
    for p in range(H // 2):
        slot = p % 2
        if p + 1 < H // 2:
            load_wpair(p + 1, 1 - slot)
        emit_av(pending[0])
        pending[0] = None
        qt = q16p if p % 2 == 0 else q16pB
        kt_ = k16p if p % 2 == 0 else k16pB
        for wtl, bias_t, dst16, is_q in ((wq16, bias_q, qt, True),
                                         (wk16, bias_k, kt_, False)):
            wt16 = wtl[slot]
            for half in range(HLV):
                PP = PP1 if half == 0 else PP2
                for c in range(NCH):
                    for j in range(2):
                        P.do("t", lambda eo, c=c, j=j, half=half, wt16=wt16, PP=PP:
                             eo.matmul(PP.h[:, j * 512:(j + 1) * 512],
                                       wt16.h[:, c * 128:(c + 1) * 128],
                                       xn16[c].h[:, SD + half * 1024 + j * 512:
                                                 SD + half * 1024 + (j + 1) * 512],
                                       start=(c == 0), stop=(c == NCH - 1)),
                             r=[wt16.b, xn16[c].b], w=[PP.b])
                cs = dslc(half * 1024, (half + 1) * 1024)
                P.do("s", lambda eo, PP=PP, cs=cs, bias_t=bias_t, dst16=dst16, p=p:
                     eo.activation(dst16.h[:, cs], PP.h[:, :], ACT.Identity,
                                   bias=bias_t.h[:, p:p + 1]),
                     r=[PP.b, bias_t.b], w=[dst16.b])
                csq = slice(half * 1024, (half + 1) * 1024)
                P.do("s", lambda eo, PP=PP, csq=csq, bias_t=bias_t, p=p:
                     eo.activation(sq16.h[:, csq], PP.h[:, :], ACT.Square,
                                   bias=bias_t.h[:, p:p + 1]),
                     r=[PP.b, bias_t.b], w=[sq16.b])
            ssq_rows(sq16)
            finish_rrows(is_q)
            for hh in range(2):
                hb = hh * 64
                for half in range(HLV):
                    PP = PP1 if half == 0 else PP2
                    bcast_row16(PP, hh, half)
                    cs = dslc(half * 1024, (half + 1) * 1024)
                    P.do("v", lambda eo, dst16=dst16, hb=hb, cs=cs, PP=PP:
                         eo.tensor_tensor(out=dst16.h[hb:hb + 64, cs],
                                          in0=dst16.h[hb:hb + 64, cs],
                                          in1=PP.h[hb:hb + 64, :], op=ALU.mult),
                         r=[dst16.b, PP.b], w=[dst16.b])
        for hh in range(2):
            vT_make(wv16[slot], hh, bvrow, p * 128,
                    [xn16[c] for c in range(NCH)], [SD] * NCH)
            hb = hh * 64
            hd = p * 2 + hh
            for halfb in range(2):
                for ti in range(QTH):
                    t = halfb * QTH + ti
                    PP = PP1 if ti % 2 == 0 else PP2
                    for hf2 in range(NG // 2):
                        for jj in range(2):
                            ks = dslc((hf2 * 2 + jj) * 512, (hf2 * 2 + jj + 1) * 512)
                            P.do("t", lambda eo, t=t, jj=jj, PP=PP, hb=hb,
                                 ks=ks, qt=qt, kt_=kt_:
                                 eo.matmul(PP.h[:, jj * 512:(jj + 1) * 512],
                                           qt.h[hb:hb + 64,
                                                SD + t * 128:SD + (t + 1) * 128],
                                           kt_.h[hb:hb + 64, ks],
                                           start=True, stop=True),
                                 r=[qt.b, kt_.b], w=[PP.b])
                        if (ti + hf2) % 2 == 0:
                            P.do("s", lambda eo, ti=ti, hf2=hf2, PP=PP:
                                 eo.activation(
                                     s16[ti].h[:, hf2 * 1024:(hf2 + 1) * 1024],
                                     PP.h[:, :], ACT.Identity),
                                 r=[PP.b], w=[s16[ti].b])
                        else:
                            P.do("v", lambda eo, ti=ti, hf2=hf2, PP=PP:
                                 eo.tensor_copy(
                                     s16[ti].h[:, hf2 * 1024:(hf2 + 1) * 1024],
                                     PP.h[:, :]), r=[PP.b], w=[s16[ti].b])
                P.do("v", lambda eo: eo.tensor_scalar(out=st["tauA"].h[:, :],
                     in0=st["t1"].h[:, :], scalar1=0.0, scalar2=tau0c.h[:, :],
                     op0=ALU.mult, op1=ALU.add), r=[st["t1"].b, tau0c.b],
                     w=[st["tauA"].b])
                P.do("v", lambda eo: eo.memset(st["lo"].h[:, :], -4.2), w=[st["lo"].b])
                P.do("v", lambda eo: eo.memset(st["hi"].h[:, :], 4.2), w=[st["hi"].b])
                P.do("v", lambda eo: eo.memset(st["bd"].h[:, :], 1e9), w=[st["bd"].b])
                P.do("v", lambda eo: eo.memset(st["bt"].h[:, :], 0.0), w=[st["bt"].b])
                P.do("v", lambda eo: eo.tensor_scalar(out=st["tnA"].h[:, :],
                     in0=st["tauA"].h[:, :], scalar1=-1.0, scalar2=None,
                     op0=ALU.mult), r=[st["tauA"].b], w=[st["tnA"].b])
                cur, prv = "A", "B"
                QV = QTH // 2
                NSUB = max(0, NPROBE - 2)
                for probe in range(NPROBE):
                    sub = probe < NSUB
                    tau_c, c_c = st["tau" + cur], st["c" + cur]
                    tau_p, c_p = st["tau" + prv], st["c" + prv]
                    tn_c, tn_p = st["tn" + cur], st["tn" + prv]
                    cbV, cbS = stc["c" + cur]
                    for ti in range(QV):
                        ap = (s16[ti].h[:, 0:S:4] if sub else s16[ti].h[:, 0:S])
                        nout = S // 4 if sub else S
                        P.do("v", lambda eo, ti=ti, tau_c=tau_c, c_c=c_c, ap=ap,
                             nout=nout: eo.tensor_scalar(out=dump8.h[:, 0:nout],
                                              in0=ap,
                                              scalar1=tau_c.h[:, ti:ti + 1],
                                              scalar2=None, op0=ALU.is_gt,
                                              op1=ALU.add,
                                              accum_out=c_c.h[:, ti:ti + 1]),
                             r=[s16[ti].b, tau_c.b], w=[dump8.b, cbV.b])
                    for ti in range(QV, QTH):
                        ap = (s16[ti].h[:, 0:S:4] if sub else s16[ti].h[:, 0:S])
                        nout = S // 4 if sub else S
                        P.do("s", lambda eo, ti=ti, tn_c=tn_c, c_c=c_c, ap=ap,
                             nout=nout: eo.activation(sq16.h[:, 0:nout], ap,
                                           ACT.Sign, bias=tn_c.h[:, ti:ti + 1],
                                           accum_out=c_c.h[:, ti:ti + 1]),
                             r=[s16[ti].b, tn_c.b], w=[sq16.b, cbS.b])
                    if sub:
                        P.do("v", lambda eo, c_c=c_c: eo.tensor_scalar(
                            out=c_c.h[:, 0:QV], in0=c_c.h[:, 0:QV],
                            scalar1=4.0, scalar2=None, op0=ALU.mult),
                            r=[cbV.b], w=[cbV.b])
                        P.do("v", lambda eo, c_c=c_c: eo.tensor_scalar(
                            out=c_c.h[:, QV:QTH], in0=c_c.h[:, QV:QTH],
                            scalar1=2.0, scalar2=float(S // 2), op0=ALU.mult,
                            op1=ALU.add), r=[cbS.b], w=[cbS.b])
                    else:
                        P.do("v", lambda eo, c_c=c_c: eo.tensor_scalar(
                            out=c_c.h[:, QV:QTH], in0=c_c.h[:, QV:QTH],
                            scalar1=0.5, scalar2=float(S // 2), op0=ALU.mult,
                            op1=ALU.add), r=[cbS.b], w=[cbS.b])
                    P.raw("v", lambda eo: eo.drain())
                    d1, t1, t2, t3 = st["d1"], st["t1"], st["t2"], st["t3"]
                    lo_, hi_, bt_, bd_ = st["lo"], st["hi"], st["bt"], st["bd"]

                    def U(fn, r, w):
                        P.do("v", fn, r=r, w=w)
                    U(lambda eo, c_c=c_c: eo.tensor_scalar(out=d1.h[:, :],
                      in0=c_c.h[:, :], scalar1=-float(KK), scalar2=None,
                      op0=ALU.add), [cbV.b, cbS.b], [d1.b])
                    U(lambda eo: eo.tensor_scalar(out=msk.h[:, :], in0=d1.h[:, :],
                      scalar1=0.0, scalar2=None, op0=ALU.is_ge), [d1.b], [msk.b])
                    U(lambda eo: eo.tensor_scalar(out=mskn.h[:, :], in0=d1.h[:, :],
                      scalar1=0.0, scalar2=None, op0=ALU.is_lt), [d1.b], [mskn.b])
                    U(lambda eo, tau_c=tau_c: eo.copy_predicated(lo_.h[:, :],
                      msk.h[:, :], tau_c.h[:, :]), [msk.b, tau_c.b, lo_.b], [lo_.b])
                    U(lambda eo, tau_c=tau_c: eo.copy_predicated(hi_.h[:, :],
                      mskn.h[:, :], tau_c.h[:, :]), [mskn.b, tau_c.b, hi_.b], [hi_.b])
                    if not sub:
                        U(lambda eo: eo.tensor_scalar(out=t2.h[:, :], in0=d1.h[:, :],
                          scalar1=-1.0, scalar2=None, op0=ALU.mult), [d1.b], [t2.b])
                        U(lambda eo: eo.tensor_tensor(out=t2.h[:, :], in0=t2.h[:, :],
                          in1=d1.h[:, :], op=ALU.max), [t2.b, d1.b], [t2.b])
                        U(lambda eo: eo.tensor_tensor(out=msk.h[:, :], in0=t2.h[:, :],
                          in1=bd_.h[:, :], op=ALU.is_le), [t2.b, bd_.b], [msk.b])
                        U(lambda eo, tau_c=tau_c: eo.copy_predicated(bt_.h[:, :],
                          msk.h[:, :], tau_c.h[:, :]), [msk.b, tau_c.b, bt_.b], [bt_.b])
                        U(lambda eo: eo.copy_predicated(bd_.h[:, :], msk.h[:, :],
                          t2.h[:, :]), [msk.b, t2.b, bd_.b], [bd_.b])
                    if probe < NPROBE - 1:
                        if probe == 0:
                            U(lambda eo: eo.tensor_scalar(out=t2.h[:, :],
                              in0=d1.h[:, :], scalar1=irho0c.h[:, :], scalar2=None,
                              op0=ALU.mult), [d1.b, irho0c.b], [t2.b])
                        else:
                            U(lambda eo, tau_c=tau_c, tau_p=tau_p: eo.tensor_tensor(
                              out=t1.h[:, :], in0=tau_c.h[:, :], in1=tau_p.h[:, :],
                              op=ALU.subtract), [tau_c.b, tau_p.b], [t1.b])
                            U(lambda eo, c_c=c_c, c_p=c_p: eo.tensor_tensor(
                              out=t2.h[:, :], in0=c_p.h[:, :], in1=c_c.h[:, :],
                              op=ALU.subtract),
                              [stc["c" + prv][0].b, stc["c" + prv][1].b, cbV.b, cbS.b],
                              [t2.b])
                            U(lambda eo: eo.reciprocal(t1.h[:, :], t1.h[:, :]),
                              [t1.b], [t1.b])
                            U(lambda eo: eo.tensor_tensor(out=t2.h[:, :],
                              in0=t2.h[:, :], in1=t1.h[:, :], op=ALU.mult),
                              [t2.b, t1.b], [t2.b])
                            U(lambda eo: eo.tensor_scalar(out=t2.h[:, :],
                              in0=t2.h[:, :], scalar1=20.0, scalar2=2.0e6,
                              op0=ALU.max, op1=ALU.min), [t2.b], [t2.b])
                            U(lambda eo: eo.reciprocal(t2.h[:, :], t2.h[:, :]),
                              [t2.b], [t2.b])
                            U(lambda eo: eo.tensor_tensor(out=t2.h[:, :],
                              in0=d1.h[:, :], in1=t2.h[:, :], op=ALU.mult),
                              [d1.b, t2.b], [t2.b])
                        U(lambda eo, tau_c=tau_c, tau_p=tau_p: eo.tensor_tensor(
                          out=tau_p.h[:, :], in0=tau_c.h[:, :], in1=t2.h[:, :],
                          op=ALU.add), [tau_c.b, t2.b], [tau_p.b])
                        U(lambda eo: eo.tensor_tensor(out=t1.h[:, :],
                          in0=hi_.h[:, :], in1=lo_.h[:, :], op=ALU.subtract),
                          [hi_.b, lo_.b], [t1.b])
                        U(lambda eo: eo.tensor_scalar(out=t1.h[:, :],
                          in0=t1.h[:, :], scalar1=0.05, scalar2=None,
                          op0=ALU.mult), [t1.b], [t1.b])
                        U(lambda eo: eo.tensor_tensor(out=t3.h[:, :],
                          in0=lo_.h[:, :], in1=t1.h[:, :], op=ALU.add),
                          [lo_.b, t1.b], [t3.b])
                        U(lambda eo, tau_p=tau_p: eo.tensor_tensor(
                          out=tau_p.h[:, :], in0=tau_p.h[:, :], in1=t3.h[:, :],
                          op=ALU.max), [tau_p.b, t3.b], [tau_p.b])
                        U(lambda eo: eo.tensor_tensor(out=t3.h[:, :],
                          in0=hi_.h[:, :], in1=t1.h[:, :], op=ALU.subtract),
                          [hi_.b, t1.b], [t3.b])
                        U(lambda eo, tau_p=tau_p: eo.tensor_tensor(
                          out=tau_p.h[:, :], in0=tau_p.h[:, :], in1=t3.h[:, :],
                          op=ALU.min), [tau_p.b, t3.b], [tau_p.b])
                        U(lambda eo, tau_p=tau_p, tn_p=tn_p: eo.tensor_scalar(
                          out=tn_p.h[:, :], in0=tau_p.h[:, :], scalar1=-1.0,
                          scalar2=None, op0=ALU.mult), [tau_p.b], [tn_p.b])
                        cur, prv = prv, cur
                P.do("s", lambda eo: eo.activation(th16t.h[:, :], st["bt"].h[:, :],
                     ACT.Exp), r=[st["bt"].b], w=[th16t.b])
                for ti in range(QTH):
                    t = halfb * QTH + ti
                    P.do("y", lambda eo, t=t, ti=ti: eo.dma_start(
                        out=throw.h[0:1, t * 128:(t + 1) * 128],
                        in_=th16t.h[:, ti:ti + 1]), r=[th16t.b], w=[throw.b],
                        dma=True)
                emit_av(pending[0])
                pending[0] = (hh, hb, hd, halfb, qt, kt_)
    emit_av(pending[0])
    pending[0] = None
    P.barrier()

    sp16 = s16
    for c in range(NCH):
        P.do("y", lambda eo, c=c: eo.dma_start(out=sp16[c].h[:, 0:S],
             in_=sp_scr.h[c * 128:(c + 1) * 128, :]), r=[sp_scr.b],
             w=[sp16[c].b], dma=True)

    def load_wog(oc, slot):
        load_cols(Wo, wo16[slot], oc * 128)
        load_cols(Wg, wg16[slot], oc * 128)

    load_wog(0, 0)
    for oc in range(NCH):
        slot = oc % 2
        if oc + 1 < NCH:
            load_wog(oc + 1, 1 - slot)
        for gidx in range(NG):
            qs0 = slice(gidx * 512, (gidx + 1) * 512)
            qsd = dslc(gidx * 512, (gidx + 1) * 512)
            PU = PQ1 if gidx % 2 == 0 else PQ2
            for c in range(NCH):
                P.do("t", lambda eo, c=c, qs0=qs0, PU=PU, slot=slot: eo.matmul(
                    PU.h[:, :], wo16[slot].h[:, c * 128:(c + 1) * 128],
                    sp16[c].h[:, qs0], start=(c == 0), stop=False),
                    r=[wo16[slot].b, sp16[c].b], w=[PU.b])
            P.do("t", lambda eo, PU=PU, oc=oc: eo.matmul(
                PU.h[:, :], borow.h[0:1, oc * 128:(oc + 1) * 128],
                onesP.h[0:1, 0:512], start=False, stop=True),
                r=[borow.b, onesP.b], w=[PU.b])
            for c in range(NCH):
                P.do("t", lambda eo, c=c, qsd=qsd, slot=slot: eo.matmul(
                    PQ3.h[:, :], wg16[slot].h[:, c * 128:(c + 1) * 128],
                    xn16[c].h[:, qsd], start=(c == 0), stop=(c == NCH - 1)),
                    r=[wg16[slot].b, xn16[c].b], w=[PQ3.b])
            P.do("s", lambda eo, oc=oc: eo.activation(
                tmp16b.h[:, :], PQ3.h[:, :], ACT.Sigmoid,
                bias=bias_g.h[:, oc:oc + 1]), r=[PQ3.b, bias_g.b], w=[tmp16b.b])
            P.do("v", lambda eo, oc=oc, qsd=qsd, PU=PU: eo.tensor_tensor(
                out=tmp32a.h[:, :], in0=PU.h[:, :], in1=xn16[oc].h[:, qsd],
                op=ALU.subtract), r=[PU.b, xn16[oc].b], w=[tmp32a.b])
            P.do("v", lambda eo: eo.tensor_tensor(out=tmp32a.h[:, :],
                 in0=tmp32a.h[:, :], in1=tmp16b.h[:, :], op=ALU.mult),
                 r=[tmp32a.b, tmp16b.b], w=[tmp32a.b])
            P.do("v", lambda eo, oc=oc, qsd=qsd: eo.tensor_tensor(
                out=tmp16a.h[:, :], in0=tmp32a.h[:, :], in1=xn16[oc].h[:, qsd],
                op=ALU.add), r=[tmp32a.b, xn16[oc].b], w=[tmp16a.b])
            P.do("y", lambda eo, oc=oc, gidx=gidx: eo.dma_start(
                out=s_scr.h[oc * 128:(oc + 1) * 128, gidx * 512:(gidx + 1) * 512],
                in_=tmp16a.h[:, :]), r=[tmp16a.b], w=[s_scr.b], dma=True)
    P.barrier()

    qc = s16[0:4]
    kc16 = s16[4:8]
    sqf = sq16
    qshf = q16p
    h1_16 = s16
    SHIFTOFF = [SD] * 4 + [SD - 1] * 4

    for l in range(2):
        for srcd, dstt in ((qsc_c, qscol), (ksc_c, kscol), (lng_c, lngc),
                           (lnb_c, lnbc), (b1p_c, b1c)):
            P.do("y", lambda eo, srcd=srcd, dstt=dstt, l=l: eo.dma_start(
                out=dstt.h[:, :], in_=srcd.h[l]), r=[srcd.b], w=[dstt.b], dma=True)
        P.do("g", lambda eo, l=l: eo.dma_start(out=b2row.h[:, :], in_=b2_r.h[l]),
             r=[b2_r.b], w=[b2row.b], dma=True)

        def load_wqkv(oc, slot, l=l):
            r0 = (l * 8 + oc) * 128
            src = wqkv_qk.h[r0:r0 + 128, :]
            P.do("g", lambda eo, src=src, slot=slot: eo.dma_start(
                out=wqkv16[slot].h[:, :], in_=src), r=[wqkv_qk.b],
                w=[wqkv16[slot].b], dma=True)

        load_wqkv(0, 0)
        for oc in range(8):
            slot = oc % 2
            if oc + 1 < 8:
                load_wqkv(oc + 1, 1 - slot)
            isq = oc < 4
            dst16 = qc[oc] if isq else kc16[oc - 4]
            for half in range(HLV):
                PP = PP1 if half == 0 else PP2
                for c in range(NCH):
                    base = SHIFTOFF[c]
                    for j in range(2):
                        P.do("t", lambda eo, c=c, j=j, half=half, slot=slot,
                             base=base, PP=PP: eo.matmul(
                                 PP.h[:, j * 512:(j + 1) * 512],
                                 wqkv16[slot].h[:, c * 128:(c + 1) * 128],
                                 xn16[c].h[:, base + half * 1024 + j * 512:
                                           base + half * 1024 + (j + 1) * 512],
                                 start=(c == 0), stop=(c == NCH - 1)),
                             r=[wqkv16[slot].b, xn16[c].b], w=[PP.b])
                cs = slice(half * 1024, (half + 1) * 1024)
                P.do("s", lambda eo, PP=PP, cs=cs, dst16=dst16: eo.activation(
                    dst16.h[:, cs], PP.h[:, :], ACT.Identity),
                    r=[PP.b], w=[dst16.b])
                P.do("s", lambda eo, PP=PP, cs=cs: eo.activation(
                    sqf.h[:, cs], PP.h[:, :], ACT.Square), r=[PP.b], w=[sqf.b])
            ssq_rows(sqf)
            finish_rrows(False)
            for hh in range(2):
                hb = hh * 64
                for half in range(HLV):
                    PP = PP1 if half == 0 else PP2
                    bcast_row16(PP, hh, half)
                    cs = slice(half * 1024, (half + 1) * 1024)
                    P.do("v", lambda eo, dst16=dst16, hb=hb, cs=cs, PP=PP:
                         eo.tensor_tensor(out=dst16.h[hb:hb + 64, cs],
                                          in0=dst16.h[hb:hb + 64, cs],
                                          in1=PP.h[hb:hb + 64, :], op=ALU.mult),
                         r=[dst16.b, PP.b], w=[dst16.b])

        for tiles, aidx, scol in ((qc, 0, qscol), (kc16, 2, kscol)):
            for qtr in range(NG):
                q0 = qtr * 512
                P.do("y", lambda eo, aidx=aidx, q0=q0: eo.dma_start(
                    out=rot2[0].h[:, :], in_=rotT.h[aidx][:, q0:q0 + 512]),
                    r=[rotT.b], w=[rot2[0].b], dma=True)
                P.do("y", lambda eo, aidx=aidx, q0=q0: eo.dma_start(
                    out=rot2[1].h[:, :], in_=rotT.h[aidx + 1][:, q0:q0 + 512]),
                    r=[rotT.b], w=[rot2[1].b], dma=True)
                for c4 in range(4):
                    tl = tiles[c4]
                    qsl = slice(q0, q0 + 512)
                    P.do("v", lambda eo, tl=tl, qsl=qsl, scol=scol: eo.tensor_scalar(
                        out=sqf.h[:, qsl], in0=tl.h[:, qsl], scalar1=scol.h[:, :],
                        scalar2=None, op0=ALU.mult), r=[tl.b, scol.b], w=[sqf.b])
                    for b0 in (0, 64):
                        P.do("y", lambda eo, b0=b0, qsl=qsl: eo.dma_start(
                            out=qshf.h[b0:b0 + 32, qsl],
                            in_=sqf.h[b0 + 32:b0 + 64, qsl]),
                            r=[sqf.b], w=[qshf.b], dma=True)
                        P.do("y", lambda eo, b0=b0, qsl=qsl: eo.dma_start(
                            out=qshf.h[b0 + 32:b0 + 64, qsl],
                            in_=sqf.h[b0:b0 + 32, qsl]),
                            r=[sqf.b], w=[qshf.b], dma=True)
                    P.do("v", lambda eo, qsl=qsl: eo.tensor_tensor(
                        out=sqf.h[:, qsl], in0=sqf.h[:, qsl], in1=rot2[0].h[:, :],
                        op=ALU.mult), r=[sqf.b, rot2[0].b], w=[sqf.b])
                    P.do("v", lambda eo, qsl=qsl: eo.tensor_tensor(
                        out=qshf.h[:, qsl], in0=qshf.h[:, qsl], in1=rot2[1].h[:, :],
                        op=ALU.mult), r=[qshf.b, rot2[1].b], w=[qshf.b])
                    P.do("v", lambda eo, tl=tl, qsl=qsl: eo.tensor_tensor(
                        out=tl.h[:, qsl], in0=sqf.h[:, qsl], in1=qshf.h[:, qsl],
                        op=ALU.add), r=[sqf.b, qshf.b], w=[tl.b])
        otile = qc
        for h in range(HF):
            hslot = h % 2
            r0 = (l * HF + h) * 128
            src = wqkv_v.h[r0:r0 + 128, :]
            P.do("g", lambda eo, src=src, hslot=hslot: eo.dma_start(
                out=fwo16[hslot].h[:, :], in_=src), r=[wqkv_v.b],
                w=[fwo16[hslot].b], dma=True)
            for t in range(NQT):
                PQ = PQ1 if t % 2 == 0 else PQ2
                for c in range(NCH):
                    base = SHIFTOFF[c]
                    P.do("t", lambda eo, t=t, c=c, PQ=PQ, base=base, hslot=hslot:
                         eo.matmul(PQ.h[:, 0:64],
                                   xn16[c].h[:, base + t * 128:base + (t + 1) * 128],
                                   fwo16[hslot].h[:, c * 64:(c + 1) * 64],
                                   start=(c == 0), stop=(c == NCH - 1)),
                         r=[xn16[c].b, fwo16[hslot].b], w=[PQ.b])
                eng = "v" if t % 2 == 0 else "s"
                if eng == "v":
                    P.do("v", lambda eo, hslot=hslot, t=t, PQ=PQ: eo.tensor_copy(
                        vhat[hslot][t].h[:, 0:DH], PQ.h[:, 0:64]),
                        r=[PQ.b], w=[vhat[hslot][t].b])
                else:
                    P.do("s", lambda eo, hslot=hslot, t=t, PQ=PQ: eo.activation(
                        vhat[hslot][t].h[:, 0:DH], PQ.h[:, 0:64],
                        ACT.Identity), r=[PQ.b], w=[vhat[hslot][t].b])
            cq = h // 2
            b0 = (h % 2) * 64
            for w in range(NW):
                kts = list(range(max(0, 4 * (w - 1)), 4 * (w + 1)))
                first, last = kts[0], kts[-1]
                def simk(i, kt, w=w, b0=b0, cq=cq):
                    PQ = PQ1 if i % 2 == 0 else PQ2
                    P.do("t", lambda eo, kt=kt, w=w, PQ=PQ, b0=b0, cq=cq: eo.matmul(
                        PQ.h[:, :], kc16[cq].h[b0:b0 + 64, kt * 128:(kt + 1) * 128],
                        qc[cq].h[b0:b0 + 64, w * 512:(w + 1) * 512],
                        start=True, stop=True), r=[kc16[cq].b, qc[cq].b], w=[PQ.b])
                simk(0, kts[0])
                for i, kt in enumerate(kts):
                    PQ = PQ1 if i % 2 == 0 else PQ2
                    yt = yT[i % 3]
                    off = (kt - 4 * w) * 128
                    if i + 1 < len(kts):
                        simk(i + 1, kts[i + 1])
                    iv = (off + 512) // 128
                    P.do("v", lambda eo, PQ=PQ, iv=iv: eo.tensor_tensor(
                        out=tmp32a.h[:, :], in0=PQ.h[:, :],
                        in1=madd.h[:, iv * 512:(iv + 1) * 512], op=ALU.add),
                        r=[PQ.b, madd.b], w=[tmp32a.b])
                    P.do("s", lambda eo, yt=yt: eo.activation(
                        yt.h[:, :], tmp32a.h[:, :], ACT.Exp, scale=8.0),
                        r=[tmp32a.b], w=[yt.b])
                    P.do("t", lambda eo, kt=kt, h=h, first=first, last=last,
                         hslot=hslot, yt=yt: eo.matmul(
                             PQ3.h[0:DH + 1, :], vhat[hslot][kt].h[:, :], yt.h[:, :],
                             start=(kt == first), stop=(kt == last)),
                         r=[vhat[hslot][kt].b, yt.b], w=[PQ3.b])

                def store(h=h, w=w):
                    P.do("y", lambda eo: eo.dma_start(
                        out=otile[h // 2].h[(h % 2) * 64:(h % 2) * 64 + 64,
                                            w * 512:(w + 1) * 512],
                        in_=tmp16a.h[0:64, :]), r=[tmp16a.b],
                        w=[otile[h // 2].b], dma=True)
                av_divide(store)
        P.barrier()
        def load_fwo(oc, slot, l=l):
            r0 = (l * 8 + oc) * 128
            src = fwo.h[r0:r0 + 128, :]
            P.do("g", lambda eo, src=src, slot=slot: eo.dma_start(
                out=fwo16[slot].h[:, :], in_=src), r=[fwo.b],
                w=[fwo16[slot].b], dma=True)

        load_fwo(0, 0)
        for oc in range(NCH):
            slot = oc % 2
            if oc + 1 < NCH:
                load_fwo(oc + 1, 1 - slot)
            for gidx in range(NG):
                qsd = dslc(gidx * 512, (gidx + 1) * 512)
                PU = PQ1 if gidx % 2 == 0 else PQ2
                for c in range(4):
                    P.do("t", lambda eo, c=c, gidx=gidx, PU=PU, slot=slot:
                         eo.matmul(PU.h[:, :],
                                   fwo16[slot].h[:, c * 128:(c + 1) * 128],
                                   otile[c].h[:, gidx * 512:(gidx + 1) * 512],
                                   start=(c == 0), stop=(c == 3)),
                         r=[fwo16[slot].b, otile[c].b], w=[PU.b])
                P.do("v", lambda eo, oc=oc, qsd=qsd, PU=PU: eo.tensor_tensor(
                    out=xn16[oc].h[:, qsd], in0=PU.h[:, :], in1=xn16[oc].h[:, qsd],
                    op=ALU.add), r=[PU.b, xn16[oc].b], w=[xn16[oc].b])
        P.barrier()
        ln_stats([(xn16[c], slice(SHIFTOFF[c], SHIFTOFF[c] + S))
                  for c in range(NCH)])
        ln_rows()
        for half in range(HLV):
            bcast_to(PP1, rstdrow16, half)
            bcast_to(PP2, murow16, half)
            for c in range(NCH):
                base = SHIFTOFF[c]
                for j in range(2):
                    csx = slice(base + half * 1024 + j * 512,
                                base + half * 1024 + (j + 1) * 512)
                    cs = slice(half * 1024 + j * 512, half * 1024 + (j + 1) * 512)
                    ps = slice(j * 512, (j + 1) * 512)
                    P.do("v", lambda eo, c=c, csx=csx, ps=ps: eo.tensor_tensor(
                        out=wtmp2.h[:, :], in0=xn16[c].h[:, csx], in1=PP1.h[:, ps],
                        op=ALU.mult), r=[xn16[c].b, PP1.b], w=[wtmp2.b])
                    P.do("v", lambda eo, ps=ps: eo.tensor_tensor(
                        out=wtmp2.h[:, :], in0=wtmp2.h[:, :], in1=PP2.h[:, ps],
                        op=ALU.subtract), r=[wtmp2.b, PP2.b], w=[wtmp2.b])
                    P.do("v", lambda eo, c=c, cs=cs: eo.tensor_scalar(
                        out=h1_16[c].h[:, cs], in0=wtmp2.h[:, :],
                        scalar1=lngc.h[:, c:c + 1], scalar2=lnbc.h[:, c:c + 1],
                        op0=ALU.mult, op1=ALU.add),
                        r=[wtmp2.b, lngc.b, lnbc.b], w=[h1_16[c].b])
        P.barrier()
        w2half = []
        for k in range(NK2):
            wt = wbig[k // 2]
            ap = wt.h[:, (k % 2) * 512:(k % 2) * 512 + 512]
            w2half.append((wt, ap))

        w1a2 = [w1ab, wbig[9]]
        w1g2 = [w1gb, wbig[10]]

        def load_w1(m, gate, buf, l=l):
            mm_ = m + NK2 if gate else m
            dst = (w1g2 if gate else w1a2)[buf]
            r0 = (l * NM1 + mm_) * 128
            src = w1p.h[r0:r0 + 128, :]
            P.do("g", lambda eo, src=src, dst=dst: eo.dma_start(
                out=dst.h[:, :], in_=src), r=[w1p.b], w=[dst.b], dma=True)

        w1order = [(gh_, m_) for gh_ in range(NG // 2) for m_ in range(NK2)]
        slots6 = [(PQ1.h[:, :], PQ1.b), (PQ2.h[:, :], PQ2.b),
                  (PP1.h[:, 0:512], PP1s[0]), (PP1.h[:, 512:1024], PP1s[1]),
                  (PP2.h[:, 0:512], PP2s[0]), (PP2.h[:, 512:1024], PP2s[1])]
        w1rot = [0]
        load_w1(w1order[0][1], False, 0)
        load_w1(w1order[0][1], True, 0)
        for ghalf in range(NG // 2):
            for m in range(NK2):
                i1 = ghalf * NK2 + m
                if i1 + 1 < len(w1order):
                    load_w1(w1order[i1 + 1][1], False, (i1 + 1) % 2)
                    load_w1(w1order[i1 + 1][1], True, (i1 + 1) % 2)
                wa, wg = w1a2[i1 % 2], w1g2[i1 % 2]
                for gi in range(2):
                    gidx = ghalf * 2 + gi
                    qs = slice(gidx * 512, (gidx + 1) * 512)
                    pa, pba = slots6[w1rot[0] % 6]
                    pg, pbg = slots6[(w1rot[0] + 1) % 6]
                    w1rot[0] += 2
                    for c in range(NCH):
                        P.do("t", lambda eo, c=c, qs=qs, pa=pa, wa=wa: eo.matmul(
                            pa, wa.h[:, c * 128:(c + 1) * 128],
                            h1_16[c].h[:, qs], start=(c == 0), stop=(c == NCH - 1)),
                            r=[wa.b, h1_16[c].b], w=[pba])
                    for c in range(NCH):
                        P.do("t", lambda eo, c=c, qs=qs, pg=pg, wg=wg: eo.matmul(
                            pg, wg.h[:, c * 128:(c + 1) * 128],
                            h1_16[c].h[:, qs], start=(c == 0), stop=(c == NCH - 1)),
                            r=[wg.b, h1_16[c].b], w=[pbg])
                    P.do("s", lambda eo, m=m, pa=pa: eo.activation(
                        tmp16b.h[:, :], pa, ACT.Identity,
                        bias=b1c.h[:, m:m + 1]), r=[pba, b1c.b], w=[tmp16b.b])
                    P.do("s", lambda eo, m=m, pg=pg: eo.activation(
                        gel16.h[:, :], pg, ACT.Gelu,
                        bias=b1c.h[:, NK2 + m:NK2 + m + 1]),
                        r=[pbg, b1c.b], w=[gel16.b])
                    P.do("v", lambda eo, m=m, gi=gi: eo.tensor_tensor(
                        out=h2t[m].h[:, gi * 512:(gi + 1) * 512],
                        in0=tmp16b.h[:, :], in1=gel16.h[:, :], op=ALU.mult),
                        r=[tmp16b.b, gel16.b], w=[h2t[m].b])
            for ochalf in range(2):
                for k in range(NK2):
                    wt, _ = w2half[k]
                    src = w2p.h[l][k * 128:(k + 1) * 128,
                                   ochalf * 512:(ochalf + 1) * 512]
                    P.do("g", lambda eo, src=src, wt=wt, k=k: eo.dma_start(
                        out=w2half[k][1], in_=src), r=[w2p.b], w=[wt.b], dma=True)
                for gi in range(2):
                    gidx = ghalf * 2 + gi
                    qsd = dslc(gidx * 512, (gidx + 1) * 512)
                    pslices = [(PP1.h[:, 0:512], PP1s[0]),
                               (PP1.h[:, 512:1024], PP1s[1]),
                               (PP2.h[:, 0:512], PP2s[0]),
                               (PP2.h[:, 512:1024], PP2s[1])]
                    for k in range(NK2):
                        wt, wap = w2half[k]
                        for j4 in range(4):
                            pap, pb = pslices[j4]
                            P.do("t", lambda eo, k=k, j4=j4, wap=wap, pap=pap,
                                 gi=gi: eo.matmul(
                                     pap, wap[:, j4 * 128:(j4 + 1) * 128],
                                     h2t[k].h[:, gi * 512:(gi + 1) * 512],
                                     start=(k == 0), stop=False),
                                 r=[wt.b, h2t[k].b], w=[pb])
                    for j4 in range(4):
                        oc = ochalf * 4 + j4
                        pap, pb = pslices[j4]
                        P.do("t", lambda eo, oc=oc, pap=pap: eo.matmul(
                            pap, b2row.h[0:1, oc * 128:(oc + 1) * 128],
                            onesP.h[64:65, 0:512], start=False, stop=True),
                            r=[b2row.b, onesP.b], w=[pb])
                    for j4 in range(4):
                        oc = ochalf * 4 + j4
                        pap, pb = pslices[j4]
                        P.do("v", lambda eo, oc=oc, qsd=qsd, pap=pap:
                             eo.tensor_tensor(out=xn16[oc].h[:, qsd], in0=pap,
                                              in1=xn16[oc].h[:, qsd], op=ALU.add),
                             r=[pb, xn16[oc].b], w=[xn16[oc].b])
        P.barrier()

    fin_x = [(wtmp2.h[:, :], wtmp2.b), (RW[0].h[:, 0:512], RW[0].b)]
    fin_s = [(tmp16a.h[:, :], tmp16a.b), (h2t[0].h[:, 0:512], h2t[0].b)]
    fin_u = [(tmp16b.h[:, :], tmp16b.b), (gel16.h[:, :], gel16.b)]
    fin_m = [(tmp32a.h[:, :], tmp32a.b), (RW[1].h[:, 0:512], RW[1].b)]
    fin_o = [(wtmp2.h[:, :], wtmp2.b), (RW[0].h[:, 512:1024], RW[0].b)]
    it = 0
    for c in range(NCH):
        for gidx in range(NG):
            pi = it % 2
            it += 1
            xa, xb = fin_x[pi]
            sa, sb_ = fin_s[pi]
            ua, ub = fin_u[pi]
            ma, mb = fin_m[pi]
            oa, ob = fin_o[pi]
            qsd = dslc(gidx * 512, (gidx + 1) * 512)
            qs0 = slice(gidx * 512, (gidx + 1) * 512)
            P.do("y", lambda eo, c=c, qs0=qs0, xa=xa: eo.dma_start(
                out=xa, in_=xT.h[c * 128:(c + 1) * 128, qs0]),
                r=[xT.b], w=[xb], dma=True)
            P.do("y", lambda eo, c=c, qs0=qs0, sa=sa: eo.dma_start(
                out=sa, in_=s_scr.h[c * 128:(c + 1) * 128, qs0]),
                r=[s_scr.b], w=[sb_], dma=True)
            P.do("s", lambda eo, c=c, qsd=qsd, ua=ua: eo.activation(
                ua, xn16[c].h[:, qsd], ACT.Silu),
                r=[xn16[c].b], w=[ub])
            P.do("v", lambda eo, ua=ua, sa=sa, ma=ma: eo.tensor_tensor(
                out=ma, in0=ua, in1=sa, op=ALU.mult),
                r=[ub, sb_], w=[mb])
            P.do("v", lambda eo, xa=xa, ma=ma, oa=oa: eo.tensor_tensor(
                out=oa, in0=xa, in1=ma, op=ALU.add),
                r=[xb, mb], w=[ob])
            P.do("y", lambda eo, c=c, qs0=qs0, oa=oa: eo.dma_start(
                out=out_T.h[c * 128:(c + 1) * 128, qs0], in_=oa),
                r=[ob], w=[out_T.b], dma=True)
    P.barrier()

    with nc.Block() as block:
        @block.tensor
        def _(te):
            for clo in P.q["t"]:
                clo()

        @block.vector
        def _(ve):
            for clo in P.q["v"]:
                clo()

        @block.scalar
        def _(se):
            for clo in P.q["s"]:
                clo()

        @block.sync
        def _(ye):
            for clo in P.q["y"]:
                clo()

        @block.gpsimd
        def _(ge):
            for clo in P.q["g"]:
                clo()
    es.close()
    CFG['counts'] = dict(P.cnt)
    return nc


def _host_constants(S):
    DH_ = 64
    WINDOW = 512
    half_idx = np.arange(0, DH_, 2, dtype=np.float32)
    inv_freq = 1.0 / 10000.0 ** (half_idx / DH_)
    t = np.arange(S, dtype=np.float32)
    freqs = t[:, None] * inv_freq[None, :]
    freqs = np.concatenate([freqs, freqs], axis=-1)
    base = (half_idx + 0.4 * DH_) / (1.4 * DH_)
    power = (t - S // 2) / float(WINDOW // 2)
    scale = base[None, :] ** power[:, None]
    scale = np.concatenate([scale, scale], axis=-1)
    cos, sin = np.cos(freqs), np.sin(freqs)
    sgn = np.concatenate([-np.ones(32, np.float32), np.ones(32, np.float32)])
    AQ = (cos * scale).T
    BQ = (sin * scale).T * sgn[:, None]
    AK = (cos / scale).T
    BK = (sin / scale).T * sgn[:, None]
    rotT = np.stack([np.tile(a, (2, 1)) for a in (AQ, BQ, AK, BK)]).astype(np.float16)
    import ml_dtypes as _mld
    E2 = np.zeros((128, 33), _mld.float8_e4m3fn)
    E2[0:64, 0] = 1.0
    E2[64:128, 32] = 1.0
    import ml_dtypes
    kk = np.arange(128)[:, None]
    qq = np.arange(512)[None, :]
    madd = np.zeros((128, 8 * 512), np.float32)
    for iv in range(8):
        off = iv * 128 - 512
        valid = (qq >= kk + off) & (qq <= kk + off + 512)
        madd[:, iv * 512:(iv + 1) * 512] = np.where(valid, 0.0, -28000.0)
    madd8 = madd.astype(ml_dtypes.float8_e5m2)
    return dict(rotT=rotT, E2_d=E2, madd_d=madd8)


def kernel(**inputs):
    S = CFG["S"]
    D = 1024
    HIDP = 2816
    HID = 2730
    f32 = np.float32

    x = np.ascontiguousarray(np.asarray(inputs["L0_attn_x"], f32))
    B = x.shape[0]

    def g(k):
        return np.ascontiguousarray(np.asarray(inputs[k], f32))

    w1 = g("L0_ffn_w1")
    b1 = g("L0_ffn_b1")
    w2 = g("L0_ffn_w2")
    w1p = np.zeros((2, D, 2 * HIDP), f32)
    w1p[:, :, 0:HID] = w1[:, :, 0:HID]
    w1p[:, :, HIDP:HIDP + HID] = w1[:, :, HID:2 * HID]
    b1p = np.zeros((2, 2 * HIDP), f32)
    b1p[:, 0:HID] = b1[:, 0:HID]
    b1p[:, HIDP:HIDP + HID] = b1[:, HID:2 * HID]
    w2p = np.zeros((2, HIDP, D), f32)
    w2p[:, 0:HID, :] = w2

    def cols(a):
        return np.ascontiguousarray(a.reshape(-1, 128).T)

    def ptiles(a, tw):
        K, M = a.shape
        nch, nt = K // 128, M // tw
        return np.ascontiguousarray(a.reshape(nch, 128, nt, tw)
                                    .transpose(2, 1, 0, 3)
                                    .reshape(nt, 128, nch * tw))

    shared = dict(
        Wq=ptiles(g("L0_attn_Wq"), 128).reshape(-1, 1024),
        Wk=ptiles(g("L0_attn_Wk"), 128).reshape(-1, 1024),
        Wv=ptiles(g("L0_attn_Wv"), 128).reshape(-1, 1024),
        Wo=ptiles(g("L0_attn_Wo"), 128).reshape(-1, 1024),
        Wg=ptiles(g("L0_attn_Wg"), 128).reshape(-1, 1024),
        bqk_c=np.stack([cols(g("L0_attn_bq")), cols(g("L0_attn_bk"))]),
        bg_c=cols(g("L0_attn_bg")),
        bo_r=np.ascontiguousarray(g("L0_attn_bo").reshape(1, D)),
        bv_r=np.ascontiguousarray(g("L0_attn_bv").reshape(1, D)),
        Wt_r=cols(g("L0_attn_Wt").reshape(-1)),
        bt_s=np.ascontiguousarray(g("L0_attn_bt").reshape(1, 1)),
        wqkv_qk=np.concatenate([ptiles(wq_[:, 0:1024], 128)
                                for wq_ in g("L0_ffn_wqkv")]).reshape(-1, 1024),
        wqkv_v=np.concatenate([ptiles(wq_[:, 1024:1536], 64)
                               for wq_ in g("L0_ffn_wqkv")]).reshape(-1, 512),
        fwo=np.concatenate([ptiles(f_, 128)
                            for f_ in g("L0_ffn_wo")]).reshape(-1, 512),
        w1p=np.concatenate([ptiles(w1p[i], 128)
                            for i in range(2)]).reshape(-1, 1024),
        b1p_c=np.stack([cols(b1p[i]) for i in range(2)]),
        w2p=w2p,
        b2_r=np.ascontiguousarray(g("L0_ffn_b2").reshape(2, 1, D)),
        lng_c=np.stack([cols(g("L0_ffn_ln_g")[i]) for i in range(2)]),
        lnb_c=np.stack([cols(g("L0_ffn_ln_b")[i]) for i in range(2)]),
        qsc_c=np.ascontiguousarray(
            np.tile(g("L0_ffn_q_scale"), (1, 2)).reshape(2, 128, 1)),
        ksc_c=np.ascontiguousarray(
            np.tile(g("L0_ffn_k_scale"), (1, 2)).reshape(2, 128, 1)),
        **_host_constants(S),
    )
    nc = build()
    in_maps = []
    for b in range(B):
        m = dict(shared)
        m["xT"] = np.ascontiguousarray(x[b].T)
        in_maps.append(m)
    res = run_bass_kernel_spmd(nc, in_maps, core_ids=list(range(B)))
    out = np.stack([np.asarray(r["out_T"]).T for r in res.results])
    return np.ascontiguousarray(out.astype(np.float32))


# version 21
# speedup vs baseline: 1.0268x; 1.0268x over previous
import numpy as np
from contextlib import ExitStack

import concourse.bass as bass
import concourse.mybir as mybir
from concourse.bass_utils import run_bass_kernel_spmd

F32 = mybir.dt.float32
F16 = mybir.dt.float16
ALU = mybir.AluOpType
ACT = mybir.ActivationFunctionType
AX = mybir.AxisListType

CFG = dict(S=2048, NPROBE=4)


def _config(S=2048, NPROBE=4):
    CFG["S"] = S
    CFG["NPROBE"] = NPROBE


class Buf:
    __slots__ = ("w", "r")

    def __init__(self):
        self.w = None
        self.r = {}


class Tile:
    __slots__ = ("h", "b")

    def __init__(self, h):
        self.h = h
        self.b = Buf()


class Prog:
    ENGS = ("t", "v", "s", "y", "g")
    KDMA = 4

    def __init__(self, nc, es):
        self.nc = nc
        self.semids = ["t", "v", "s"] + [f"y{i}" for i in range(self.KDMA)] \
            + [f"g{i}" for i in range(self.KDMA)]
        self.sem = {sid: es.enter_context(nc.semaphore("sem_" + sid))
                    for sid in self.semids}
        self.cnt = {sid: 0 for sid in self.semids}
        self.dman = {"y": 0, "g": 0}
        self.seen = {e: {sid: 0 for sid in self.semids} for e in self.ENGS}
        self.q = {e: [] for e in self.ENGS}
        self.engobj = dict(t=nc.tensor, v=nc.vector, s=nc.scalar, y=nc.sync,
                           g=nc.gpsimd)

    def do(self, e, fn, r=(), w=(), dma=False):
        need = {}

        def add(src):
            if src is None:
                return
            f, c = src
            if e == "t" and f == "t":
                return
            if c > need.get(f, 0):
                need[f] = c

        for b in r:
            add(b.w)
        for b in w:
            add(b.w)
            for f, c in b.r.items():
                add((f, c))
        if dma:
            n = self.dman[e]
            sid = f"{e}{n % self.KDMA}"
            self.dman[e] = n + 1
            if 16 * (n // self.KDMA) > need.get(sid, 0):
                need[sid] = 16 * (n // self.KDMA)
            inc = 16
        else:
            sid = e
            inc = 1
        waits = []
        for f, c in need.items():
            if self.seen[e][f] < c:
                self.seen[e][f] = c
                waits.append((f, c))
        self.cnt[sid] += inc
        mycnt = self.cnt[sid]

        def clo(eo=self.engobj[e], waits=tuple(waits), fn=fn, sem=self.sem[sid],
                inc=inc, sems=self.sem):
            for f, c in waits:
                eo.wait_ge(sems[f], c)
            fn(eo).then_inc(sem, inc)

        self.q[e].append(clo)
        src = (sid, mycnt)
        for b in r:
            if b.r.get(sid, 0) < mycnt:
                b.r[sid] = mycnt
        for b in w:
            b.w = src
            b.r = {}
        return src

    def raw(self, e, fn):
        self.q[e].append(lambda eo=self.engobj[e], fn=fn: fn(eo))

    def barrier(self):
        snap = dict(self.cnt)
        sems = self.sem
        for e in self.ENGS:
            waits = []
            for sid in self.semids:
                if sid != e and self.seen[e][sid] < snap[sid]:
                    self.seen[e][sid] = snap[sid]
                    waits.append((sid, snap[sid]))

            def clo(eo=self.engobj[e], waits=tuple(waits)):
                for f, c in waits:
                    eo.wait_ge(sems[f], c)

            self.q[e].append(clo)


def build():
    S = CFG["S"]
    NPROBE = CFG["NPROBE"]
    D = 1024
    NCH = 8
    H = 16
    DH = 64
    KK = S // 4
    NQT = S // 128
    NG = S // 512
    QTH = NQT // 2
    HF = 8
    INNER = 512
    NW = S // 512
    HIDP = 2816
    NM1 = 2 * HIDP // 128
    NK2 = HIDP // 128
    SD = 2
    HLV = max(1, S // 1024)

    nc = bass.Bass("TRN2", target_bir_lowering=False)
    es = ExitStack()
    P = Prog(nc, es)

    def dram(name, shape, dt, kind):
        return Tile(nc.dram_tensor(name, shape, dt, kind=kind).ap())

    def sb(name, shape, dt):
        return Tile(es.enter_context(nc.sbuf_tensor(name, shape, dt)))

    def psum(name, shape, dt=F32):
        return Tile(es.enter_context(nc.psum_tensor(name, shape, dt)))

    xT = dram("xT", [D, S], F32, "ExternalInput")
    Wq = dram("Wq", [NCH * 128, D], F32, "ExternalInput")
    Wk = dram("Wk", [NCH * 128, D], F32, "ExternalInput")
    Wv = dram("Wv", [NCH * 128, D], F32, "ExternalInput")
    Wo = dram("Wo", [NCH * 128, D], F32, "ExternalInput")
    Wg = dram("Wg", [NCH * 128, D], F32, "ExternalInput")
    bqk_c = dram("bqk_c", [2, 128, NCH], F32, "ExternalInput")
    bg_c = dram("bg_c", [128, NCH], F32, "ExternalInput")
    bo_r = dram("bo_r", [1, D], F32, "ExternalInput")
    bv_r = dram("bv_r", [1, D], F32, "ExternalInput")
    Wt_r = dram("Wt_r", [128, NCH], F32, "ExternalInput")
    bt_s = dram("bt_s", [1, 1], F32, "ExternalInput")
    wqkv_qk = dram("wqkv_qk", [2 * 8 * 128, 1024], F32, "ExternalInput")
    wqkv_v = dram("wqkv_v", [2 * HF * 128, 512], F32, "ExternalInput")
    fwo = dram("fwo", [2 * 8 * 128, 512], F32, "ExternalInput")
    w1p = dram("w1p", [2 * NM1 * 128, 1024], F32, "ExternalInput")
    b1p_c = dram("b1p_c", [2, 128, NM1], F32, "ExternalInput")
    w2p = dram("w2p", [2, HIDP, D], F32, "ExternalInput")
    b2_r = dram("b2_r", [2, 1, D], F32, "ExternalInput")
    lng_c = dram("lng_c", [2, 128, NCH], F32, "ExternalInput")
    lnb_c = dram("lnb_c", [2, 128, NCH], F32, "ExternalInput")
    qsc_c = dram("qsc_c", [2, 128, 1], F32, "ExternalInput")
    ksc_c = dram("ksc_c", [2, 128, 1], F32, "ExternalInput")
    rotT = dram("rotT", [4, 128, S], F16, "ExternalInput")
    E2_d = dram("E2_d", [128, 33], mybir.dt.float8e4, "ExternalInput")
    madd_d = dram("madd_d", [128, 8 * 512], mybir.dt.float8e5, "ExternalInput")

    out_T = dram("out_T", [D, S], F32, "ExternalOutput")
    s_scr = dram("s_scr", [D, S], F16, "Internal")
    sp_scr = dram("sp_scr", [D, S], F16, "Internal")

    F8 = mybir.dt.float8e4

    def view(pack, p0, p1):
        t = Tile.__new__(Tile)
        t.h = pack.h[p0:p1, :]
        t.b = pack.b
        return t

    def cview(pack, c0, w):
        t = Tile.__new__(Tile)
        t.h = pack.h[:, c0:c0 + w]
        t.b = Buf()
        return t

    xn16 = [sb(f"xn16_{c}", [128, S + SD], F16) for c in range(NCH)]
    P8 = [sb(f"P8_{i}", [128, S + SD], F16) for i in range(8)]
    q16p = sb("q16p", [128, S + SD], F16)
    k16p = sb("k16p", [128, S + SD], F16)
    q16pB = sb("q16pB", [128, S + SD], F16)
    k16pB = sb("k16pB", [128, S + SD], F16)
    sq16 = sb("sq16", [128, S + SD], F16)
    dump8 = sb("dump8", [128, S], F8)
    s16 = P8
    h2t = [sb(f"h2t_{k}", [128, 1024], F16) for k in range(NK2)]
    wbig = [sb(f"wbig_{i}", [128, D], F16) for i in range(11)]
    wq16 = wbig[0:2]
    wk16 = wbig[2:4]
    wv16 = wbig[4:6]
    wo16 = wbig[6:8]
    wg16 = wbig[8:10]
    wqkv16 = [wbig[10], wbig[0]]
    w1ab = sb("w1ab", [128, D], F16)
    w1gb = sb("w1gb", [128, D], F16)
    fwo16 = [sb(f"fwo16_{i}", [128, INNER], F16) for i in range(2)]
    rot2 = [sb(f"rot2_{i}", [128, 512], F16) for i in range(2)]
    yT = [sb(f"yT_{k}", [128, 512], F16) for k in range(3)]
    vhat = [[sb(f"vh_{hh}_{t}", [128, DH + 1], F16) for t in range(NQT)]
            for hh in range(2)]
    E2 = sb("E2", [128, 33], mybir.dt.float8e4)
    madd = sb("madd", [128, 8 * 512], mybir.dt.float8e5)
    onesP = sb("onesP", [128, 512], F16)
    ones_c = sb("ones_c", [128, 1], F16)
    epsc = sb("epsc", [128, 1], F32)
    clampc = sb("clampc", [128, 1], F32)
    ln8c = sb("ln8c", [128, 1], F32)
    bias_q = sb("bias_q", [128, NCH], F32)
    bias_k = sb("bias_k", [128, NCH], F32)
    bias_g = sb("bias_g", [128, NCH], F32)
    b1c = sb("b1c", [128, NM1], F32)
    lngc = sb("lngc", [128, NCH], F32)
    lnbc = sb("lnbc", [128, NCH], F32)
    qscol = sb("qscol", [128, 1], F32)
    kscol = sb("kscol", [128, 1], F32)
    RW = [sb(f"RW_{i}", [128, S], F32) for i in range(2)]
    rrowA = view(RW[0], 0, 33)
    srow = view(RW[0], 0, 1)
    s2row = view(RW[1], 0, 1)
    murow = srow
    rstdrow = s2row
    R16 = [sb(f"R16_{i}", [128, S], F16) for i in range(2)]
    rrow16A = view(R16[0], 0, 33)
    rrow16 = [view(R16[0], 0, 1), view(R16[0], 32, 33)]
    murow16 = view(R16[0], 0, 1)
    rstdrow16 = view(R16[1], 0, 1)
    throw = view(R16[1], 64, 65)

    brows = sb("brows", [128, D], F16)
    borow = view(brows, 0, 1)
    bvrow = view(brows, 32, 33)
    b2row = view(brows, 64, 65)
    thb = sb("thb", [128, 512], F16)
    tcol16 = cview(thb, 0, 1)
    rd16 = view(thb, 64, 65)
    th16t = cview(thb, 8, QTH)
    wtmp2 = sb("wtmp2", [128, 512], F32)
    tmp32a = sb("tmp32a", [128, 512], F32)
    rd32 = view(tmp32a, 64, 65)
    tmp16a = sb("tmp16a", [128, 512], F16)
    tmp16b = sb("tmp16b", [128, 512], F16)
    gel16 = thb
    rdb16 = view(tmp16b, 64, 128)
    itemp = sb("itemp", [1, 1], F32)
    lnit = sb("lnit", [128, 1], F32)
    scal1 = cview(wtmp2, 0, 1)
    scal16 = cview(thb, 16, 1)
    tau0c = sb("tau0c", [128, 1], F32)
    irho0c = sb("irho0c", [128, 1], F32)
    _stnames = ("tauA", "tauB", "cA", "cB", "lo", "hi", "bt", "bd", "d1",
                "t1", "t2", "t3", "tnA", "tnB")
    _stpack = sb("stpack", [128, 14 * QTH], F32)

    st = {n: cview(_stpack, i * QTH, QTH) for i, n in enumerate(_stnames)}
    stc = {n: (cview(_stpack, _stnames.index(n) * QTH, QTH // 2),
               cview(_stpack, _stnames.index(n) * QTH + QTH // 2, QTH - QTH // 2))
           for n in ("cA", "cB")}
    _mpack = sb("mpack", [128, 2 * QTH], mybir.dt.uint8)
    msk = cview(_mpack, 0, QTH)
    mskn = cview(_mpack, QTH, QTH)
    meanT = cview(_stpack, 0, NCH)
    wtT = cview(_stpack, NCH, NCH)
    tcol = cview(_stpack, 2 * NCH, 1)

    PP1 = psum("PP1", [128, 1024])
    PP2 = psum("PP2", [128, 1024])
    PQ1 = psum("PQ1", [128, 512])
    PQ2 = psum("PQ2", [128, 512])
    PQ3 = psum("PQ3", [128, 512])
    PP1s = [Buf(), Buf()]
    PP2s = [Buf(), Buf()]

    DATA = slice(SD, SD + S)
    SH = slice(SD - 1, SD - 1 + S)

    def dslc(a, b2):
        return slice(SD + a, SD + b2)

    x16 = s16
    for c in range(NCH):
        P.do("g", lambda eo, c=c: eo.dma_start(
            out=x16[c].h[:, DATA], in_=xT.h[c * 128:(c + 1) * 128, :]),
            r=[xT.b], w=[x16[c].b], dma=True)
    for src_d, dst in ((E2_d, E2), (madd_d, madd), (bg_c, bias_g), (Wt_r, wtT)):
        P.do("y", lambda eo, src_d=src_d, dst=dst: eo.dma_start(
            out=dst.h[:, :], in_=src_d.h), r=[src_d.b], w=[dst.b], dma=True)
    for i, dst in ((0, bias_q), (1, bias_k)):
        P.do("y", lambda eo, i=i, dst=dst: eo.dma_start(
            out=dst.h[:, :], in_=bqk_c.h[i]), r=[bqk_c.b], w=[dst.b], dma=True)
    for src_d, dst in ((bo_r, borow), (bv_r, bvrow)):
        P.do("g", lambda eo, src_d=src_d, dst=dst: eo.dma_start(
            out=dst.h[:, :], in_=src_d.h), r=[src_d.b], w=[dst.b], dma=True)
    P.do("v", lambda eo: eo.memset(ones_c.h[:, :], 1.0), w=[ones_c.b])
    P.do("v", lambda eo: eo.memset(onesP.h[:, :], 1.0), w=[onesP.b])
    P.do("v", lambda eo: eo.memset(epsc.h[:, :], 1e-5), w=[epsc.b])
    P.do("v", lambda eo: eo.memset(clampc.h[:, :], 1e-18), w=[clampc.b])
    P.do("v", lambda eo: eo.memset(ln8c.h[:, :], float(np.log(8.0))), w=[ln8c.b])
    P.do("v", lambda eo: eo.memset(st["t1"].h[:, :], 0.0), w=[st["t1"].b])
    for hh in range(2):
        for t in range(NQT):
            P.do("v", lambda eo, hh=hh, t=t: eo.memset(
                vhat[hh][t].h[:, DH:DH + 1], 1.0), w=[vhat[hh][t].b])
    for c in range(NCH):
        P.do("v", lambda eo, c=c: eo.memset(xn16[c].h[:, 0:SD], 0.0), w=[xn16[c].b])

    def ln_stats(slices):
        for piece in range(NG):
            cols = [slice(a.start + piece * 512, a.start + (piece + 1) * 512)
                    for _, a in slices]
            for c in range(NCH):
                t, _ = slices[c]
                P.do("t", lambda eo, t=t, c=c, cc=cols[c]: eo.matmul(
                    PQ1.h[0:1, :], ones_c.h[:, :], t.h[:, cc],
                    start=(c == 0), stop=(c == NCH - 1)), r=[t.b, ones_c.b], w=[PQ1.b])
            P.do("s", lambda eo, piece=piece: eo.activation(
                srow.h[0:1, piece * 512:(piece + 1) * 512], PQ1.h[0:1, :],
                ACT.Identity), r=[PQ1.b], w=[srow.b])
            for c in range(NCH):
                t, _ = slices[c]
                P.do("v", lambda eo, t=t, cc=cols[c]: eo.tensor_tensor(
                    out=sq16.h[:, 0:512], in0=t.h[:, cc], in1=t.h[:, cc],
                    op=ALU.mult), r=[t.b], w=[sq16.b])
                P.do("t", lambda eo, c=c: eo.matmul(
                    PQ2.h[0:1, :], ones_c.h[:, :], sq16.h[:, 0:512],
                    start=(c == 0), stop=(c == NCH - 1)),
                    r=[sq16.b, ones_c.b], w=[PQ2.b])
            P.do("s", lambda eo, piece=piece: eo.activation(
                s2row.h[0:1, piece * 512:(piece + 1) * 512], PQ2.h[0:1, :],
                ACT.Identity), r=[PQ2.b], w=[s2row.b])

    def ln_rows():
        P.do("v", lambda eo: eo.tensor_scalar(out=murow.h[0:1, :], in0=srow.h[0:1, :],
             scalar1=1.0 / D, scalar2=None, op0=ALU.mult), r=[srow.b], w=[murow.b])
        for jh in range(S // 1024):
            hs = slice(jh * 1024, (jh + 1) * 1024)
            P.do("v", lambda eo, hs=hs: eo.tensor_tensor(
                out=PP1.h[0:1, :], in0=murow.h[0:1, hs],
                in1=murow.h[0:1, hs], op=ALU.mult), r=[murow.b], w=[PP1.b])
            P.do("v", lambda eo, hs=hs: eo.tensor_scalar(
                out=rstdrow.h[0:1, hs], in0=s2row.h[0:1, hs],
                scalar1=1.0 / D, scalar2=None, op0=ALU.mult),
                r=[s2row.b], w=[rstdrow.b])
            P.do("v", lambda eo, hs=hs: eo.tensor_tensor(
                out=rstdrow.h[0:1, hs], in0=rstdrow.h[0:1, hs],
                in1=PP1.h[0:1, :], op=ALU.subtract),
                r=[PP1.b, rstdrow.b], w=[rstdrow.b])
        P.do("s", lambda eo: eo.activation(rstdrow.h[0:1, :], rstdrow.h[0:1, :],
             ACT.Ln, bias=epsc.h[0:1, :]), r=[rstdrow.b, epsc.b], w=[rstdrow.b])
        P.do("s", lambda eo: eo.activation(rstdrow.h[0:1, :], rstdrow.h[0:1, :],
             ACT.Exp, scale=-0.5), r=[rstdrow.b], w=[rstdrow.b])
        P.do("v", lambda eo: eo.tensor_tensor(out=murow16.h[0:1, :],
             in0=murow.h[0:1, :], in1=rstdrow.h[0:1, :], op=ALU.mult),
             r=[murow.b, rstdrow.b], w=[murow16.b])
        P.do("v", lambda eo: eo.tensor_copy(rstdrow16.h[0:1, :], rstdrow.h[0:1, :]),
             r=[rstdrow.b], w=[rstdrow16.b])

    def bcast_to(pp, row16, half):
        c0 = half * 1024
        for j in range(2):
            P.do("t", lambda eo, j=j, c0=c0, pp=pp, row16=row16: eo.matmul(
                pp.h[:, j * 512:(j + 1) * 512], onesP.h[0:1, 0:128],
                row16.h[0:1, c0 + j * 512:c0 + (j + 1) * 512], start=True, stop=True),
                r=[onesP.b, row16.b], w=[pp.b])

    def ssq_rows(sqtile):
        for piece in range(NG):
            PQ = PQ2 if piece % 2 == 0 else PQ3
            P.do("t", lambda eo, piece=piece, PQ=PQ, sqtile=sqtile:
                 eo.matmul(PQ.h[0:33, :], E2.h[:, :],
                           sqtile.h[:, piece * 512:(piece + 1) * 512],
                           start=True, stop=True), r=[sqtile.b, E2.b], w=[PQ.b])
            eng = "s" if piece % 2 == 0 else "v"
            if eng == "s":
                P.do("s", lambda eo, piece=piece, PQ=PQ: eo.activation(
                    rrowA.h[:, piece * 512:(piece + 1) * 512], PQ.h[0:33, :],
                    ACT.Identity), r=[PQ.b], w=[rrowA.b])
            else:
                P.do("v", lambda eo, piece=piece, PQ=PQ: eo.tensor_copy(
                    rrowA.h[:, piece * 512:(piece + 1) * 512], PQ.h[0:33, :]),
                    r=[PQ.b], w=[rrowA.b])

    def finish_rrows(with_itemp):
        P.do("v", lambda eo: eo.tensor_scalar(
            out=rrowA.h[:, :], in0=rrowA.h[:, :], scalar1=clampc.h[0:33, :],
            scalar2=None, op0=ALU.max), r=[rrowA.b, clampc.b], w=[rrowA.b])
        P.do("s", lambda eo: eo.activation(rrowA.h[:, :], rrowA.h[:, :], ACT.Ln),
             r=[rrowA.b], w=[rrowA.b])
        if with_itemp:
            P.do("s", lambda eo: eo.activation(rrow16A.h[:, :], rrowA.h[:, :],
                 ACT.Exp, scale=-0.5, bias=lnit.h[0:33, :]),
                 r=[rrowA.b, lnit.b], w=[rrow16A.b])
        else:
            P.do("s", lambda eo: eo.activation(rrow16A.h[:, :], rrowA.h[:, :],
                 ACT.Exp, scale=-0.5), r=[rrowA.b], w=[rrow16A.b])

    def bcast_row16(pp, hh, half):
        c0 = half * 1024
        b0 = hh * 32
        for j in range(2):
            P.do("t", lambda eo, j=j, c0=c0, pp=pp, b0=b0, hh=hh: eo.matmul(
                pp.h[:, j * 512:(j + 1) * 512], onesP.h[b0:b0 + 1, 0:128],
                rrow16[hh].h[0:1, c0 + j * 512:c0 + (j + 1) * 512],
                start=True, stop=True), r=[onesP.b, rrow16A.b], w=[pp.b])

    def av_divide(dst_fn):
        P.do("s", lambda eo: eo.activation(rd32.h[0:1, :], PQ3.h[DH:DH + 1, :],
             ACT.Ln), r=[PQ3.b], w=[rd32.b])
        P.do("s", lambda eo: eo.activation(rd16.h[0:1, :], rd32.h[0:1, :],
             ACT.Exp, scale=-1.0), r=[rd32.b], w=[rd16.b])
        P.do("t", lambda eo: eo.matmul(PQ1.h[0:64, :], onesP.h[64:65, 0:64],
             rd16.h[0:1, :], start=True, stop=True), r=[onesP.b, rd16.b], w=[PQ1.b])
        P.do("s", lambda eo: eo.activation(rdb16.h[:, :], PQ1.h[0:64, :],
             ACT.Identity), r=[PQ1.b], w=[rdb16.b])
        P.do("v", lambda eo: eo.tensor_tensor(out=tmp16a.h[0:64, :],
             in0=PQ3.h[0:DH, :], in1=rdb16.h[:, :], op=ALU.mult),
             r=[PQ3.b, rdb16.b], w=[tmp16a.b])
        dst_fn()

    ln_stats([(x16[c], DATA) for c in range(NCH)])
    ln_rows()
    for half in range(HLV):
        bcast_to(PP1, rstdrow16, half)
        bcast_to(PP2, murow16, half)
        for c in range(NCH):
            for j in range(2):
                cs = dslc(half * 1024 + j * 512, half * 1024 + (j + 1) * 512)
                ps = slice(j * 512, (j + 1) * 512)
                P.do("v", lambda eo, c=c, cs=cs, ps=ps: eo.tensor_tensor(
                    out=wtmp2.h[:, :], in0=x16[c].h[:, cs], in1=PP1.h[:, ps],
                    op=ALU.mult), r=[x16[c].b, PP1.b], w=[wtmp2.b])
                P.do("v", lambda eo, c=c, cs=cs, ps=ps: eo.tensor_tensor(
                    out=xn16[c].h[:, cs], in0=wtmp2.h[:, :], in1=PP2.h[:, ps],
                    op=ALU.subtract), r=[wtmp2.b, PP2.b], w=[xn16[c].b])
    P.barrier()

    for c in range(NCH):
        P.do("s", lambda eo, c=c: eo.activation(
            sq16.h[:, 0:S], xn16[c].h[:, DATA], ACT.Identity,
            accum_out=meanT.h[:, c:c + 1]), r=[xn16[c].b], w=[sq16.b, meanT.b])
    P.do("v", lambda eo: eo.tensor_tensor(out=meanT.h[:, :], in0=meanT.h[:, :],
         in1=wtT.h[:, :], op=ALU.mult), r=[meanT.b, wtT.b], w=[meanT.b])
    P.do("v", lambda eo: eo.tensor_reduce(out=tcol.h[:, :], in_=meanT.h[:, :],
         axis=AX.X, op=ALU.add), r=[meanT.b], w=[tcol.b])
    P.do("v", lambda eo: eo.tensor_scalar(out=tcol16.h[:, :], in0=tcol.h[:, :],
         scalar1=1.0 / S, scalar2=None, op0=ALU.mult), r=[tcol.b], w=[tcol16.b])
    P.do("t", lambda eo: eo.matmul(PQ1.h[0:1, 0:1], tcol16.h[:, :], ones_c.h[:, :],
         start=True, stop=True), r=[tcol16.b, ones_c.b], w=[PQ1.b])
    P.do("y", lambda eo: eo.dma_start(out=scal1.h[0:1, :], in_=bt_s.h),
         r=[bt_s.b], w=[scal1.b], dma=True)
    P.do("s", lambda eo: eo.activation(itemp.h[0:1, :], PQ1.h[0:1, 0:1], ACT.Sigmoid,
         bias=scal1.h[0:1, :]), r=[PQ1.b, scal1.b], w=[itemp.b])
    P.do("v", lambda eo: eo.tensor_scalar(out=itemp.h[0:1, :], in0=itemp.h[0:1, :],
         scalar1=0.5, scalar2=None, op0=ALU.add), r=[itemp.b], w=[itemp.b])
    P.do("v", lambda eo: eo.reciprocal(itemp.h[0:1, :], itemp.h[0:1, :]),
         r=[itemp.b], w=[itemp.b])
    P.do("s", lambda eo: eo.activation(scal1.h[0:1, :], itemp.h[0:1, :], ACT.Ln),
         r=[itemp.b], w=[scal1.b])
    P.do("v", lambda eo: eo.tensor_copy(scal16.h[0:1, :], scal1.h[0:1, :]),
         r=[scal1.b], w=[scal16.b])
    P.do("t", lambda eo: eo.matmul(PQ1.h[:, 0:1], onesP.h[0:1, 0:128],
         scal16.h[0:1, :], start=True, stop=True), r=[onesP.b, scal16.b], w=[PQ1.b])
    P.do("v", lambda eo: eo.tensor_copy(lnit.h[:, :], PQ1.h[:, 0:1]),
         r=[PQ1.b], w=[lnit.b])
    for mul, dst in ((0.6744898 * 0.125, tau0c), (1.0 / 5207.5, irho0c)):
        P.do("v", lambda eo, mul=mul: eo.tensor_scalar(out=scal1.h[0:1, :],
             in0=itemp.h[0:1, :], scalar1=mul, scalar2=None, op0=ALU.mult),
             r=[itemp.b], w=[scal1.b])
        P.do("v", lambda eo: eo.tensor_copy(scal16.h[0:1, :], scal1.h[0:1, :]),
             r=[scal1.b], w=[scal16.b])
        P.do("t", lambda eo: eo.matmul(PQ1.h[:, 0:1], onesP.h[0:1, 0:128],
             scal16.h[0:1, :], start=True, stop=True),
             r=[onesP.b, scal16.b], w=[PQ1.b])
        P.do("v", lambda eo, dst=dst: eo.tensor_copy(dst.h[:, :], PQ1.h[:, 0:1]),
             r=[PQ1.b], w=[dst.b])
    P.barrier()

    def load_cols(Wd, dst, col0, ncol=128):
        t_ = col0 // 128
        src = Wd.h[t_ * 128:(t_ + 1) * 128, :]
        P.do("g", lambda eo, src=src, dst=dst: eo.dma_start(
            out=dst.h[:, 0:(ncol * NCH)], in_=src), r=[Wd.b], w=[dst.b], dma=True)

    def load_wpair(p, slot):
        load_cols(Wq, wq16[slot], p * 128)
        load_cols(Wk, wk16[slot], p * 128)
        load_cols(Wv, wv16[slot], p * 128)

    def vT_make(wvt, hh, bias_row, bias_off, src_tiles, src_off):
        for t in range(NQT):
            PQ = PQ1 if t % 2 == 0 else PQ2
            for c in range(NCH):
                tt, off = src_tiles[c], src_off[c]
                P.do("t", lambda eo, t=t, c=c, PQ=PQ, tt=tt, off=off, wvt=wvt, hh=hh:
                     eo.matmul(PQ.h[:, 0:64],
                               tt.h[:, off + t * 128:off + (t + 1) * 128],
                               wvt.h[:, c * 128 + hh * 64:c * 128 + hh * 64 + 64],
                               start=(c == 0), stop=False),
                     r=[tt.b, wvt.b], w=[PQ.b])
            P.do("t", lambda eo, t=t, PQ=PQ, bias_row=bias_row, bias_off=bias_off,
                 hh=hh: eo.matmul(
                     PQ.h[:, 0:64], onesP.h[32:33, 0:128],
                     bias_row.h[0:1, bias_off + hh * 64:bias_off + hh * 64 + 64],
                     start=False, stop=True),
                 r=[onesP.b, bias_row.b], w=[PQ.b])
            eng = "v" if t % 2 == 0 else "s"
            if eng == "v":
                P.do("v", lambda eo, hh=hh, t=t, PQ=PQ: eo.tensor_copy(
                    vhat[hh][t].h[:, 0:DH], PQ.h[:, 0:64]),
                    r=[PQ.b], w=[vhat[hh][t].b])
            else:
                P.do("s", lambda eo, hh=hh, t=t, PQ=PQ: eo.activation(
                    vhat[hh][t].h[:, 0:DH], PQ.h[:, 0:64],
                    ACT.Identity), r=[PQ.b], w=[vhat[hh][t].b])

    pending = [None]

    def emit_av(ctx):
        if ctx is None:
            return
        hh, hb, hd, halfb, qt, kt_ = ctx
        for gi in range(NG // 2):
            gidx = halfb * (NG // 2) + gi
            qs = dslc(gidx * 512, (gidx + 1) * 512)
            P.do("t", lambda eo, gidx=gidx: eo.matmul(PQ2.h[:, :],
                 onesP.h[64:65, 0:128],
                 throw.h[0:1, gidx * 512:(gidx + 1) * 512],
                 start=True, stop=True), r=[onesP.b, throw.b], w=[PQ2.b])
            P.do("s", lambda eo: eo.activation(thb.h[:, :], PQ2.h[:, :],
                 ACT.Identity), r=[PQ2.b], w=[thb.b])
            def qkb(kc, qs=qs, hb=hb):
                PQ = PQ1 if kc % 2 == 0 else PQ2
                P.do("t", lambda eo, kc=kc, qs=qs, PQ=PQ, hb=hb: eo.matmul(
                    PQ.h[:, :],
                    kt_.h[hb:hb + 64, SD + kc * 128:SD + (kc + 1) * 128],
                    qt.h[hb:hb + 64, qs], start=True, stop=True),
                    r=[kt_.b, qt.b], w=[PQ.b])
            qkb(0)
            for kc in range(NQT):
                PQ = PQ1 if kc % 2 == 0 else PQ2
                yt = yT[kc % 3]
                if kc + 1 < NQT:
                    qkb(kc + 1)
                P.do("s", lambda eo, PQ=PQ, yt=yt: eo.activation(
                    yt.h[:, :], PQ.h[:, :], ACT.Exp), r=[PQ.b], w=[yt.b])
                P.do("v", lambda eo, yt=yt: eo.tensor_tensor(
                    out=tmp16a.h[:, :], in0=yt.h[:, :], in1=thb.h[:, :],
                    op=ALU.is_gt), r=[yt.b, thb.b], w=[tmp16a.b])
                P.do("v", lambda eo, yt=yt: eo.tensor_tensor(
                    out=yt.h[:, :], in0=yt.h[:, :], in1=tmp16a.h[:, :],
                    op=ALU.mult), r=[yt.b, tmp16a.b], w=[yt.b])
                P.do("t", lambda eo, kc=kc, hh=hh, yt=yt: eo.matmul(
                    PQ3.h[0:DH + 1, :], vhat[hh][kc].h[:, :], yt.h[:, :],
                    start=(kc == 0), stop=(kc == NQT - 1)),
                    r=[vhat[hh][kc].b, yt.b], w=[PQ3.b])

            def store(hd=hd, gidx=gidx):
                P.do("y", lambda eo: eo.dma_start(
                    out=sp_scr.h[hd * 64:(hd + 1) * 64,
                                 gidx * 512:(gidx + 1) * 512],
                    in_=tmp16a.h[0:64, :]), r=[tmp16a.b], w=[sp_scr.b],
                    dma=True)
            av_divide(store)


    load_wpair(0, 0)
    for p in range(H // 2):
        slot = p % 2
        if p + 1 < H // 2:
            load_wpair(p + 1, 1 - slot)
        emit_av(pending[0])
        pending[0] = None
        qt = q16p if p % 2 == 0 else q16pB
        kt_ = k16p if p % 2 == 0 else k16pB
        for wtl, bias_t, dst16, is_q in ((wq16, bias_q, qt, True),
                                         (wk16, bias_k, kt_, False)):
            wt16 = wtl[slot]
            for half in range(HLV):
                PP = PP1 if half == 0 else PP2
                for c in range(NCH):
                    for j in range(2):
                        P.do("t", lambda eo, c=c, j=j, half=half, wt16=wt16, PP=PP:
                             eo.matmul(PP.h[:, j * 512:(j + 1) * 512],
                                       wt16.h[:, c * 128:(c + 1) * 128],
                                       xn16[c].h[:, SD + half * 1024 + j * 512:
                                                 SD + half * 1024 + (j + 1) * 512],
                                       start=(c == 0), stop=(c == NCH - 1)),
                             r=[wt16.b, xn16[c].b], w=[PP.b])
                cs = dslc(half * 1024, (half + 1) * 1024)
                P.do("s", lambda eo, PP=PP, cs=cs, bias_t=bias_t, dst16=dst16, p=p:
                     eo.activation(dst16.h[:, cs], PP.h[:, :], ACT.Identity,
                                   bias=bias_t.h[:, p:p + 1]),
                     r=[PP.b, bias_t.b], w=[dst16.b])
                csq = slice(half * 1024, (half + 1) * 1024)
                P.do("s", lambda eo, PP=PP, csq=csq, bias_t=bias_t, p=p:
                     eo.activation(sq16.h[:, csq], PP.h[:, :], ACT.Square,
                                   bias=bias_t.h[:, p:p + 1]),
                     r=[PP.b, bias_t.b], w=[sq16.b])
            ssq_rows(sq16)
            finish_rrows(is_q)
            for hh in range(2):
                hb = hh * 64
                for half in range(HLV):
                    PP = PP1 if half == 0 else PP2
                    bcast_row16(PP, hh, half)
                    cs = dslc(half * 1024, (half + 1) * 1024)
                    P.do("v", lambda eo, dst16=dst16, hb=hb, cs=cs, PP=PP:
                         eo.tensor_tensor(out=dst16.h[hb:hb + 64, cs],
                                          in0=dst16.h[hb:hb + 64, cs],
                                          in1=PP.h[hb:hb + 64, :], op=ALU.mult),
                         r=[dst16.b, PP.b], w=[dst16.b])
        for hh in range(2):
            vT_make(wv16[slot], hh, bvrow, p * 128,
                    [xn16[c] for c in range(NCH)], [SD] * NCH)
            hb = hh * 64
            hd = p * 2 + hh
            for halfb in range(2):
                for ti in range(QTH):
                    t = halfb * QTH + ti
                    PP = PP1 if ti % 2 == 0 else PP2
                    for hf2 in range(NG // 2):
                        for jj in range(2):
                            ks = dslc((hf2 * 2 + jj) * 512, (hf2 * 2 + jj + 1) * 512)
                            P.do("t", lambda eo, t=t, jj=jj, PP=PP, hb=hb,
                                 ks=ks, qt=qt, kt_=kt_:
                                 eo.matmul(PP.h[:, jj * 512:(jj + 1) * 512],
                                           qt.h[hb:hb + 64,
                                                SD + t * 128:SD + (t + 1) * 128],
                                           kt_.h[hb:hb + 64, ks],
                                           start=True, stop=True),
                                 r=[qt.b, kt_.b], w=[PP.b])
                        if (ti + hf2) % 2 == 0:
                            P.do("s", lambda eo, ti=ti, hf2=hf2, PP=PP:
                                 eo.activation(
                                     s16[ti].h[:, hf2 * 1024:(hf2 + 1) * 1024],
                                     PP.h[:, :], ACT.Identity),
                                 r=[PP.b], w=[s16[ti].b])
                        else:
                            P.do("v", lambda eo, ti=ti, hf2=hf2, PP=PP:
                                 eo.tensor_copy(
                                     s16[ti].h[:, hf2 * 1024:(hf2 + 1) * 1024],
                                     PP.h[:, :]), r=[PP.b], w=[s16[ti].b])
                P.do("v", lambda eo: eo.tensor_scalar(out=st["tauA"].h[:, :],
                     in0=st["t1"].h[:, :], scalar1=0.0, scalar2=tau0c.h[:, :],
                     op0=ALU.mult, op1=ALU.add), r=[st["t1"].b, tau0c.b],
                     w=[st["tauA"].b])
                P.do("v", lambda eo: eo.memset(st["lo"].h[:, :], -4.2), w=[st["lo"].b])
                P.do("v", lambda eo: eo.memset(st["hi"].h[:, :], 4.2), w=[st["hi"].b])
                P.do("v", lambda eo: eo.memset(st["bd"].h[:, :], 1e9), w=[st["bd"].b])
                P.do("v", lambda eo: eo.memset(st["bt"].h[:, :], 0.0), w=[st["bt"].b])
                P.do("v", lambda eo: eo.tensor_scalar(out=st["tnA"].h[:, :],
                     in0=st["tauA"].h[:, :], scalar1=-1.0, scalar2=None,
                     op0=ALU.mult), r=[st["tauA"].b], w=[st["tnA"].b])
                cur, prv = "A", "B"
                QV = QTH // 2
                NSUB = max(0, NPROBE - 2)
                for probe in range(NPROBE):
                    sub = probe < NSUB
                    tau_c, c_c = st["tau" + cur], st["c" + cur]
                    tau_p, c_p = st["tau" + prv], st["c" + prv]
                    tn_c, tn_p = st["tn" + cur], st["tn" + prv]
                    cbV, cbS = stc["c" + cur]
                    for ti in range(QV):
                        ap = (s16[ti].h[:, 0:S:4] if sub else s16[ti].h[:, 0:S])
                        nout = S // 4 if sub else S
                        P.do("v", lambda eo, ti=ti, tau_c=tau_c, c_c=c_c, ap=ap,
                             nout=nout: eo.tensor_scalar(out=dump8.h[:, 0:nout],
                                              in0=ap,
                                              scalar1=tau_c.h[:, ti:ti + 1],
                                              scalar2=None, op0=ALU.is_gt,
                                              op1=ALU.add,
                                              accum_out=c_c.h[:, ti:ti + 1]),
                             r=[s16[ti].b, tau_c.b], w=[dump8.b, cbV.b])
                    for ti in range(QV, QTH):
                        ap = (s16[ti].h[:, 0:S:4] if sub else s16[ti].h[:, 0:S])
                        nout = S // 4 if sub else S
                        P.do("s", lambda eo, ti=ti, tn_c=tn_c, c_c=c_c, ap=ap,
                             nout=nout: eo.activation(sq16.h[:, 0:nout], ap,
                                           ACT.Sign, bias=tn_c.h[:, ti:ti + 1],
                                           accum_out=c_c.h[:, ti:ti + 1]),
                             r=[s16[ti].b, tn_c.b], w=[sq16.b, cbS.b])
                    if sub:
                        P.do("v", lambda eo, c_c=c_c: eo.tensor_scalar(
                            out=c_c.h[:, 0:QV], in0=c_c.h[:, 0:QV],
                            scalar1=4.0, scalar2=None, op0=ALU.mult),
                            r=[cbV.b], w=[cbV.b])
                        P.do("v", lambda eo, c_c=c_c: eo.tensor_scalar(
                            out=c_c.h[:, QV:QTH], in0=c_c.h[:, QV:QTH],
                            scalar1=2.0, scalar2=float(S // 2), op0=ALU.mult,
                            op1=ALU.add), r=[cbS.b], w=[cbS.b])
                    else:
                        P.do("v", lambda eo, c_c=c_c: eo.tensor_scalar(
                            out=c_c.h[:, QV:QTH], in0=c_c.h[:, QV:QTH],
                            scalar1=0.5, scalar2=float(S // 2), op0=ALU.mult,
                            op1=ALU.add), r=[cbS.b], w=[cbS.b])
                    P.raw("v", lambda eo: eo.drain())
                    d1, t1, t2, t3 = st["d1"], st["t1"], st["t2"], st["t3"]
                    lo_, hi_, bt_, bd_ = st["lo"], st["hi"], st["bt"], st["bd"]

                    def U(fn, r, w):
                        P.do("v", fn, r=r, w=w)
                    U(lambda eo, c_c=c_c: eo.tensor_scalar(out=d1.h[:, :],
                      in0=c_c.h[:, :], scalar1=-float(KK), scalar2=None,
                      op0=ALU.add), [cbV.b, cbS.b], [d1.b])
                    U(lambda eo: eo.tensor_scalar(out=msk.h[:, :], in0=d1.h[:, :],
                      scalar1=0.0, scalar2=None, op0=ALU.is_ge), [d1.b], [msk.b])
                    U(lambda eo: eo.tensor_scalar(out=mskn.h[:, :], in0=d1.h[:, :],
                      scalar1=0.0, scalar2=None, op0=ALU.is_lt), [d1.b], [mskn.b])
                    U(lambda eo, tau_c=tau_c: eo.copy_predicated(lo_.h[:, :],
                      msk.h[:, :], tau_c.h[:, :]), [msk.b, tau_c.b, lo_.b], [lo_.b])
                    U(lambda eo, tau_c=tau_c: eo.copy_predicated(hi_.h[:, :],
                      mskn.h[:, :], tau_c.h[:, :]), [mskn.b, tau_c.b, hi_.b], [hi_.b])
                    if not sub:
                        U(lambda eo: eo.tensor_scalar(out=t2.h[:, :], in0=d1.h[:, :],
                          scalar1=-1.0, scalar2=None, op0=ALU.mult), [d1.b], [t2.b])
                        U(lambda eo: eo.tensor_tensor(out=t2.h[:, :], in0=t2.h[:, :],
                          in1=d1.h[:, :], op=ALU.max), [t2.b, d1.b], [t2.b])
                        U(lambda eo: eo.tensor_tensor(out=msk.h[:, :], in0=t2.h[:, :],
                          in1=bd_.h[:, :], op=ALU.is_le), [t2.b, bd_.b], [msk.b])
                        U(lambda eo, tau_c=tau_c: eo.copy_predicated(bt_.h[:, :],
                          msk.h[:, :], tau_c.h[:, :]), [msk.b, tau_c.b, bt_.b], [bt_.b])
                        U(lambda eo: eo.copy_predicated(bd_.h[:, :], msk.h[:, :],
                          t2.h[:, :]), [msk.b, t2.b, bd_.b], [bd_.b])
                    if probe < NPROBE - 1:
                        if probe == 0:
                            U(lambda eo: eo.tensor_scalar(out=t2.h[:, :],
                              in0=d1.h[:, :], scalar1=irho0c.h[:, :], scalar2=None,
                              op0=ALU.mult), [d1.b, irho0c.b], [t2.b])
                        else:
                            U(lambda eo, tau_c=tau_c, tau_p=tau_p: eo.tensor_tensor(
                              out=t1.h[:, :], in0=tau_c.h[:, :], in1=tau_p.h[:, :],
                              op=ALU.subtract), [tau_c.b, tau_p.b], [t1.b])
                            U(lambda eo, c_c=c_c, c_p=c_p: eo.tensor_tensor(
                              out=t2.h[:, :], in0=c_p.h[:, :], in1=c_c.h[:, :],
                              op=ALU.subtract),
                              [stc["c" + prv][0].b, stc["c" + prv][1].b, cbV.b, cbS.b],
                              [t2.b])
                            U(lambda eo: eo.reciprocal(t1.h[:, :], t1.h[:, :]),
                              [t1.b], [t1.b])
                            U(lambda eo: eo.tensor_tensor(out=t2.h[:, :],
                              in0=t2.h[:, :], in1=t1.h[:, :], op=ALU.mult),
                              [t2.b, t1.b], [t2.b])
                            U(lambda eo: eo.tensor_scalar(out=t2.h[:, :],
                              in0=t2.h[:, :], scalar1=20.0, scalar2=2.0e6,
                              op0=ALU.max, op1=ALU.min), [t2.b], [t2.b])
                            U(lambda eo: eo.reciprocal(t2.h[:, :], t2.h[:, :]),
                              [t2.b], [t2.b])
                            U(lambda eo: eo.tensor_tensor(out=t2.h[:, :],
                              in0=d1.h[:, :], in1=t2.h[:, :], op=ALU.mult),
                              [d1.b, t2.b], [t2.b])
                        U(lambda eo, tau_c=tau_c, tau_p=tau_p: eo.tensor_tensor(
                          out=tau_p.h[:, :], in0=tau_c.h[:, :], in1=t2.h[:, :],
                          op=ALU.add), [tau_c.b, t2.b], [tau_p.b])
                        U(lambda eo: eo.tensor_tensor(out=t1.h[:, :],
                          in0=hi_.h[:, :], in1=lo_.h[:, :], op=ALU.subtract),
                          [hi_.b, lo_.b], [t1.b])
                        U(lambda eo: eo.tensor_scalar(out=t1.h[:, :],
                          in0=t1.h[:, :], scalar1=0.05, scalar2=None,
                          op0=ALU.mult), [t1.b], [t1.b])
                        U(lambda eo: eo.tensor_tensor(out=t3.h[:, :],
                          in0=lo_.h[:, :], in1=t1.h[:, :], op=ALU.add),
                          [lo_.b, t1.b], [t3.b])
                        U(lambda eo, tau_p=tau_p: eo.tensor_tensor(
                          out=tau_p.h[:, :], in0=tau_p.h[:, :], in1=t3.h[:, :],
                          op=ALU.max), [tau_p.b, t3.b], [tau_p.b])
                        U(lambda eo: eo.tensor_tensor(out=t3.h[:, :],
                          in0=hi_.h[:, :], in1=t1.h[:, :], op=ALU.subtract),
                          [hi_.b, t1.b], [t3.b])
                        U(lambda eo, tau_p=tau_p: eo.tensor_tensor(
                          out=tau_p.h[:, :], in0=tau_p.h[:, :], in1=t3.h[:, :],
                          op=ALU.min), [tau_p.b, t3.b], [tau_p.b])
                        U(lambda eo, tau_p=tau_p, tn_p=tn_p: eo.tensor_scalar(
                          out=tn_p.h[:, :], in0=tau_p.h[:, :], scalar1=-1.0,
                          scalar2=None, op0=ALU.mult), [tau_p.b], [tn_p.b])
                        cur, prv = prv, cur
                P.do("s", lambda eo: eo.activation(th16t.h[:, :], st["bt"].h[:, :],
                     ACT.Exp), r=[st["bt"].b], w=[th16t.b])
                for ti in range(QTH):
                    t = halfb * QTH + ti
                    P.do("y", lambda eo, t=t, ti=ti: eo.dma_start(
                        out=throw.h[0:1, t * 128:(t + 1) * 128],
                        in_=th16t.h[:, ti:ti + 1]), r=[th16t.b], w=[throw.b],
                        dma=True)
                emit_av(pending[0])
                pending[0] = (hh, hb, hd, halfb, qt, kt_)
    emit_av(pending[0])
    pending[0] = None
    P.barrier()

    sp16 = s16
    for c in range(NCH):
        P.do("y", lambda eo, c=c: eo.dma_start(out=sp16[c].h[:, 0:S],
             in_=sp_scr.h[c * 128:(c + 1) * 128, :]), r=[sp_scr.b],
             w=[sp16[c].b], dma=True)

    def load_wog(oc, slot):
        load_cols(Wo, wo16[slot], oc * 128)
        load_cols(Wg, wg16[slot], oc * 128)

    load_wog(0, 0)
    for oc in range(NCH):
        slot = oc % 2
        if oc + 1 < NCH:
            load_wog(oc + 1, 1 - slot)
        for gidx in range(NG):
            qs0 = slice(gidx * 512, (gidx + 1) * 512)
            qsd = dslc(gidx * 512, (gidx + 1) * 512)
            PU = PQ1 if gidx % 2 == 0 else PQ2
            for c in range(NCH):
                P.do("t", lambda eo, c=c, qs0=qs0, PU=PU, slot=slot: eo.matmul(
                    PU.h[:, :], wo16[slot].h[:, c * 128:(c + 1) * 128],
                    sp16[c].h[:, qs0], start=(c == 0), stop=False),
                    r=[wo16[slot].b, sp16[c].b], w=[PU.b])
            P.do("t", lambda eo, PU=PU, oc=oc: eo.matmul(
                PU.h[:, :], borow.h[0:1, oc * 128:(oc + 1) * 128],
                onesP.h[0:1, 0:512], start=False, stop=True),
                r=[borow.b, onesP.b], w=[PU.b])
            for c in range(NCH):
                P.do("t", lambda eo, c=c, qsd=qsd, slot=slot: eo.matmul(
                    PQ3.h[:, :], wg16[slot].h[:, c * 128:(c + 1) * 128],
                    xn16[c].h[:, qsd], start=(c == 0), stop=(c == NCH - 1)),
                    r=[wg16[slot].b, xn16[c].b], w=[PQ3.b])
            P.do("s", lambda eo, oc=oc: eo.activation(
                tmp16b.h[:, :], PQ3.h[:, :], ACT.Sigmoid,
                bias=bias_g.h[:, oc:oc + 1]), r=[PQ3.b, bias_g.b], w=[tmp16b.b])
            P.do("v", lambda eo, oc=oc, qsd=qsd, PU=PU: eo.tensor_tensor(
                out=tmp32a.h[:, :], in0=PU.h[:, :], in1=xn16[oc].h[:, qsd],
                op=ALU.subtract), r=[PU.b, xn16[oc].b], w=[tmp32a.b])
            P.do("v", lambda eo: eo.tensor_tensor(out=tmp32a.h[:, :],
                 in0=tmp32a.h[:, :], in1=tmp16b.h[:, :], op=ALU.mult),
                 r=[tmp32a.b, tmp16b.b], w=[tmp32a.b])
            P.do("v", lambda eo, oc=oc, qsd=qsd: eo.tensor_tensor(
                out=tmp16a.h[:, :], in0=tmp32a.h[:, :], in1=xn16[oc].h[:, qsd],
                op=ALU.add), r=[tmp32a.b, xn16[oc].b], w=[tmp16a.b])
            P.do("y", lambda eo, oc=oc, gidx=gidx: eo.dma_start(
                out=s_scr.h[oc * 128:(oc + 1) * 128, gidx * 512:(gidx + 1) * 512],
                in_=tmp16a.h[:, :]), r=[tmp16a.b], w=[s_scr.b], dma=True)
    P.barrier()

    qc = s16[0:4]
    kc16 = s16[4:8]
    sqf = sq16
    qshf = q16p
    h1_16 = s16
    SHIFTOFF = [SD] * 4 + [SD - 1] * 4

    for l in range(2):
        for srcd, dstt in ((qsc_c, qscol), (ksc_c, kscol), (lng_c, lngc),
                           (lnb_c, lnbc), (b1p_c, b1c)):
            P.do("y", lambda eo, srcd=srcd, dstt=dstt, l=l: eo.dma_start(
                out=dstt.h[:, :], in_=srcd.h[l]), r=[srcd.b], w=[dstt.b], dma=True)
        P.do("g", lambda eo, l=l: eo.dma_start(out=b2row.h[:, :], in_=b2_r.h[l]),
             r=[b2_r.b], w=[b2row.b], dma=True)

        def load_wqkv(oc, slot, l=l):
            r0 = (l * 8 + oc) * 128
            src = wqkv_qk.h[r0:r0 + 128, :]
            P.do("g", lambda eo, src=src, slot=slot: eo.dma_start(
                out=wqkv16[slot].h[:, :], in_=src), r=[wqkv_qk.b],
                w=[wqkv16[slot].b], dma=True)

        load_wqkv(0, 0)
        for oc in range(8):
            slot = oc % 2
            if oc + 1 < 8:
                load_wqkv(oc + 1, 1 - slot)
            isq = oc < 4
            dst16 = qc[oc] if isq else kc16[oc - 4]
            for half in range(HLV):
                PP = PP1 if half == 0 else PP2
                for c in range(NCH):
                    base = SHIFTOFF[c]
                    for j in range(2):
                        P.do("t", lambda eo, c=c, j=j, half=half, slot=slot,
                             base=base, PP=PP: eo.matmul(
                                 PP.h[:, j * 512:(j + 1) * 512],
                                 wqkv16[slot].h[:, c * 128:(c + 1) * 128],
                                 xn16[c].h[:, base + half * 1024 + j * 512:
                                           base + half * 1024 + (j + 1) * 512],
                                 start=(c == 0), stop=(c == NCH - 1)),
                             r=[wqkv16[slot].b, xn16[c].b], w=[PP.b])
                cs = slice(half * 1024, (half + 1) * 1024)
                P.do("s", lambda eo, PP=PP, cs=cs, dst16=dst16: eo.activation(
                    dst16.h[:, cs], PP.h[:, :], ACT.Identity),
                    r=[PP.b], w=[dst16.b])
                P.do("s", lambda eo, PP=PP, cs=cs: eo.activation(
                    sqf.h[:, cs], PP.h[:, :], ACT.Square), r=[PP.b], w=[sqf.b])
            ssq_rows(sqf)
            finish_rrows(False)
            for hh in range(2):
                hb = hh * 64
                for half in range(HLV):
                    PP = PP1 if half == 0 else PP2
                    bcast_row16(PP, hh, half)
                    cs = slice(half * 1024, (half + 1) * 1024)
                    P.do("v", lambda eo, dst16=dst16, hb=hb, cs=cs, PP=PP:
                         eo.tensor_tensor(out=dst16.h[hb:hb + 64, cs],
                                          in0=dst16.h[hb:hb + 64, cs],
                                          in1=PP.h[hb:hb + 64, :], op=ALU.mult),
                         r=[dst16.b, PP.b], w=[dst16.b])

        for tiles, aidx, scol in ((qc, 0, qscol), (kc16, 2, kscol)):
            for qtr in range(NG):
                q0 = qtr * 512
                P.do("y", lambda eo, aidx=aidx, q0=q0: eo.dma_start(
                    out=rot2[0].h[:, :], in_=rotT.h[aidx][:, q0:q0 + 512]),
                    r=[rotT.b], w=[rot2[0].b], dma=True)
                P.do("y", lambda eo, aidx=aidx, q0=q0: eo.dma_start(
                    out=rot2[1].h[:, :], in_=rotT.h[aidx + 1][:, q0:q0 + 512]),
                    r=[rotT.b], w=[rot2[1].b], dma=True)
                for c4 in range(4):
                    tl = tiles[c4]
                    qsl = slice(q0, q0 + 512)
                    P.do("v", lambda eo, tl=tl, qsl=qsl, scol=scol: eo.tensor_scalar(
                        out=sqf.h[:, qsl], in0=tl.h[:, qsl], scalar1=scol.h[:, :],
                        scalar2=None, op0=ALU.mult), r=[tl.b, scol.b], w=[sqf.b])
                    for b0 in (0, 64):
                        P.do("y", lambda eo, b0=b0, qsl=qsl: eo.dma_start(
                            out=qshf.h[b0:b0 + 32, qsl],
                            in_=sqf.h[b0 + 32:b0 + 64, qsl]),
                            r=[sqf.b], w=[qshf.b], dma=True)
                        P.do("y", lambda eo, b0=b0, qsl=qsl: eo.dma_start(
                            out=qshf.h[b0 + 32:b0 + 64, qsl],
                            in_=sqf.h[b0:b0 + 32, qsl]),
                            r=[sqf.b], w=[qshf.b], dma=True)
                    P.do("v", lambda eo, qsl=qsl: eo.tensor_tensor(
                        out=sqf.h[:, qsl], in0=sqf.h[:, qsl], in1=rot2[0].h[:, :],
                        op=ALU.mult), r=[sqf.b, rot2[0].b], w=[sqf.b])
                    P.do("v", lambda eo, qsl=qsl: eo.tensor_tensor(
                        out=qshf.h[:, qsl], in0=qshf.h[:, qsl], in1=rot2[1].h[:, :],
                        op=ALU.mult), r=[qshf.b, rot2[1].b], w=[qshf.b])
                    P.do("v", lambda eo, tl=tl, qsl=qsl: eo.tensor_tensor(
                        out=tl.h[:, qsl], in0=sqf.h[:, qsl], in1=qshf.h[:, qsl],
                        op=ALU.add), r=[sqf.b, qshf.b], w=[tl.b])
        otile = qc
        for h in range(HF):
            hslot = h % 2
            r0 = (l * HF + h) * 128
            src = wqkv_v.h[r0:r0 + 128, :]
            P.do("g", lambda eo, src=src, hslot=hslot: eo.dma_start(
                out=fwo16[hslot].h[:, :], in_=src), r=[wqkv_v.b],
                w=[fwo16[hslot].b], dma=True)
            for t in range(NQT):
                PQ = PQ1 if t % 2 == 0 else PQ2
                for c in range(NCH):
                    base = SHIFTOFF[c]
                    P.do("t", lambda eo, t=t, c=c, PQ=PQ, base=base, hslot=hslot:
                         eo.matmul(PQ.h[:, 0:64],
                                   xn16[c].h[:, base + t * 128:base + (t + 1) * 128],
                                   fwo16[hslot].h[:, c * 64:(c + 1) * 64],
                                   start=(c == 0), stop=(c == NCH - 1)),
                         r=[xn16[c].b, fwo16[hslot].b], w=[PQ.b])
                eng = "v" if t % 2 == 0 else "s"
                if eng == "v":
                    P.do("v", lambda eo, hslot=hslot, t=t, PQ=PQ: eo.tensor_copy(
                        vhat[hslot][t].h[:, 0:DH], PQ.h[:, 0:64]),
                        r=[PQ.b], w=[vhat[hslot][t].b])
                else:
                    P.do("s", lambda eo, hslot=hslot, t=t, PQ=PQ: eo.activation(
                        vhat[hslot][t].h[:, 0:DH], PQ.h[:, 0:64],
                        ACT.Identity), r=[PQ.b], w=[vhat[hslot][t].b])
            cq = h // 2
            b0 = (h % 2) * 64
            for w in range(NW):
                kts = list(range(max(0, 4 * (w - 1)), 4 * (w + 1)))
                first, last = kts[0], kts[-1]
                def simk(i, kt, w=w, b0=b0, cq=cq):
                    PQ = PQ1 if i % 2 == 0 else PQ2
                    P.do("t", lambda eo, kt=kt, w=w, PQ=PQ, b0=b0, cq=cq: eo.matmul(
                        PQ.h[:, :], kc16[cq].h[b0:b0 + 64, kt * 128:(kt + 1) * 128],
                        qc[cq].h[b0:b0 + 64, w * 512:(w + 1) * 512],
                        start=True, stop=True), r=[kc16[cq].b, qc[cq].b], w=[PQ.b])
                simk(0, kts[0])
                for i, kt in enumerate(kts):
                    PQ = PQ1 if i % 2 == 0 else PQ2
                    yt = yT[i % 3]
                    off = (kt - 4 * w) * 128
                    if i + 1 < len(kts):
                        simk(i + 1, kts[i + 1])
                    iv = (off + 512) // 128
                    P.do("v", lambda eo, PQ=PQ, iv=iv: eo.tensor_tensor(
                        out=tmp32a.h[:, :], in0=PQ.h[:, :],
                        in1=madd.h[:, iv * 512:(iv + 1) * 512], op=ALU.add),
                        r=[PQ.b, madd.b], w=[tmp32a.b])
                    P.do("s", lambda eo, yt=yt: eo.activation(
                        yt.h[:, :], tmp32a.h[:, :], ACT.Exp, scale=8.0),
                        r=[tmp32a.b], w=[yt.b])
                    P.do("t", lambda eo, kt=kt, h=h, first=first, last=last,
                         hslot=hslot, yt=yt: eo.matmul(
                             PQ3.h[0:DH + 1, :], vhat[hslot][kt].h[:, :], yt.h[:, :],
                             start=(kt == first), stop=(kt == last)),
                         r=[vhat[hslot][kt].b, yt.b], w=[PQ3.b])

                def store(h=h, w=w):
                    P.do("y", lambda eo: eo.dma_start(
                        out=otile[h // 2].h[(h % 2) * 64:(h % 2) * 64 + 64,
                                            w * 512:(w + 1) * 512],
                        in_=tmp16a.h[0:64, :]), r=[tmp16a.b],
                        w=[otile[h // 2].b], dma=True)
                av_divide(store)
        P.barrier()
        def load_fwo(oc, slot, l=l):
            r0 = (l * 8 + oc) * 128
            src = fwo.h[r0:r0 + 128, :]
            P.do("g", lambda eo, src=src, slot=slot: eo.dma_start(
                out=fwo16[slot].h[:, :], in_=src), r=[fwo.b],
                w=[fwo16[slot].b], dma=True)

        load_fwo(0, 0)
        for oc in range(NCH):
            slot = oc % 2
            if oc + 1 < NCH:
                load_fwo(oc + 1, 1 - slot)
            for gidx in range(NG):
                qsd = dslc(gidx * 512, (gidx + 1) * 512)
                PU = PQ1 if gidx % 2 == 0 else PQ2
                for c in range(4):
                    P.do("t", lambda eo, c=c, gidx=gidx, PU=PU, slot=slot:
                         eo.matmul(PU.h[:, :],
                                   fwo16[slot].h[:, c * 128:(c + 1) * 128],
                                   otile[c].h[:, gidx * 512:(gidx + 1) * 512],
                                   start=(c == 0), stop=(c == 3)),
                         r=[fwo16[slot].b, otile[c].b], w=[PU.b])
                P.do("v", lambda eo, oc=oc, qsd=qsd, PU=PU: eo.tensor_tensor(
                    out=xn16[oc].h[:, qsd], in0=PU.h[:, :], in1=xn16[oc].h[:, qsd],
                    op=ALU.add), r=[PU.b, xn16[oc].b], w=[xn16[oc].b])
        P.barrier()
        ln_stats([(xn16[c], slice(SHIFTOFF[c], SHIFTOFF[c] + S))
                  for c in range(NCH)])
        ln_rows()
        for half in range(HLV):
            bcast_to(PP1, rstdrow16, half)
            bcast_to(PP2, murow16, half)
            for c in range(NCH):
                base = SHIFTOFF[c]
                for j in range(2):
                    csx = slice(base + half * 1024 + j * 512,
                                base + half * 1024 + (j + 1) * 512)
                    cs = slice(half * 1024 + j * 512, half * 1024 + (j + 1) * 512)
                    ps = slice(j * 512, (j + 1) * 512)
                    P.do("v", lambda eo, c=c, csx=csx, ps=ps: eo.tensor_tensor(
                        out=wtmp2.h[:, :], in0=xn16[c].h[:, csx], in1=PP1.h[:, ps],
                        op=ALU.mult), r=[xn16[c].b, PP1.b], w=[wtmp2.b])
                    P.do("v", lambda eo, ps=ps: eo.tensor_tensor(
                        out=wtmp2.h[:, :], in0=wtmp2.h[:, :], in1=PP2.h[:, ps],
                        op=ALU.subtract), r=[wtmp2.b, PP2.b], w=[wtmp2.b])
                    P.do("v", lambda eo, c=c, cs=cs: eo.tensor_scalar(
                        out=h1_16[c].h[:, cs], in0=wtmp2.h[:, :],
                        scalar1=lngc.h[:, c:c + 1], scalar2=lnbc.h[:, c:c + 1],
                        op0=ALU.mult, op1=ALU.add),
                        r=[wtmp2.b, lngc.b, lnbc.b], w=[h1_16[c].b])
        P.barrier()
        w2half = []
        for k in range(NK2):
            wt = wbig[k // 2]
            ap = wt.h[:, (k % 2) * 512:(k % 2) * 512 + 512]
            w2half.append((wt, ap))

        w1a2 = [w1ab, wbig[9]]
        w1g2 = [w1gb, wbig[10]]

        def load_w1(m, gate, buf, l=l):
            mm_ = m + NK2 if gate else m
            dst = (w1g2 if gate else w1a2)[buf]
            r0 = (l * NM1 + mm_) * 128
            src = w1p.h[r0:r0 + 128, :]
            P.do("g", lambda eo, src=src, dst=dst: eo.dma_start(
                out=dst.h[:, :], in_=src), r=[w1p.b], w=[dst.b], dma=True)

        w1order = [(gh_, m_) for gh_ in range(NG // 2) for m_ in range(NK2)]
        slots6 = [(PQ1.h[:, :], PQ1.b), (PQ2.h[:, :], PQ2.b),
                  (PP1.h[:, 0:512], PP1s[0]), (PP1.h[:, 512:1024], PP1s[1]),
                  (PP2.h[:, 0:512], PP2s[0]), (PP2.h[:, 512:1024], PP2s[1])]
        w1rot = [0]
        load_w1(w1order[0][1], False, 0)
        load_w1(w1order[0][1], True, 0)
        for ghalf in range(NG // 2):
            for m in range(NK2):
                i1 = ghalf * NK2 + m
                if i1 + 1 < len(w1order):
                    load_w1(w1order[i1 + 1][1], False, (i1 + 1) % 2)
                    load_w1(w1order[i1 + 1][1], True, (i1 + 1) % 2)
                wa, wg = w1a2[i1 % 2], w1g2[i1 % 2]
                for gi in range(2):
                    gidx = ghalf * 2 + gi
                    qs = slice(gidx * 512, (gidx + 1) * 512)
                    pa, pba = slots6[w1rot[0] % 6]
                    pg, pbg = slots6[(w1rot[0] + 1) % 6]
                    w1rot[0] += 2
                    for c in range(NCH):
                        P.do("t", lambda eo, c=c, qs=qs, pa=pa, wa=wa: eo.matmul(
                            pa, wa.h[:, c * 128:(c + 1) * 128],
                            h1_16[c].h[:, qs], start=(c == 0), stop=(c == NCH - 1)),
                            r=[wa.b, h1_16[c].b], w=[pba])
                    for c in range(NCH):
                        P.do("t", lambda eo, c=c, qs=qs, pg=pg, wg=wg: eo.matmul(
                            pg, wg.h[:, c * 128:(c + 1) * 128],
                            h1_16[c].h[:, qs], start=(c == 0), stop=(c == NCH - 1)),
                            r=[wg.b, h1_16[c].b], w=[pbg])
                    P.do("s", lambda eo, m=m, pa=pa: eo.activation(
                        tmp16b.h[:, :], pa, ACT.Identity,
                        bias=b1c.h[:, m:m + 1]), r=[pba, b1c.b], w=[tmp16b.b])
                    P.do("s", lambda eo, m=m, pg=pg: eo.activation(
                        gel16.h[:, :], pg, ACT.Gelu,
                        bias=b1c.h[:, NK2 + m:NK2 + m + 1]),
                        r=[pbg, b1c.b], w=[gel16.b])
                    P.do("v", lambda eo, m=m, gi=gi: eo.tensor_tensor(
                        out=h2t[m].h[:, gi * 512:(gi + 1) * 512],
                        in0=tmp16b.h[:, :], in1=gel16.h[:, :], op=ALU.mult),
                        r=[tmp16b.b, gel16.b], w=[h2t[m].b])
            for ochalf in range(2):
                for k in range(NK2):
                    wt, wap = w2half[k]
                    stg = tmp32a if k % 2 == 0 else wtmp2
                    src = w2p.h[l][k * 128:(k + 1) * 128,
                                   ochalf * 512:(ochalf + 1) * 512]
                    P.do("y", lambda eo, src=src, stg=stg: eo.dma_start(
                        out=stg.h[:, :], in_=src), r=[w2p.b], w=[stg.b],
                        dma=True)
                    P.do("v", lambda eo, wap=wap, stg=stg: eo.tensor_copy(
                        wap, stg.h[:, :]), r=[stg.b], w=[wt.b])
                for gi in range(2):
                    gidx = ghalf * 2 + gi
                    qsd = dslc(gidx * 512, (gidx + 1) * 512)
                    pslices = [(PP1.h[:, 0:512], PP1s[0]),
                               (PP1.h[:, 512:1024], PP1s[1]),
                               (PP2.h[:, 0:512], PP2s[0]),
                               (PP2.h[:, 512:1024], PP2s[1])]
                    for k in range(NK2):
                        wt, wap = w2half[k]
                        for j4 in range(4):
                            pap, pb = pslices[j4]
                            P.do("t", lambda eo, k=k, j4=j4, wap=wap, pap=pap,
                                 gi=gi: eo.matmul(
                                     pap, wap[:, j4 * 128:(j4 + 1) * 128],
                                     h2t[k].h[:, gi * 512:(gi + 1) * 512],
                                     start=(k == 0), stop=False),
                                 r=[wt.b, h2t[k].b], w=[pb])
                    for j4 in range(4):
                        oc = ochalf * 4 + j4
                        pap, pb = pslices[j4]
                        P.do("t", lambda eo, oc=oc, pap=pap: eo.matmul(
                            pap, b2row.h[0:1, oc * 128:(oc + 1) * 128],
                            onesP.h[64:65, 0:512], start=False, stop=True),
                            r=[b2row.b, onesP.b], w=[pb])
                    for j4 in range(4):
                        oc = ochalf * 4 + j4
                        pap, pb = pslices[j4]
                        P.do("v", lambda eo, oc=oc, qsd=qsd, pap=pap:
                             eo.tensor_tensor(out=xn16[oc].h[:, qsd], in0=pap,
                                              in1=xn16[oc].h[:, qsd], op=ALU.add),
                             r=[pb, xn16[oc].b], w=[xn16[oc].b])
        P.barrier()

    fin_x = [(wtmp2.h[:, :], wtmp2.b), (RW[0].h[:, 0:512], RW[0].b)]
    fin_s = [(tmp16a.h[:, :], tmp16a.b), (h2t[0].h[:, 0:512], h2t[0].b)]
    fin_u = [(tmp16b.h[:, :], tmp16b.b), (gel16.h[:, :], gel16.b)]
    fin_m = [(tmp32a.h[:, :], tmp32a.b), (RW[1].h[:, 0:512], RW[1].b)]
    fin_o = [(wtmp2.h[:, :], wtmp2.b), (RW[0].h[:, 512:1024], RW[0].b)]
    it = 0
    for c in range(NCH):
        for gidx in range(NG):
            pi = it % 2
            it += 1
            xa, xb = fin_x[pi]
            sa, sb_ = fin_s[pi]
            ua, ub = fin_u[pi]
            ma, mb = fin_m[pi]
            oa, ob = fin_o[pi]
            qsd = dslc(gidx * 512, (gidx + 1) * 512)
            qs0 = slice(gidx * 512, (gidx + 1) * 512)
            P.do("y", lambda eo, c=c, qs0=qs0, xa=xa: eo.dma_start(
                out=xa, in_=xT.h[c * 128:(c + 1) * 128, qs0]),
                r=[xT.b], w=[xb], dma=True)
            P.do("y", lambda eo, c=c, qs0=qs0, sa=sa: eo.dma_start(
                out=sa, in_=s_scr.h[c * 128:(c + 1) * 128, qs0]),
                r=[s_scr.b], w=[sb_], dma=True)
            P.do("s", lambda eo, c=c, qsd=qsd, ua=ua: eo.activation(
                ua, xn16[c].h[:, qsd], ACT.Silu),
                r=[xn16[c].b], w=[ub])
            P.do("v", lambda eo, ua=ua, sa=sa, ma=ma: eo.tensor_tensor(
                out=ma, in0=ua, in1=sa, op=ALU.mult),
                r=[ub, sb_], w=[mb])
            P.do("v", lambda eo, xa=xa, ma=ma, oa=oa: eo.tensor_tensor(
                out=oa, in0=xa, in1=ma, op=ALU.add),
                r=[xb, mb], w=[ob])
            P.do("y", lambda eo, c=c, qs0=qs0, oa=oa: eo.dma_start(
                out=out_T.h[c * 128:(c + 1) * 128, qs0], in_=oa),
                r=[ob], w=[out_T.b], dma=True)
    P.barrier()

    with nc.Block() as block:
        @block.tensor
        def _(te):
            for clo in P.q["t"]:
                clo()

        @block.vector
        def _(ve):
            for clo in P.q["v"]:
                clo()

        @block.scalar
        def _(se):
            for clo in P.q["s"]:
                clo()

        @block.sync
        def _(ye):
            for clo in P.q["y"]:
                clo()

        @block.gpsimd
        def _(ge):
            for clo in P.q["g"]:
                clo()
    es.close()
    CFG['counts'] = dict(P.cnt)
    return nc


def _host_constants(S):
    DH_ = 64
    WINDOW = 512
    half_idx = np.arange(0, DH_, 2, dtype=np.float32)
    inv_freq = 1.0 / 10000.0 ** (half_idx / DH_)
    t = np.arange(S, dtype=np.float32)
    freqs = t[:, None] * inv_freq[None, :]
    freqs = np.concatenate([freqs, freqs], axis=-1)
    base = (half_idx + 0.4 * DH_) / (1.4 * DH_)
    power = (t - S // 2) / float(WINDOW // 2)
    scale = base[None, :] ** power[:, None]
    scale = np.concatenate([scale, scale], axis=-1)
    cos, sin = np.cos(freqs), np.sin(freqs)
    sgn = np.concatenate([-np.ones(32, np.float32), np.ones(32, np.float32)])
    AQ = (cos * scale).T
    BQ = (sin * scale).T * sgn[:, None]
    AK = (cos / scale).T
    BK = (sin / scale).T * sgn[:, None]
    rotT = np.stack([np.tile(a, (2, 1)) for a in (AQ, BQ, AK, BK)]).astype(np.float16)
    import ml_dtypes as _mld
    E2 = np.zeros((128, 33), _mld.float8_e4m3fn)
    E2[0:64, 0] = 1.0
    E2[64:128, 32] = 1.0
    import ml_dtypes
    kk = np.arange(128)[:, None]
    qq = np.arange(512)[None, :]
    madd = np.zeros((128, 8 * 512), np.float32)
    for iv in range(8):
        off = iv * 128 - 512
        valid = (qq >= kk + off) & (qq <= kk + off + 512)
        madd[:, iv * 512:(iv + 1) * 512] = np.where(valid, 0.0, -28000.0)
    madd8 = madd.astype(ml_dtypes.float8_e5m2)
    return dict(rotT=rotT, E2_d=E2, madd_d=madd8)


def kernel(**inputs):
    S = CFG["S"]
    D = 1024
    HIDP = 2816
    HID = 2730
    f32 = np.float32

    x = np.ascontiguousarray(np.asarray(inputs["L0_attn_x"], f32))
    B = x.shape[0]

    def g(k):
        return np.ascontiguousarray(np.asarray(inputs[k], f32))

    w1 = g("L0_ffn_w1")
    b1 = g("L0_ffn_b1")
    w2 = g("L0_ffn_w2")
    w1p = np.zeros((2, D, 2 * HIDP), f32)
    w1p[:, :, 0:HID] = w1[:, :, 0:HID]
    w1p[:, :, HIDP:HIDP + HID] = w1[:, :, HID:2 * HID]
    b1p = np.zeros((2, 2 * HIDP), f32)
    b1p[:, 0:HID] = b1[:, 0:HID]
    b1p[:, HIDP:HIDP + HID] = b1[:, HID:2 * HID]
    w2p = np.zeros((2, HIDP, D), f32)
    w2p[:, 0:HID, :] = w2

    def cols(a):
        return np.ascontiguousarray(a.reshape(-1, 128).T)

    def ptiles(a, tw):
        K, M = a.shape
        nch, nt = K // 128, M // tw
        return np.ascontiguousarray(a.reshape(nch, 128, nt, tw)
                                    .transpose(2, 1, 0, 3)
                                    .reshape(nt, 128, nch * tw))

    shared = dict(
        Wq=ptiles(g("L0_attn_Wq"), 128).reshape(-1, 1024),
        Wk=ptiles(g("L0_attn_Wk"), 128).reshape(-1, 1024),
        Wv=ptiles(g("L0_attn_Wv"), 128).reshape(-1, 1024),
        Wo=ptiles(g("L0_attn_Wo"), 128).reshape(-1, 1024),
        Wg=ptiles(g("L0_attn_Wg"), 128).reshape(-1, 1024),
        bqk_c=np.stack([cols(g("L0_attn_bq")), cols(g("L0_attn_bk"))]),
        bg_c=cols(g("L0_attn_bg")),
        bo_r=np.ascontiguousarray(g("L0_attn_bo").reshape(1, D)),
        bv_r=np.ascontiguousarray(g("L0_attn_bv").reshape(1, D)),
        Wt_r=cols(g("L0_attn_Wt").reshape(-1)),
        bt_s=np.ascontiguousarray(g("L0_attn_bt").reshape(1, 1)),
        wqkv_qk=np.concatenate([ptiles(wq_[:, 0:1024], 128)
                                for wq_ in g("L0_ffn_wqkv")]).reshape(-1, 1024),
        wqkv_v=np.concatenate([ptiles(wq_[:, 1024:1536], 64)
                               for wq_ in g("L0_ffn_wqkv")]).reshape(-1, 512),
        fwo=np.concatenate([ptiles(f_, 128)
                            for f_ in g("L0_ffn_wo")]).reshape(-1, 512),
        w1p=np.concatenate([ptiles(w1p[i], 128)
                            for i in range(2)]).reshape(-1, 1024),
        b1p_c=np.stack([cols(b1p[i]) for i in range(2)]),
        w2p=w2p,
        b2_r=np.ascontiguousarray(g("L0_ffn_b2").reshape(2, 1, D)),
        lng_c=np.stack([cols(g("L0_ffn_ln_g")[i]) for i in range(2)]),
        lnb_c=np.stack([cols(g("L0_ffn_ln_b")[i]) for i in range(2)]),
        qsc_c=np.ascontiguousarray(
            np.tile(g("L0_ffn_q_scale"), (1, 2)).reshape(2, 128, 1)),
        ksc_c=np.ascontiguousarray(
            np.tile(g("L0_ffn_k_scale"), (1, 2)).reshape(2, 128, 1)),
        **_host_constants(S),
    )
    nc = build()
    in_maps = []
    for b in range(B):
        m = dict(shared)
        m["xT"] = np.ascontiguousarray(x[b].T)
        in_maps.append(m)
    res = run_bass_kernel_spmd(nc, in_maps, core_ids=list(range(B)))
    out = np.stack([np.asarray(r["out_T"]).T for r in res.results])
    return np.ascontiguousarray(out.astype(np.float32))


# version 22
# speedup vs baseline: 1.0275x; 1.0006x over previous
import numpy as np
from contextlib import ExitStack

import concourse.bass as bass
import concourse.mybir as mybir
from concourse.bass_utils import run_bass_kernel_spmd

F32 = mybir.dt.float32
F16 = mybir.dt.float16
ALU = mybir.AluOpType
ACT = mybir.ActivationFunctionType
AX = mybir.AxisListType

CFG = dict(S=2048, NPROBE=4)


def _config(S=2048, NPROBE=4):
    CFG["S"] = S
    CFG["NPROBE"] = NPROBE


class Buf:
    __slots__ = ("w", "r")

    def __init__(self):
        self.w = None
        self.r = {}


class Tile:
    __slots__ = ("h", "b")

    def __init__(self, h):
        self.h = h
        self.b = Buf()


class Prog:
    ENGS = ("t", "v", "s", "y", "g")
    KDMA = 4

    def __init__(self, nc, es):
        self.nc = nc
        self.semids = ["t", "v", "s"] + [f"y{i}" for i in range(self.KDMA)] \
            + [f"g{i}" for i in range(self.KDMA)]
        self.sem = {sid: es.enter_context(nc.semaphore("sem_" + sid))
                    for sid in self.semids}
        self.cnt = {sid: 0 for sid in self.semids}
        self.dman = {"y": 0, "g": 0}
        self.seen = {e: {sid: 0 for sid in self.semids} for e in self.ENGS}
        self.q = {e: [] for e in self.ENGS}
        self.engobj = dict(t=nc.tensor, v=nc.vector, s=nc.scalar, y=nc.sync,
                           g=nc.gpsimd)

    def do(self, e, fn, r=(), w=(), dma=False):
        need = {}

        def add(src):
            if src is None:
                return
            f, c = src
            if e == "t" and f == "t":
                return
            if c > need.get(f, 0):
                need[f] = c

        for b in r:
            add(b.w)
        for b in w:
            add(b.w)
            for f, c in b.r.items():
                add((f, c))
        if dma:
            n = self.dman[e]
            sid = f"{e}{n % self.KDMA}"
            self.dman[e] = n + 1
            if 16 * (n // self.KDMA) > need.get(sid, 0):
                need[sid] = 16 * (n // self.KDMA)
            inc = 16
        else:
            sid = e
            inc = 1
        waits = []
        for f, c in need.items():
            if self.seen[e][f] < c:
                self.seen[e][f] = c
                waits.append((f, c))
        self.cnt[sid] += inc
        mycnt = self.cnt[sid]

        def clo(eo=self.engobj[e], waits=tuple(waits), fn=fn, sem=self.sem[sid],
                inc=inc, sems=self.sem):
            for f, c in waits:
                eo.wait_ge(sems[f], c)
            fn(eo).then_inc(sem, inc)

        self.q[e].append(clo)
        src = (sid, mycnt)
        for b in r:
            if b.r.get(sid, 0) < mycnt:
                b.r[sid] = mycnt
        for b in w:
            b.w = src
            b.r = {}
        return src

    def raw(self, e, fn):
        self.q[e].append(lambda eo=self.engobj[e], fn=fn: fn(eo))

    def barrier(self):
        snap = dict(self.cnt)
        sems = self.sem
        for e in self.ENGS:
            waits = []
            for sid in self.semids:
                if sid != e and self.seen[e][sid] < snap[sid]:
                    self.seen[e][sid] = snap[sid]
                    waits.append((sid, snap[sid]))

            def clo(eo=self.engobj[e], waits=tuple(waits)):
                for f, c in waits:
                    eo.wait_ge(sems[f], c)

            self.q[e].append(clo)


def build():
    S = CFG["S"]
    NPROBE = CFG["NPROBE"]
    D = 1024
    NCH = 8
    H = 16
    DH = 64
    KK = S // 4
    NQT = S // 128
    NG = S // 512
    QTH = NQT // 2
    HF = 8
    INNER = 512
    NW = S // 512
    HIDP = 2816
    NM1 = 2 * HIDP // 128
    NK2 = HIDP // 128
    SD = 2
    HLV = max(1, S // 1024)

    nc = bass.Bass("TRN2", target_bir_lowering=False)
    es = ExitStack()
    P = Prog(nc, es)

    def dram(name, shape, dt, kind):
        return Tile(nc.dram_tensor(name, shape, dt, kind=kind).ap())

    def sb(name, shape, dt):
        return Tile(es.enter_context(nc.sbuf_tensor(name, shape, dt)))

    def psum(name, shape, dt=F32):
        return Tile(es.enter_context(nc.psum_tensor(name, shape, dt)))

    xT = dram("xT", [D, S], F32, "ExternalInput")
    Wq = dram("Wq", [NCH * 128, D], F32, "ExternalInput")
    Wk = dram("Wk", [NCH * 128, D], F32, "ExternalInput")
    Wv = dram("Wv", [NCH * 128, D], F32, "ExternalInput")
    Wo = dram("Wo", [NCH * 128, D], F32, "ExternalInput")
    Wg = dram("Wg", [NCH * 128, D], F32, "ExternalInput")
    bqk_c = dram("bqk_c", [2, 128, NCH], F32, "ExternalInput")
    bg_c = dram("bg_c", [128, NCH], F32, "ExternalInput")
    bo_r = dram("bo_r", [1, D], F32, "ExternalInput")
    bv_r = dram("bv_r", [1, D], F32, "ExternalInput")
    Wt_r = dram("Wt_r", [128, NCH], F32, "ExternalInput")
    bt_s = dram("bt_s", [1, 1], F32, "ExternalInput")
    wqkv_qk = dram("wqkv_qk", [2 * 8 * 128, 1024], F32, "ExternalInput")
    wqkv_v = dram("wqkv_v", [2 * HF * 128, 512], F32, "ExternalInput")
    fwo = dram("fwo", [2 * 8 * 128, 512], F32, "ExternalInput")
    w1p = dram("w1p", [2 * NM1 * 128, 1024], F32, "ExternalInput")
    b1p_c = dram("b1p_c", [2, 128, NM1], F32, "ExternalInput")
    w2p = dram("w2p", [2, HIDP, D], F32, "ExternalInput")
    b2_r = dram("b2_r", [2, 1, D], F32, "ExternalInput")
    lng_c = dram("lng_c", [2, 128, NCH], F32, "ExternalInput")
    lnb_c = dram("lnb_c", [2, 128, NCH], F32, "ExternalInput")
    qsc_c = dram("qsc_c", [2, 128, 1], F32, "ExternalInput")
    ksc_c = dram("ksc_c", [2, 128, 1], F32, "ExternalInput")
    rotT = dram("rotT", [4, 128, S], F16, "ExternalInput")
    E2_d = dram("E2_d", [128, 33], mybir.dt.float8e4, "ExternalInput")
    madd_d = dram("madd_d", [128, 8 * 512], mybir.dt.float8e5, "ExternalInput")

    out_T = dram("out_T", [D, S], F32, "ExternalOutput")
    s_scr = dram("s_scr", [D, S], F16, "Internal")
    sp_scr = dram("sp_scr", [D, S], F16, "Internal")

    F8 = mybir.dt.float8e4

    def view(pack, p0, p1):
        t = Tile.__new__(Tile)
        t.h = pack.h[p0:p1, :]
        t.b = pack.b
        return t

    def cview(pack, c0, w):
        t = Tile.__new__(Tile)
        t.h = pack.h[:, c0:c0 + w]
        t.b = Buf()
        return t

    xn16 = [sb(f"xn16_{c}", [128, S + SD], F16) for c in range(NCH)]
    P8 = [sb(f"P8_{i}", [128, S + SD], F16) for i in range(8)]
    q16p = sb("q16p", [128, S + SD], F16)
    k16p = sb("k16p", [128, S + SD], F16)
    q16pB = sb("q16pB", [128, S + SD], F16)
    k16pB = sb("k16pB", [128, S + SD], F16)
    sq16 = sb("sq16", [128, S + SD], F16)
    dump8 = sb("dump8", [128, S], F8)
    s16 = P8
    h2t = [sb(f"h2t_{k}", [128, 1024], F16) for k in range(NK2)]
    wbig = [sb(f"wbig_{i}", [128, D], F16) for i in range(11)]
    wq16 = wbig[0:2]
    wk16 = wbig[2:4]
    wv16 = wbig[4:6]
    wo16 = wbig[6:8]
    wg16 = wbig[8:10]
    wqkv16 = [wbig[10], wbig[0]]
    w1ab = sb("w1ab", [128, D], F16)
    w1gb = sb("w1gb", [128, D], F16)
    fwo16 = [sb(f"fwo16_{i}", [128, INNER], F16) for i in range(2)]
    rot2 = [sb(f"rot2_{i}", [128, 512], F16) for i in range(2)]
    yT = [sb(f"yT_{k}", [128, 512], F16) for k in range(3)]
    vhat = [[sb(f"vh_{hh}_{t}", [128, DH + 1], F16) for t in range(NQT)]
            for hh in range(2)]
    E2 = sb("E2", [128, 33], mybir.dt.float8e4)
    madd = sb("madd", [128, 8 * 512], mybir.dt.float8e5)
    onesP = sb("onesP", [128, 512], F16)
    ones_c = sb("ones_c", [128, 1], F16)
    epsc = sb("epsc", [128, 1], F32)
    clampc = sb("clampc", [128, 1], F32)
    ln8c = sb("ln8c", [128, 1], F32)
    bias_q = sb("bias_q", [128, NCH], F32)
    bias_k = sb("bias_k", [128, NCH], F32)
    bias_g = sb("bias_g", [128, NCH], F32)
    b1c = sb("b1c", [128, NM1], F32)
    lngc = sb("lngc", [128, NCH], F32)
    lnbc = sb("lnbc", [128, NCH], F32)
    qscol = sb("qscol", [128, 1], F32)
    kscol = sb("kscol", [128, 1], F32)
    RW = [sb(f"RW_{i}", [128, S], F32) for i in range(2)]
    rrowA = view(RW[0], 0, 33)
    srow = view(RW[0], 0, 1)
    s2row = view(RW[1], 0, 1)
    murow = srow
    rstdrow = s2row
    R16 = [sb(f"R16_{i}", [128, S], F16) for i in range(2)]
    rrow16A = view(R16[0], 0, 33)
    rrow16 = [view(R16[0], 0, 1), view(R16[0], 32, 33)]
    murow16 = view(R16[0], 0, 1)
    rstdrow16 = view(R16[1], 0, 1)
    throw = view(R16[1], 64, 65)

    brows = sb("brows", [128, D], F16)
    borow = view(brows, 0, 1)
    bvrow = view(brows, 32, 33)
    b2row = view(brows, 64, 65)
    thb = sb("thb", [128, 512], F16)
    tcol16 = cview(thb, 0, 1)
    rd16 = view(thb, 64, 65)
    th16t = cview(thb, 8, QTH)
    wtmp2 = sb("wtmp2", [128, 512], F32)
    tmp32a = sb("tmp32a", [128, 512], F32)
    rd32 = view(tmp32a, 64, 65)
    tmp16a = sb("tmp16a", [128, 512], F16)
    tmp16b = sb("tmp16b", [128, 512], F16)
    gel16 = thb
    rdb16 = view(tmp16b, 64, 128)
    itemp = sb("itemp", [1, 1], F32)
    lnit = sb("lnit", [128, 1], F32)
    scal1 = cview(wtmp2, 0, 1)
    scal16 = cview(thb, 16, 1)
    tau0c = sb("tau0c", [128, 1], F32)
    irho0c = sb("irho0c", [128, 1], F32)
    _stnames = ("tauA", "tauB", "cA", "cB", "lo", "hi", "bt", "bd", "d1",
                "t1", "t2", "t3", "tnA", "tnB")
    _stpack = sb("stpack", [128, 14 * QTH], F32)

    st = {n: cview(_stpack, i * QTH, QTH) for i, n in enumerate(_stnames)}
    stc = {n: (cview(_stpack, _stnames.index(n) * QTH, QTH // 2),
               cview(_stpack, _stnames.index(n) * QTH + QTH // 2, QTH - QTH // 2))
           for n in ("cA", "cB")}
    _mpack = sb("mpack", [128, 2 * QTH], mybir.dt.uint8)
    msk = cview(_mpack, 0, QTH)
    mskn = cview(_mpack, QTH, QTH)
    meanT = cview(_stpack, 0, NCH)
    wtT = cview(_stpack, NCH, NCH)
    tcol = cview(_stpack, 2 * NCH, 1)

    PP1 = psum("PP1", [128, 1024])
    PP2 = psum("PP2", [128, 1024])
    PQ1 = psum("PQ1", [128, 512])
    PQ2 = psum("PQ2", [128, 512])
    PQ3 = psum("PQ3", [128, 512])
    PP1s = [Buf(), Buf()]
    PP2s = [Buf(), Buf()]

    DATA = slice(SD, SD + S)
    SH = slice(SD - 1, SD - 1 + S)

    def dslc(a, b2):
        return slice(SD + a, SD + b2)

    x16 = s16
    for c in range(NCH):
        P.do("g", lambda eo, c=c: eo.dma_start(
            out=x16[c].h[:, DATA], in_=xT.h[c * 128:(c + 1) * 128, :]),
            r=[xT.b], w=[x16[c].b], dma=True)
    for src_d, dst in ((E2_d, E2), (madd_d, madd), (bg_c, bias_g), (Wt_r, wtT)):
        P.do("y", lambda eo, src_d=src_d, dst=dst: eo.dma_start(
            out=dst.h[:, :], in_=src_d.h), r=[src_d.b], w=[dst.b], dma=True)
    for i, dst in ((0, bias_q), (1, bias_k)):
        P.do("y", lambda eo, i=i, dst=dst: eo.dma_start(
            out=dst.h[:, :], in_=bqk_c.h[i]), r=[bqk_c.b], w=[dst.b], dma=True)
    for src_d, dst in ((bo_r, borow), (bv_r, bvrow)):
        P.do("g", lambda eo, src_d=src_d, dst=dst: eo.dma_start(
            out=dst.h[:, :], in_=src_d.h), r=[src_d.b], w=[dst.b], dma=True)
    P.do("v", lambda eo: eo.memset(ones_c.h[:, :], 1.0), w=[ones_c.b])
    P.do("v", lambda eo: eo.memset(onesP.h[:, :], 1.0), w=[onesP.b])
    P.do("v", lambda eo: eo.memset(epsc.h[:, :], 1e-5), w=[epsc.b])
    P.do("v", lambda eo: eo.memset(clampc.h[:, :], 1e-18), w=[clampc.b])
    P.do("v", lambda eo: eo.memset(ln8c.h[:, :], float(np.log(8.0))), w=[ln8c.b])
    P.do("v", lambda eo: eo.memset(st["t1"].h[:, :], 0.0), w=[st["t1"].b])
    for hh in range(2):
        for t in range(NQT):
            P.do("v", lambda eo, hh=hh, t=t: eo.memset(
                vhat[hh][t].h[:, DH:DH + 1], 1.0), w=[vhat[hh][t].b])
    for c in range(NCH):
        P.do("v", lambda eo, c=c: eo.memset(xn16[c].h[:, 0:SD], 0.0), w=[xn16[c].b])

    def ln_stats(slices):
        for piece in range(NG):
            cols = [slice(a.start + piece * 512, a.start + (piece + 1) * 512)
                    for _, a in slices]
            for c in range(NCH):
                t, _ = slices[c]
                P.do("t", lambda eo, t=t, c=c, cc=cols[c]: eo.matmul(
                    PQ1.h[0:1, :], ones_c.h[:, :], t.h[:, cc],
                    start=(c == 0), stop=(c == NCH - 1)), r=[t.b, ones_c.b], w=[PQ1.b])
            P.do("s", lambda eo, piece=piece: eo.activation(
                srow.h[0:1, piece * 512:(piece + 1) * 512], PQ1.h[0:1, :],
                ACT.Identity), r=[PQ1.b], w=[srow.b])
            for c in range(NCH):
                t, _ = slices[c]
                P.do("v", lambda eo, t=t, cc=cols[c]: eo.tensor_tensor(
                    out=sq16.h[:, 0:512], in0=t.h[:, cc], in1=t.h[:, cc],
                    op=ALU.mult), r=[t.b], w=[sq16.b])
                P.do("t", lambda eo, c=c: eo.matmul(
                    PQ2.h[0:1, :], ones_c.h[:, :], sq16.h[:, 0:512],
                    start=(c == 0), stop=(c == NCH - 1)),
                    r=[sq16.b, ones_c.b], w=[PQ2.b])
            P.do("s", lambda eo, piece=piece: eo.activation(
                s2row.h[0:1, piece * 512:(piece + 1) * 512], PQ2.h[0:1, :],
                ACT.Identity), r=[PQ2.b], w=[s2row.b])

    def ln_rows():
        P.do("v", lambda eo: eo.tensor_scalar(out=murow.h[0:1, :], in0=srow.h[0:1, :],
             scalar1=1.0 / D, scalar2=None, op0=ALU.mult), r=[srow.b], w=[murow.b])
        for jh in range(S // 1024):
            hs = slice(jh * 1024, (jh + 1) * 1024)
            P.do("v", lambda eo, hs=hs: eo.tensor_tensor(
                out=PP1.h[0:1, :], in0=murow.h[0:1, hs],
                in1=murow.h[0:1, hs], op=ALU.mult), r=[murow.b], w=[PP1.b])
            P.do("v", lambda eo, hs=hs: eo.tensor_scalar(
                out=rstdrow.h[0:1, hs], in0=s2row.h[0:1, hs],
                scalar1=1.0 / D, scalar2=None, op0=ALU.mult),
                r=[s2row.b], w=[rstdrow.b])
            P.do("v", lambda eo, hs=hs: eo.tensor_tensor(
                out=rstdrow.h[0:1, hs], in0=rstdrow.h[0:1, hs],
                in1=PP1.h[0:1, :], op=ALU.subtract),
                r=[PP1.b, rstdrow.b], w=[rstdrow.b])
        P.do("s", lambda eo: eo.activation(rstdrow.h[0:1, :], rstdrow.h[0:1, :],
             ACT.Ln, bias=epsc.h[0:1, :]), r=[rstdrow.b, epsc.b], w=[rstdrow.b])
        P.do("s", lambda eo: eo.activation(rstdrow.h[0:1, :], rstdrow.h[0:1, :],
             ACT.Exp, scale=-0.5), r=[rstdrow.b], w=[rstdrow.b])
        P.do("v", lambda eo: eo.tensor_tensor(out=murow16.h[0:1, :],
             in0=murow.h[0:1, :], in1=rstdrow.h[0:1, :], op=ALU.mult),
             r=[murow.b, rstdrow.b], w=[murow16.b])
        P.do("v", lambda eo: eo.tensor_copy(rstdrow16.h[0:1, :], rstdrow.h[0:1, :]),
             r=[rstdrow.b], w=[rstdrow16.b])

    def bcast_to(pp, row16, half):
        c0 = half * 1024
        for j in range(2):
            P.do("t", lambda eo, j=j, c0=c0, pp=pp, row16=row16: eo.matmul(
                pp.h[:, j * 512:(j + 1) * 512], onesP.h[0:1, 0:128],
                row16.h[0:1, c0 + j * 512:c0 + (j + 1) * 512], start=True, stop=True),
                r=[onesP.b, row16.b], w=[pp.b])

    def ssq_rows(sqtile):
        for piece in range(NG):
            PQ = PQ2 if piece % 2 == 0 else PQ3
            P.do("t", lambda eo, piece=piece, PQ=PQ, sqtile=sqtile:
                 eo.matmul(PQ.h[0:33, :], E2.h[:, :],
                           sqtile.h[:, piece * 512:(piece + 1) * 512],
                           start=True, stop=True), r=[sqtile.b, E2.b], w=[PQ.b])
            eng = "s" if piece % 2 == 0 else "v"
            if eng == "s":
                P.do("s", lambda eo, piece=piece, PQ=PQ: eo.activation(
                    rrowA.h[:, piece * 512:(piece + 1) * 512], PQ.h[0:33, :],
                    ACT.Identity), r=[PQ.b], w=[rrowA.b])
            else:
                P.do("v", lambda eo, piece=piece, PQ=PQ: eo.tensor_copy(
                    rrowA.h[:, piece * 512:(piece + 1) * 512], PQ.h[0:33, :]),
                    r=[PQ.b], w=[rrowA.b])

    def finish_rrows(with_itemp):
        P.do("v", lambda eo: eo.tensor_scalar(
            out=rrowA.h[:, :], in0=rrowA.h[:, :], scalar1=clampc.h[0:33, :],
            scalar2=None, op0=ALU.max), r=[rrowA.b, clampc.b], w=[rrowA.b])
        P.do("s", lambda eo: eo.activation(rrowA.h[:, :], rrowA.h[:, :], ACT.Ln),
             r=[rrowA.b], w=[rrowA.b])
        if with_itemp:
            P.do("s", lambda eo: eo.activation(rrow16A.h[:, :], rrowA.h[:, :],
                 ACT.Exp, scale=-0.5, bias=lnit.h[0:33, :]),
                 r=[rrowA.b, lnit.b], w=[rrow16A.b])
        else:
            P.do("s", lambda eo: eo.activation(rrow16A.h[:, :], rrowA.h[:, :],
                 ACT.Exp, scale=-0.5), r=[rrowA.b], w=[rrow16A.b])

    def bcast_row16(pp, hh, half):
        c0 = half * 1024
        b0 = hh * 32
        for j in range(2):
            P.do("t", lambda eo, j=j, c0=c0, pp=pp, b0=b0, hh=hh: eo.matmul(
                pp.h[:, j * 512:(j + 1) * 512], onesP.h[b0:b0 + 1, 0:128],
                rrow16[hh].h[0:1, c0 + j * 512:c0 + (j + 1) * 512],
                start=True, stop=True), r=[onesP.b, rrow16A.b], w=[pp.b])

    def av_divide(dst_fn):
        P.do("s", lambda eo: eo.activation(rd32.h[0:1, :], PQ3.h[DH:DH + 1, :],
             ACT.Ln), r=[PQ3.b], w=[rd32.b])
        P.do("s", lambda eo: eo.activation(rd16.h[0:1, :], rd32.h[0:1, :],
             ACT.Exp, scale=-1.0), r=[rd32.b], w=[rd16.b])
        P.do("t", lambda eo: eo.matmul(PQ1.h[0:64, :], onesP.h[64:65, 0:64],
             rd16.h[0:1, :], start=True, stop=True), r=[onesP.b, rd16.b], w=[PQ1.b])
        P.do("s", lambda eo: eo.activation(rdb16.h[:, :], PQ1.h[0:64, :],
             ACT.Identity), r=[PQ1.b], w=[rdb16.b])
        P.do("v", lambda eo: eo.tensor_tensor(out=tmp16a.h[0:64, :],
             in0=PQ3.h[0:DH, :], in1=rdb16.h[:, :], op=ALU.mult),
             r=[PQ3.b, rdb16.b], w=[tmp16a.b])
        dst_fn()

    ln_stats([(x16[c], DATA) for c in range(NCH)])
    ln_rows()
    for half in range(HLV):
        bcast_to(PP1, rstdrow16, half)
        bcast_to(PP2, murow16, half)
        for c in range(NCH):
            for j in range(2):
                cs = dslc(half * 1024 + j * 512, half * 1024 + (j + 1) * 512)
                ps = slice(j * 512, (j + 1) * 512)
                P.do("v", lambda eo, c=c, cs=cs, ps=ps: eo.tensor_tensor(
                    out=wtmp2.h[:, :], in0=x16[c].h[:, cs], in1=PP1.h[:, ps],
                    op=ALU.mult), r=[x16[c].b, PP1.b], w=[wtmp2.b])
                P.do("v", lambda eo, c=c, cs=cs, ps=ps: eo.tensor_tensor(
                    out=xn16[c].h[:, cs], in0=wtmp2.h[:, :], in1=PP2.h[:, ps],
                    op=ALU.subtract), r=[wtmp2.b, PP2.b], w=[xn16[c].b])
    P.barrier()

    for c in range(NCH):
        P.do("s", lambda eo, c=c: eo.activation(
            sq16.h[:, 0:S], xn16[c].h[:, DATA], ACT.Identity,
            accum_out=meanT.h[:, c:c + 1]), r=[xn16[c].b], w=[sq16.b, meanT.b])
    P.do("v", lambda eo: eo.tensor_tensor(out=meanT.h[:, :], in0=meanT.h[:, :],
         in1=wtT.h[:, :], op=ALU.mult), r=[meanT.b, wtT.b], w=[meanT.b])
    P.do("v", lambda eo: eo.tensor_reduce(out=tcol.h[:, :], in_=meanT.h[:, :],
         axis=AX.X, op=ALU.add), r=[meanT.b], w=[tcol.b])
    P.do("v", lambda eo: eo.tensor_scalar(out=tcol16.h[:, :], in0=tcol.h[:, :],
         scalar1=1.0 / S, scalar2=None, op0=ALU.mult), r=[tcol.b], w=[tcol16.b])
    P.do("t", lambda eo: eo.matmul(PQ1.h[0:1, 0:1], tcol16.h[:, :], ones_c.h[:, :],
         start=True, stop=True), r=[tcol16.b, ones_c.b], w=[PQ1.b])
    P.do("y", lambda eo: eo.dma_start(out=scal1.h[0:1, :], in_=bt_s.h),
         r=[bt_s.b], w=[scal1.b], dma=True)
    P.do("s", lambda eo: eo.activation(itemp.h[0:1, :], PQ1.h[0:1, 0:1], ACT.Sigmoid,
         bias=scal1.h[0:1, :]), r=[PQ1.b, scal1.b], w=[itemp.b])
    P.do("v", lambda eo: eo.tensor_scalar(out=itemp.h[0:1, :], in0=itemp.h[0:1, :],
         scalar1=0.5, scalar2=None, op0=ALU.add), r=[itemp.b], w=[itemp.b])
    P.do("v", lambda eo: eo.reciprocal(itemp.h[0:1, :], itemp.h[0:1, :]),
         r=[itemp.b], w=[itemp.b])
    P.do("s", lambda eo: eo.activation(scal1.h[0:1, :], itemp.h[0:1, :], ACT.Ln),
         r=[itemp.b], w=[scal1.b])
    P.do("v", lambda eo: eo.tensor_copy(scal16.h[0:1, :], scal1.h[0:1, :]),
         r=[scal1.b], w=[scal16.b])
    P.do("t", lambda eo: eo.matmul(PQ1.h[:, 0:1], onesP.h[0:1, 0:128],
         scal16.h[0:1, :], start=True, stop=True), r=[onesP.b, scal16.b], w=[PQ1.b])
    P.do("v", lambda eo: eo.tensor_copy(lnit.h[:, :], PQ1.h[:, 0:1]),
         r=[PQ1.b], w=[lnit.b])
    for mul, dst in ((0.6744898 * 0.125, tau0c), (1.0 / 5207.5, irho0c)):
        P.do("v", lambda eo, mul=mul: eo.tensor_scalar(out=scal1.h[0:1, :],
             in0=itemp.h[0:1, :], scalar1=mul, scalar2=None, op0=ALU.mult),
             r=[itemp.b], w=[scal1.b])
        P.do("v", lambda eo: eo.tensor_copy(scal16.h[0:1, :], scal1.h[0:1, :]),
             r=[scal1.b], w=[scal16.b])
        P.do("t", lambda eo: eo.matmul(PQ1.h[:, 0:1], onesP.h[0:1, 0:128],
             scal16.h[0:1, :], start=True, stop=True),
             r=[onesP.b, scal16.b], w=[PQ1.b])
        P.do("v", lambda eo, dst=dst: eo.tensor_copy(dst.h[:, :], PQ1.h[:, 0:1]),
             r=[PQ1.b], w=[dst.b])
    P.barrier()

    def load_cols(Wd, dst, col0, ncol=128):
        t_ = col0 // 128
        src = Wd.h[t_ * 128:(t_ + 1) * 128, :]
        P.do("g", lambda eo, src=src, dst=dst: eo.dma_start(
            out=dst.h[:, 0:(ncol * NCH)], in_=src), r=[Wd.b], w=[dst.b], dma=True)

    def load_wpair(p, slot):
        load_cols(Wq, wq16[slot], p * 128)
        load_cols(Wk, wk16[slot], p * 128)
        load_cols(Wv, wv16[slot], p * 128)

    def vT_make(wvt, hh, bias_row, bias_off, src_tiles, src_off):
        for t in range(NQT):
            PQ = PQ1 if t % 2 == 0 else PQ2
            for c in range(NCH):
                tt, off = src_tiles[c], src_off[c]
                P.do("t", lambda eo, t=t, c=c, PQ=PQ, tt=tt, off=off, wvt=wvt, hh=hh:
                     eo.matmul(PQ.h[:, 0:64],
                               tt.h[:, off + t * 128:off + (t + 1) * 128],
                               wvt.h[:, c * 128 + hh * 64:c * 128 + hh * 64 + 64],
                               start=(c == 0), stop=False),
                     r=[tt.b, wvt.b], w=[PQ.b])
            P.do("t", lambda eo, t=t, PQ=PQ, bias_row=bias_row, bias_off=bias_off,
                 hh=hh: eo.matmul(
                     PQ.h[:, 0:64], onesP.h[32:33, 0:128],
                     bias_row.h[0:1, bias_off + hh * 64:bias_off + hh * 64 + 64],
                     start=False, stop=True),
                 r=[onesP.b, bias_row.b], w=[PQ.b])
            eng = "v" if t % 2 == 0 else "s"
            if eng == "v":
                P.do("v", lambda eo, hh=hh, t=t, PQ=PQ: eo.tensor_copy(
                    vhat[hh][t].h[:, 0:DH], PQ.h[:, 0:64]),
                    r=[PQ.b], w=[vhat[hh][t].b])
            else:
                P.do("s", lambda eo, hh=hh, t=t, PQ=PQ: eo.activation(
                    vhat[hh][t].h[:, 0:DH], PQ.h[:, 0:64],
                    ACT.Identity), r=[PQ.b], w=[vhat[hh][t].b])

    pending = [None]

    def emit_av(ctx):
        if ctx is None:
            return
        hh, hb, hd, halfb, qt, kt_ = ctx
        for gi in range(NG // 2):
            gidx = halfb * (NG // 2) + gi
            qs = dslc(gidx * 512, (gidx + 1) * 512)
            P.do("t", lambda eo, gidx=gidx: eo.matmul(PQ2.h[:, :],
                 onesP.h[64:65, 0:128],
                 throw.h[0:1, gidx * 512:(gidx + 1) * 512],
                 start=True, stop=True), r=[onesP.b, throw.b], w=[PQ2.b])
            P.do("s", lambda eo: eo.activation(thb.h[:, :], PQ2.h[:, :],
                 ACT.Identity), r=[PQ2.b], w=[thb.b])
            def qkb(kc, qs=qs, hb=hb):
                PQ = PQ1 if kc % 2 == 0 else PQ2
                P.do("t", lambda eo, kc=kc, qs=qs, PQ=PQ, hb=hb: eo.matmul(
                    PQ.h[:, :],
                    kt_.h[hb:hb + 64, SD + kc * 128:SD + (kc + 1) * 128],
                    qt.h[hb:hb + 64, qs], start=True, stop=True),
                    r=[kt_.b, qt.b], w=[PQ.b])
            qkb(0)
            for kc in range(NQT):
                PQ = PQ1 if kc % 2 == 0 else PQ2
                yt = yT[kc % 3]
                if kc + 1 < NQT:
                    qkb(kc + 1)
                P.do("s", lambda eo, PQ=PQ, yt=yt: eo.activation(
                    yt.h[:, :], PQ.h[:, :], ACT.Exp), r=[PQ.b], w=[yt.b])
                P.do("v", lambda eo, yt=yt: eo.tensor_tensor(
                    out=tmp16a.h[:, :], in0=yt.h[:, :], in1=thb.h[:, :],
                    op=ALU.is_gt), r=[yt.b, thb.b], w=[tmp16a.b])
                P.do("v", lambda eo, yt=yt: eo.tensor_tensor(
                    out=yt.h[:, :], in0=yt.h[:, :], in1=tmp16a.h[:, :],
                    op=ALU.mult), r=[yt.b, tmp16a.b], w=[yt.b])
                P.do("t", lambda eo, kc=kc, hh=hh, yt=yt: eo.matmul(
                    PQ3.h[0:DH + 1, :], vhat[hh][kc].h[:, :], yt.h[:, :],
                    start=(kc == 0), stop=(kc == NQT - 1)),
                    r=[vhat[hh][kc].b, yt.b], w=[PQ3.b])

            def store(hd=hd, gidx=gidx):
                P.do("y", lambda eo: eo.dma_start(
                    out=sp_scr.h[hd * 64:(hd + 1) * 64,
                                 gidx * 512:(gidx + 1) * 512],
                    in_=tmp16a.h[0:64, :]), r=[tmp16a.b], w=[sp_scr.b],
                    dma=True)
            av_divide(store)


    load_wpair(0, 0)
    for p in range(H // 2):
        slot = p % 2
        if p + 1 < H // 2:
            load_wpair(p + 1, 1 - slot)
        emit_av(pending[0])
        pending[0] = None
        qt = q16p if p % 2 == 0 else q16pB
        kt_ = k16p if p % 2 == 0 else k16pB
        for wtl, bias_t, dst16, is_q in ((wq16, bias_q, qt, True),
                                         (wk16, bias_k, kt_, False)):
            wt16 = wtl[slot]
            for half in range(HLV):
                PP = PP1 if half == 0 else PP2
                for c in range(NCH):
                    for j in range(2):
                        P.do("t", lambda eo, c=c, j=j, half=half, wt16=wt16, PP=PP:
                             eo.matmul(PP.h[:, j * 512:(j + 1) * 512],
                                       wt16.h[:, c * 128:(c + 1) * 128],
                                       xn16[c].h[:, SD + half * 1024 + j * 512:
                                                 SD + half * 1024 + (j + 1) * 512],
                                       start=(c == 0), stop=(c == NCH - 1)),
                             r=[wt16.b, xn16[c].b], w=[PP.b])
                cs = dslc(half * 1024, (half + 1) * 1024)
                P.do("s", lambda eo, PP=PP, cs=cs, bias_t=bias_t, dst16=dst16, p=p:
                     eo.activation(dst16.h[:, cs], PP.h[:, :], ACT.Identity,
                                   bias=bias_t.h[:, p:p + 1]),
                     r=[PP.b, bias_t.b], w=[dst16.b])
                csq = slice(half * 1024, (half + 1) * 1024)
                P.do("s", lambda eo, PP=PP, csq=csq, bias_t=bias_t, p=p:
                     eo.activation(sq16.h[:, csq], PP.h[:, :], ACT.Square,
                                   bias=bias_t.h[:, p:p + 1]),
                     r=[PP.b, bias_t.b], w=[sq16.b])
            ssq_rows(sq16)
            finish_rrows(is_q)
            for hh in range(2):
                hb = hh * 64
                for half in range(HLV):
                    PP = PP1 if half == 0 else PP2
                    bcast_row16(PP, hh, half)
                    cs = dslc(half * 1024, (half + 1) * 1024)
                    P.do("v", lambda eo, dst16=dst16, hb=hb, cs=cs, PP=PP:
                         eo.tensor_tensor(out=dst16.h[hb:hb + 64, cs],
                                          in0=dst16.h[hb:hb + 64, cs],
                                          in1=PP.h[hb:hb + 64, :], op=ALU.mult),
                         r=[dst16.b, PP.b], w=[dst16.b])
        for hh in range(2):
            vT_make(wv16[slot], hh, bvrow, p * 128,
                    [xn16[c] for c in range(NCH)], [SD] * NCH)
            hb = hh * 64
            hd = p * 2 + hh
            for halfb in range(2):
                for ti in range(QTH):
                    t = halfb * QTH + ti
                    PP = PP1 if ti % 2 == 0 else PP2
                    for hf2 in range(NG // 2):
                        for jj in range(2):
                            ks = dslc((hf2 * 2 + jj) * 512, (hf2 * 2 + jj + 1) * 512)
                            P.do("t", lambda eo, t=t, jj=jj, PP=PP, hb=hb,
                                 ks=ks, qt=qt, kt_=kt_:
                                 eo.matmul(PP.h[:, jj * 512:(jj + 1) * 512],
                                           qt.h[hb:hb + 64,
                                                SD + t * 128:SD + (t + 1) * 128],
                                           kt_.h[hb:hb + 64, ks],
                                           start=True, stop=True),
                                 r=[qt.b, kt_.b], w=[PP.b])
                        if (ti + hf2) % 2 == 0:
                            P.do("s", lambda eo, ti=ti, hf2=hf2, PP=PP:
                                 eo.activation(
                                     s16[ti].h[:, hf2 * 1024:(hf2 + 1) * 1024],
                                     PP.h[:, :], ACT.Identity),
                                 r=[PP.b], w=[s16[ti].b])
                        else:
                            P.do("v", lambda eo, ti=ti, hf2=hf2, PP=PP:
                                 eo.tensor_copy(
                                     s16[ti].h[:, hf2 * 1024:(hf2 + 1) * 1024],
                                     PP.h[:, :]), r=[PP.b], w=[s16[ti].b])
                P.do("v", lambda eo: eo.tensor_scalar(out=st["tauA"].h[:, :],
                     in0=st["t1"].h[:, :], scalar1=0.0, scalar2=tau0c.h[:, :],
                     op0=ALU.mult, op1=ALU.add), r=[st["t1"].b, tau0c.b],
                     w=[st["tauA"].b])
                P.do("v", lambda eo: eo.memset(st["lo"].h[:, :], -4.2), w=[st["lo"].b])
                P.do("v", lambda eo: eo.memset(st["hi"].h[:, :], 4.2), w=[st["hi"].b])
                P.do("v", lambda eo: eo.memset(st["bd"].h[:, :], 1e9), w=[st["bd"].b])
                P.do("v", lambda eo: eo.memset(st["bt"].h[:, :], 0.0), w=[st["bt"].b])
                P.do("v", lambda eo: eo.tensor_scalar(out=st["tnA"].h[:, :],
                     in0=st["tauA"].h[:, :], scalar1=-1.0, scalar2=None,
                     op0=ALU.mult), r=[st["tauA"].b], w=[st["tnA"].b])
                cur, prv = "A", "B"
                QV = QTH // 2
                NSUB = max(0, NPROBE - 2)
                for probe in range(NPROBE):
                    sub = probe < NSUB
                    tau_c, c_c = st["tau" + cur], st["c" + cur]
                    tau_p, c_p = st["tau" + prv], st["c" + prv]
                    tn_c, tn_p = st["tn" + cur], st["tn" + prv]
                    cbV, cbS = stc["c" + cur]
                    for ti in range(QV):
                        ap = (s16[ti].h[:, 0:S:4] if sub else s16[ti].h[:, 0:S])
                        nout = S // 4 if sub else S
                        P.do("v", lambda eo, ti=ti, tau_c=tau_c, c_c=c_c, ap=ap,
                             nout=nout: eo.tensor_scalar(out=dump8.h[:, 0:nout],
                                              in0=ap,
                                              scalar1=tau_c.h[:, ti:ti + 1],
                                              scalar2=None, op0=ALU.is_gt,
                                              op1=ALU.add,
                                              accum_out=c_c.h[:, ti:ti + 1]),
                             r=[s16[ti].b, tau_c.b], w=[dump8.b, cbV.b])
                    for ti in range(QV, QTH):
                        ap = (s16[ti].h[:, 0:S:4] if sub else s16[ti].h[:, 0:S])
                        nout = S // 4 if sub else S
                        P.do("s", lambda eo, ti=ti, tn_c=tn_c, c_c=c_c, ap=ap,
                             nout=nout: eo.activation(sq16.h[:, 0:nout], ap,
                                           ACT.Sign, bias=tn_c.h[:, ti:ti + 1],
                                           accum_out=c_c.h[:, ti:ti + 1]),
                             r=[s16[ti].b, tn_c.b], w=[sq16.b, cbS.b])
                    if sub:
                        P.do("v", lambda eo, c_c=c_c: eo.tensor_scalar(
                            out=c_c.h[:, 0:QV], in0=c_c.h[:, 0:QV],
                            scalar1=4.0, scalar2=None, op0=ALU.mult),
                            r=[cbV.b], w=[cbV.b])
                        P.do("v", lambda eo, c_c=c_c: eo.tensor_scalar(
                            out=c_c.h[:, QV:QTH], in0=c_c.h[:, QV:QTH],
                            scalar1=2.0, scalar2=float(S // 2), op0=ALU.mult,
                            op1=ALU.add), r=[cbS.b], w=[cbS.b])
                    else:
                        P.do("v", lambda eo, c_c=c_c: eo.tensor_scalar(
                            out=c_c.h[:, QV:QTH], in0=c_c.h[:, QV:QTH],
                            scalar1=0.5, scalar2=float(S // 2), op0=ALU.mult,
                            op1=ALU.add), r=[cbS.b], w=[cbS.b])
                    P.raw("v", lambda eo: eo.drain())
                    d1, t1, t2, t3 = st["d1"], st["t1"], st["t2"], st["t3"]
                    lo_, hi_, bt_, bd_ = st["lo"], st["hi"], st["bt"], st["bd"]

                    def U(fn, r, w):
                        P.do("v", fn, r=r, w=w)
                    U(lambda eo, c_c=c_c: eo.tensor_scalar(out=d1.h[:, :],
                      in0=c_c.h[:, :], scalar1=-float(KK), scalar2=None,
                      op0=ALU.add), [cbV.b, cbS.b], [d1.b])
                    U(lambda eo: eo.tensor_scalar(out=msk.h[:, :], in0=d1.h[:, :],
                      scalar1=0.0, scalar2=None, op0=ALU.is_ge), [d1.b], [msk.b])
                    U(lambda eo: eo.tensor_scalar(out=mskn.h[:, :], in0=d1.h[:, :],
                      scalar1=0.0, scalar2=None, op0=ALU.is_lt), [d1.b], [mskn.b])
                    U(lambda eo, tau_c=tau_c: eo.copy_predicated(lo_.h[:, :],
                      msk.h[:, :], tau_c.h[:, :]), [msk.b, tau_c.b, lo_.b], [lo_.b])
                    U(lambda eo, tau_c=tau_c: eo.copy_predicated(hi_.h[:, :],
                      mskn.h[:, :], tau_c.h[:, :]), [mskn.b, tau_c.b, hi_.b], [hi_.b])
                    if not sub:
                        U(lambda eo: eo.tensor_scalar(out=t2.h[:, :], in0=d1.h[:, :],
                          scalar1=-1.0, scalar2=None, op0=ALU.mult), [d1.b], [t2.b])
                        U(lambda eo: eo.tensor_tensor(out=t2.h[:, :], in0=t2.h[:, :],
                          in1=d1.h[:, :], op=ALU.max), [t2.b, d1.b], [t2.b])
                        U(lambda eo: eo.tensor_tensor(out=msk.h[:, :], in0=t2.h[:, :],
                          in1=bd_.h[:, :], op=ALU.is_le), [t2.b, bd_.b], [msk.b])
                        U(lambda eo, tau_c=tau_c: eo.copy_predicated(bt_.h[:, :],
                          msk.h[:, :], tau_c.h[:, :]), [msk.b, tau_c.b, bt_.b], [bt_.b])
                        U(lambda eo: eo.copy_predicated(bd_.h[:, :], msk.h[:, :],
                          t2.h[:, :]), [msk.b, t2.b, bd_.b], [bd_.b])
                    if probe < NPROBE - 1:
                        if probe == 0:
                            U(lambda eo: eo.tensor_scalar(out=t2.h[:, :],
                              in0=d1.h[:, :], scalar1=irho0c.h[:, :], scalar2=None,
                              op0=ALU.mult), [d1.b, irho0c.b], [t2.b])
                        else:
                            U(lambda eo, tau_c=tau_c, tau_p=tau_p: eo.tensor_tensor(
                              out=t1.h[:, :], in0=tau_c.h[:, :], in1=tau_p.h[:, :],
                              op=ALU.subtract), [tau_c.b, tau_p.b], [t1.b])
                            U(lambda eo, c_c=c_c, c_p=c_p: eo.tensor_tensor(
                              out=t2.h[:, :], in0=c_p.h[:, :], in1=c_c.h[:, :],
                              op=ALU.subtract),
                              [stc["c" + prv][0].b, stc["c" + prv][1].b, cbV.b, cbS.b],
                              [t2.b])
                            U(lambda eo: eo.reciprocal(t1.h[:, :], t1.h[:, :]),
                              [t1.b], [t1.b])
                            U(lambda eo: eo.tensor_tensor(out=t2.h[:, :],
                              in0=t2.h[:, :], in1=t1.h[:, :], op=ALU.mult),
                              [t2.b, t1.b], [t2.b])
                            U(lambda eo: eo.tensor_scalar(out=t2.h[:, :],
                              in0=t2.h[:, :], scalar1=20.0, scalar2=2.0e6,
                              op0=ALU.max, op1=ALU.min), [t2.b], [t2.b])
                            U(lambda eo: eo.reciprocal(t2.h[:, :], t2.h[:, :]),
                              [t2.b], [t2.b])
                            U(lambda eo: eo.tensor_tensor(out=t2.h[:, :],
                              in0=d1.h[:, :], in1=t2.h[:, :], op=ALU.mult),
                              [d1.b, t2.b], [t2.b])
                        U(lambda eo, tau_c=tau_c, tau_p=tau_p: eo.tensor_tensor(
                          out=tau_p.h[:, :], in0=tau_c.h[:, :], in1=t2.h[:, :],
                          op=ALU.add), [tau_c.b, t2.b], [tau_p.b])
                        U(lambda eo: eo.tensor_tensor(out=t1.h[:, :],
                          in0=hi_.h[:, :], in1=lo_.h[:, :], op=ALU.subtract),
                          [hi_.b, lo_.b], [t1.b])
                        U(lambda eo: eo.tensor_scalar(out=t1.h[:, :],
                          in0=t1.h[:, :], scalar1=0.05, scalar2=None,
                          op0=ALU.mult), [t1.b], [t1.b])
                        U(lambda eo: eo.tensor_tensor(out=t3.h[:, :],
                          in0=lo_.h[:, :], in1=t1.h[:, :], op=ALU.add),
                          [lo_.b, t1.b], [t3.b])
                        U(lambda eo, tau_p=tau_p: eo.tensor_tensor(
                          out=tau_p.h[:, :], in0=tau_p.h[:, :], in1=t3.h[:, :],
                          op=ALU.max), [tau_p.b, t3.b], [tau_p.b])
                        U(lambda eo: eo.tensor_tensor(out=t3.h[:, :],
                          in0=hi_.h[:, :], in1=t1.h[:, :], op=ALU.subtract),
                          [hi_.b, t1.b], [t3.b])
                        U(lambda eo, tau_p=tau_p: eo.tensor_tensor(
                          out=tau_p.h[:, :], in0=tau_p.h[:, :], in1=t3.h[:, :],
                          op=ALU.min), [tau_p.b, t3.b], [tau_p.b])
                        U(lambda eo, tau_p=tau_p, tn_p=tn_p: eo.tensor_scalar(
                          out=tn_p.h[:, :], in0=tau_p.h[:, :], scalar1=-1.0,
                          scalar2=None, op0=ALU.mult), [tau_p.b], [tn_p.b])
                        cur, prv = prv, cur
                P.do("s", lambda eo: eo.activation(th16t.h[:, :], st["bt"].h[:, :],
                     ACT.Exp), r=[st["bt"].b], w=[th16t.b])
                for ti in range(QTH):
                    t = halfb * QTH + ti
                    P.do("y", lambda eo, t=t, ti=ti: eo.dma_start(
                        out=throw.h[0:1, t * 128:(t + 1) * 128],
                        in_=th16t.h[:, ti:ti + 1]), r=[th16t.b], w=[throw.b],
                        dma=True)
                emit_av(pending[0])
                pending[0] = (hh, hb, hd, halfb, qt, kt_)
    emit_av(pending[0])
    pending[0] = None
    P.barrier()

    sp16 = s16
    for c in range(NCH):
        P.do("y", lambda eo, c=c: eo.dma_start(out=sp16[c].h[:, 0:S],
             in_=sp_scr.h[c * 128:(c + 1) * 128, :]), r=[sp_scr.b],
             w=[sp16[c].b], dma=True)

    def load_wog(oc, slot):
        load_cols(Wo, wo16[slot], oc * 128)
        load_cols(Wg, wg16[slot], oc * 128)

    load_wog(0, 0)
    for oc in range(NCH):
        slot = oc % 2
        if oc + 1 < NCH:
            load_wog(oc + 1, 1 - slot)
        for gidx in range(NG):
            qs0 = slice(gidx * 512, (gidx + 1) * 512)
            qsd = dslc(gidx * 512, (gidx + 1) * 512)
            PU = PQ1 if gidx % 2 == 0 else PQ2
            for c in range(NCH):
                P.do("t", lambda eo, c=c, qs0=qs0, PU=PU, slot=slot: eo.matmul(
                    PU.h[:, :], wo16[slot].h[:, c * 128:(c + 1) * 128],
                    sp16[c].h[:, qs0], start=(c == 0), stop=False),
                    r=[wo16[slot].b, sp16[c].b], w=[PU.b])
            P.do("t", lambda eo, PU=PU, oc=oc: eo.matmul(
                PU.h[:, :], borow.h[0:1, oc * 128:(oc + 1) * 128],
                onesP.h[0:1, 0:512], start=False, stop=True),
                r=[borow.b, onesP.b], w=[PU.b])
            for c in range(NCH):
                P.do("t", lambda eo, c=c, qsd=qsd, slot=slot: eo.matmul(
                    PQ3.h[:, :], wg16[slot].h[:, c * 128:(c + 1) * 128],
                    xn16[c].h[:, qsd], start=(c == 0), stop=(c == NCH - 1)),
                    r=[wg16[slot].b, xn16[c].b], w=[PQ3.b])
            P.do("s", lambda eo, oc=oc: eo.activation(
                tmp16b.h[:, :], PQ3.h[:, :], ACT.Sigmoid,
                bias=bias_g.h[:, oc:oc + 1]), r=[PQ3.b, bias_g.b], w=[tmp16b.b])
            P.do("v", lambda eo, oc=oc, qsd=qsd, PU=PU: eo.tensor_tensor(
                out=tmp32a.h[:, :], in0=PU.h[:, :], in1=xn16[oc].h[:, qsd],
                op=ALU.subtract), r=[PU.b, xn16[oc].b], w=[tmp32a.b])
            P.do("v", lambda eo: eo.tensor_tensor(out=tmp32a.h[:, :],
                 in0=tmp32a.h[:, :], in1=tmp16b.h[:, :], op=ALU.mult),
                 r=[tmp32a.b, tmp16b.b], w=[tmp32a.b])
            P.do("v", lambda eo, oc=oc, qsd=qsd: eo.tensor_tensor(
                out=tmp16a.h[:, :], in0=tmp32a.h[:, :], in1=xn16[oc].h[:, qsd],
                op=ALU.add), r=[tmp32a.b, xn16[oc].b], w=[tmp16a.b])
            P.do("y", lambda eo, oc=oc, gidx=gidx: eo.dma_start(
                out=s_scr.h[oc * 128:(oc + 1) * 128, gidx * 512:(gidx + 1) * 512],
                in_=tmp16a.h[:, :]), r=[tmp16a.b], w=[s_scr.b], dma=True)
    P.barrier()

    qc = s16[0:4]
    kc16 = s16[4:8]
    sqf = sq16
    qshf = q16p
    h1_16 = s16
    SHIFTOFF = [SD] * 4 + [SD - 1] * 4

    for l in range(2):
        for srcd, dstt in ((qsc_c, qscol), (ksc_c, kscol), (lng_c, lngc),
                           (lnb_c, lnbc), (b1p_c, b1c)):
            P.do("y", lambda eo, srcd=srcd, dstt=dstt, l=l: eo.dma_start(
                out=dstt.h[:, :], in_=srcd.h[l]), r=[srcd.b], w=[dstt.b], dma=True)
        P.do("g", lambda eo, l=l: eo.dma_start(out=b2row.h[:, :], in_=b2_r.h[l]),
             r=[b2_r.b], w=[b2row.b], dma=True)

        def load_wqkv(oc, slot, l=l):
            r0 = (l * 8 + oc) * 128
            src = wqkv_qk.h[r0:r0 + 128, :]
            P.do("g", lambda eo, src=src, slot=slot: eo.dma_start(
                out=wqkv16[slot].h[:, :], in_=src), r=[wqkv_qk.b],
                w=[wqkv16[slot].b], dma=True)

        load_wqkv(0, 0)
        for oc in range(8):
            slot = oc % 2
            if oc + 1 < 8:
                load_wqkv(oc + 1, 1 - slot)
            isq = oc < 4
            dst16 = qc[oc] if isq else kc16[oc - 4]
            for half in range(HLV):
                PP = PP1 if half == 0 else PP2
                for c in range(NCH):
                    base = SHIFTOFF[c]
                    for j in range(2):
                        P.do("t", lambda eo, c=c, j=j, half=half, slot=slot,
                             base=base, PP=PP: eo.matmul(
                                 PP.h[:, j * 512:(j + 1) * 512],
                                 wqkv16[slot].h[:, c * 128:(c + 1) * 128],
                                 xn16[c].h[:, base + half * 1024 + j * 512:
                                           base + half * 1024 + (j + 1) * 512],
                                 start=(c == 0), stop=(c == NCH - 1)),
                             r=[wqkv16[slot].b, xn16[c].b], w=[PP.b])
                cs = slice(half * 1024, (half + 1) * 1024)
                P.do("s", lambda eo, PP=PP, cs=cs, dst16=dst16: eo.activation(
                    dst16.h[:, cs], PP.h[:, :], ACT.Identity),
                    r=[PP.b], w=[dst16.b])
                P.do("s", lambda eo, PP=PP, cs=cs: eo.activation(
                    sqf.h[:, cs], PP.h[:, :], ACT.Square), r=[PP.b], w=[sqf.b])
            ssq_rows(sqf)
            finish_rrows(False)
            for hh in range(2):
                hb = hh * 64
                for half in range(HLV):
                    PP = PP1 if half == 0 else PP2
                    bcast_row16(PP, hh, half)
                    cs = slice(half * 1024, (half + 1) * 1024)
                    P.do("v", lambda eo, dst16=dst16, hb=hb, cs=cs, PP=PP:
                         eo.tensor_tensor(out=dst16.h[hb:hb + 64, cs],
                                          in0=dst16.h[hb:hb + 64, cs],
                                          in1=PP.h[hb:hb + 64, :], op=ALU.mult),
                         r=[dst16.b, PP.b], w=[dst16.b])

        for tiles, aidx, scol in ((qc, 0, qscol), (kc16, 2, kscol)):
            for qtr in range(NG):
                q0 = qtr * 512
                P.do("y", lambda eo, aidx=aidx, q0=q0: eo.dma_start(
                    out=rot2[0].h[:, :], in_=rotT.h[aidx][:, q0:q0 + 512]),
                    r=[rotT.b], w=[rot2[0].b], dma=True)
                P.do("y", lambda eo, aidx=aidx, q0=q0: eo.dma_start(
                    out=rot2[1].h[:, :], in_=rotT.h[aidx + 1][:, q0:q0 + 512]),
                    r=[rotT.b], w=[rot2[1].b], dma=True)
                for c4 in range(4):
                    tl = tiles[c4]
                    qsl = slice(q0, q0 + 512)
                    P.do("v", lambda eo, tl=tl, qsl=qsl, scol=scol: eo.tensor_scalar(
                        out=sqf.h[:, qsl], in0=tl.h[:, qsl], scalar1=scol.h[:, :],
                        scalar2=None, op0=ALU.mult), r=[tl.b, scol.b], w=[sqf.b])
                    for b0 in (0, 64):
                        P.do("y", lambda eo, b0=b0, qsl=qsl: eo.dma_start(
                            out=qshf.h[b0:b0 + 32, qsl],
                            in_=sqf.h[b0 + 32:b0 + 64, qsl]),
                            r=[sqf.b], w=[qshf.b], dma=True)
                        P.do("y", lambda eo, b0=b0, qsl=qsl: eo.dma_start(
                            out=qshf.h[b0 + 32:b0 + 64, qsl],
                            in_=sqf.h[b0:b0 + 32, qsl]),
                            r=[sqf.b], w=[qshf.b], dma=True)
                    P.do("v", lambda eo, qsl=qsl: eo.tensor_tensor(
                        out=sqf.h[:, qsl], in0=sqf.h[:, qsl], in1=rot2[0].h[:, :],
                        op=ALU.mult), r=[sqf.b, rot2[0].b], w=[sqf.b])
                    P.do("v", lambda eo, qsl=qsl: eo.tensor_tensor(
                        out=qshf.h[:, qsl], in0=qshf.h[:, qsl], in1=rot2[1].h[:, :],
                        op=ALU.mult), r=[qshf.b, rot2[1].b], w=[qshf.b])
                    P.do("v", lambda eo, tl=tl, qsl=qsl: eo.tensor_tensor(
                        out=tl.h[:, qsl], in0=sqf.h[:, qsl], in1=qshf.h[:, qsl],
                        op=ALU.add), r=[sqf.b, qshf.b], w=[tl.b])
        otile = qc
        for h in range(HF):
            hslot = h % 2
            r0 = (l * HF + h) * 128
            src = wqkv_v.h[r0:r0 + 128, :]
            P.do("g", lambda eo, src=src, hslot=hslot: eo.dma_start(
                out=fwo16[hslot].h[:, :], in_=src), r=[wqkv_v.b],
                w=[fwo16[hslot].b], dma=True)
            for t in range(NQT):
                PQ = PQ1 if t % 2 == 0 else PQ2
                for c in range(NCH):
                    base = SHIFTOFF[c]
                    P.do("t", lambda eo, t=t, c=c, PQ=PQ, base=base, hslot=hslot:
                         eo.matmul(PQ.h[:, 0:64],
                                   xn16[c].h[:, base + t * 128:base + (t + 1) * 128],
                                   fwo16[hslot].h[:, c * 64:(c + 1) * 64],
                                   start=(c == 0), stop=(c == NCH - 1)),
                         r=[xn16[c].b, fwo16[hslot].b], w=[PQ.b])
                eng = "v" if t % 2 == 0 else "s"
                if eng == "v":
                    P.do("v", lambda eo, hslot=hslot, t=t, PQ=PQ: eo.tensor_copy(
                        vhat[hslot][t].h[:, 0:DH], PQ.h[:, 0:64]),
                        r=[PQ.b], w=[vhat[hslot][t].b])
                else:
                    P.do("s", lambda eo, hslot=hslot, t=t, PQ=PQ: eo.activation(
                        vhat[hslot][t].h[:, 0:DH], PQ.h[:, 0:64],
                        ACT.Identity), r=[PQ.b], w=[vhat[hslot][t].b])
            cq = h // 2
            b0 = (h % 2) * 64
            for w in range(NW):
                kts = list(range(max(0, 4 * (w - 1)), 4 * (w + 1)))
                first, last = kts[0], kts[-1]
                def simk(i, kt, w=w, b0=b0, cq=cq):
                    PQ = PQ1 if i % 2 == 0 else PQ2
                    P.do("t", lambda eo, kt=kt, w=w, PQ=PQ, b0=b0, cq=cq: eo.matmul(
                        PQ.h[:, :], kc16[cq].h[b0:b0 + 64, kt * 128:(kt + 1) * 128],
                        qc[cq].h[b0:b0 + 64, w * 512:(w + 1) * 512],
                        start=True, stop=True), r=[kc16[cq].b, qc[cq].b], w=[PQ.b])
                simk(0, kts[0])
                for i, kt in enumerate(kts):
                    PQ = PQ1 if i % 2 == 0 else PQ2
                    yt = yT[i % 3]
                    off = (kt - 4 * w) * 128
                    if i + 1 < len(kts):
                        simk(i + 1, kts[i + 1])
                    iv = (off + 512) // 128
                    P.do("v", lambda eo, PQ=PQ, iv=iv: eo.tensor_tensor(
                        out=tmp32a.h[:, :], in0=PQ.h[:, :],
                        in1=madd.h[:, iv * 512:(iv + 1) * 512], op=ALU.add),
                        r=[PQ.b, madd.b], w=[tmp32a.b])
                    P.do("s", lambda eo, yt=yt: eo.activation(
                        yt.h[:, :], tmp32a.h[:, :], ACT.Exp, scale=8.0),
                        r=[tmp32a.b], w=[yt.b])
                    P.do("t", lambda eo, kt=kt, h=h, first=first, last=last,
                         hslot=hslot, yt=yt: eo.matmul(
                             PQ3.h[0:DH + 1, :], vhat[hslot][kt].h[:, :], yt.h[:, :],
                             start=(kt == first), stop=(kt == last)),
                         r=[vhat[hslot][kt].b, yt.b], w=[PQ3.b])

                def store(h=h, w=w):
                    P.do("y", lambda eo: eo.dma_start(
                        out=otile[h // 2].h[(h % 2) * 64:(h % 2) * 64 + 64,
                                            w * 512:(w + 1) * 512],
                        in_=tmp16a.h[0:64, :]), r=[tmp16a.b],
                        w=[otile[h // 2].b], dma=True)
                av_divide(store)
        P.barrier()
        def load_fwo(oc, slot, l=l):
            r0 = (l * 8 + oc) * 128
            src = fwo.h[r0:r0 + 128, :]
            P.do("g", lambda eo, src=src, slot=slot: eo.dma_start(
                out=fwo16[slot].h[:, :], in_=src), r=[fwo.b],
                w=[fwo16[slot].b], dma=True)

        load_fwo(0, 0)
        for oc in range(NCH):
            slot = oc % 2
            if oc + 1 < NCH:
                load_fwo(oc + 1, 1 - slot)
            for gidx in range(NG):
                qsd = dslc(gidx * 512, (gidx + 1) * 512)
                PU = PQ1 if gidx % 2 == 0 else PQ2
                for c in range(4):
                    P.do("t", lambda eo, c=c, gidx=gidx, PU=PU, slot=slot:
                         eo.matmul(PU.h[:, :],
                                   fwo16[slot].h[:, c * 128:(c + 1) * 128],
                                   otile[c].h[:, gidx * 512:(gidx + 1) * 512],
                                   start=(c == 0), stop=(c == 3)),
                         r=[fwo16[slot].b, otile[c].b], w=[PU.b])
                P.do("v", lambda eo, oc=oc, qsd=qsd, PU=PU: eo.tensor_tensor(
                    out=xn16[oc].h[:, qsd], in0=PU.h[:, :], in1=xn16[oc].h[:, qsd],
                    op=ALU.add), r=[PU.b, xn16[oc].b], w=[xn16[oc].b])
        P.barrier()
        ln_stats([(xn16[c], slice(SHIFTOFF[c], SHIFTOFF[c] + S))
                  for c in range(NCH)])
        ln_rows()
        for half in range(HLV):
            bcast_to(PP1, rstdrow16, half)
            bcast_to(PP2, murow16, half)
            for c in range(NCH):
                base = SHIFTOFF[c]
                for j in range(2):
                    csx = slice(base + half * 1024 + j * 512,
                                base + half * 1024 + (j + 1) * 512)
                    cs = slice(half * 1024 + j * 512, half * 1024 + (j + 1) * 512)
                    ps = slice(j * 512, (j + 1) * 512)
                    P.do("v", lambda eo, c=c, csx=csx, ps=ps: eo.tensor_tensor(
                        out=wtmp2.h[:, :], in0=xn16[c].h[:, csx], in1=PP1.h[:, ps],
                        op=ALU.mult), r=[xn16[c].b, PP1.b], w=[wtmp2.b])
                    P.do("v", lambda eo, ps=ps: eo.tensor_tensor(
                        out=wtmp2.h[:, :], in0=wtmp2.h[:, :], in1=PP2.h[:, ps],
                        op=ALU.subtract), r=[wtmp2.b, PP2.b], w=[wtmp2.b])
                    P.do("v", lambda eo, c=c, cs=cs: eo.tensor_scalar(
                        out=h1_16[c].h[:, cs], in0=wtmp2.h[:, :],
                        scalar1=lngc.h[:, c:c + 1], scalar2=lnbc.h[:, c:c + 1],
                        op0=ALU.mult, op1=ALU.add),
                        r=[wtmp2.b, lngc.b, lnbc.b], w=[h1_16[c].b])
        P.barrier()
        w2half = []
        for k in range(NK2):
            wt = wbig[k // 2]
            ap = wt.h[:, (k % 2) * 512:(k % 2) * 512 + 512]
            w2half.append((wt, ap))

        w1a2 = [w1ab, wbig[9]]
        w1g2 = [w1gb, wbig[10]]

        def load_w1(m, gate, buf, l=l):
            mm_ = m + NK2 if gate else m
            dst = (w1g2 if gate else w1a2)[buf]
            r0 = (l * NM1 + mm_) * 128
            src = w1p.h[r0:r0 + 128, :]
            if gate:
                stg = RW[buf]
                P.do("y", lambda eo, src=src, stg=stg: eo.dma_start(
                    out=stg.h[:, 0:1024], in_=src), r=[w1p.b], w=[stg.b],
                    dma=True)
                P.do("v", lambda eo, dst=dst, stg=stg: eo.tensor_copy(
                    dst.h[:, :], stg.h[:, 0:1024]), r=[stg.b], w=[dst.b])
            else:
                P.do("g", lambda eo, src=src, dst=dst: eo.dma_start(
                    out=dst.h[:, :], in_=src), r=[w1p.b], w=[dst.b], dma=True)

        w1order = [(gh_, m_) for gh_ in range(NG // 2) for m_ in range(NK2)]
        slots6 = [(PQ1.h[:, :], PQ1.b), (PQ2.h[:, :], PQ2.b),
                  (PP1.h[:, 0:512], PP1s[0]), (PP1.h[:, 512:1024], PP1s[1]),
                  (PP2.h[:, 0:512], PP2s[0]), (PP2.h[:, 512:1024], PP2s[1])]
        w1rot = [0]
        load_w1(w1order[0][1], False, 0)
        load_w1(w1order[0][1], True, 0)
        for ghalf in range(NG // 2):
            for m in range(NK2):
                i1 = ghalf * NK2 + m
                if i1 + 1 < len(w1order):
                    load_w1(w1order[i1 + 1][1], False, (i1 + 1) % 2)
                    load_w1(w1order[i1 + 1][1], True, (i1 + 1) % 2)
                wa, wg = w1a2[i1 % 2], w1g2[i1 % 2]
                for gi in range(2):
                    gidx = ghalf * 2 + gi
                    qs = slice(gidx * 512, (gidx + 1) * 512)
                    pa, pba = slots6[w1rot[0] % 6]
                    pg, pbg = slots6[(w1rot[0] + 1) % 6]
                    w1rot[0] += 2
                    for c in range(NCH):
                        P.do("t", lambda eo, c=c, qs=qs, pa=pa, wa=wa: eo.matmul(
                            pa, wa.h[:, c * 128:(c + 1) * 128],
                            h1_16[c].h[:, qs], start=(c == 0), stop=(c == NCH - 1)),
                            r=[wa.b, h1_16[c].b], w=[pba])
                    for c in range(NCH):
                        P.do("t", lambda eo, c=c, qs=qs, pg=pg, wg=wg: eo.matmul(
                            pg, wg.h[:, c * 128:(c + 1) * 128],
                            h1_16[c].h[:, qs], start=(c == 0), stop=(c == NCH - 1)),
                            r=[wg.b, h1_16[c].b], w=[pbg])
                    P.do("s", lambda eo, m=m, pa=pa: eo.activation(
                        tmp16b.h[:, :], pa, ACT.Identity,
                        bias=b1c.h[:, m:m + 1]), r=[pba, b1c.b], w=[tmp16b.b])
                    P.do("s", lambda eo, m=m, pg=pg: eo.activation(
                        gel16.h[:, :], pg, ACT.Gelu,
                        bias=b1c.h[:, NK2 + m:NK2 + m + 1]),
                        r=[pbg, b1c.b], w=[gel16.b])
                    P.do("v", lambda eo, m=m, gi=gi: eo.tensor_tensor(
                        out=h2t[m].h[:, gi * 512:(gi + 1) * 512],
                        in0=tmp16b.h[:, :], in1=gel16.h[:, :], op=ALU.mult),
                        r=[tmp16b.b, gel16.b], w=[h2t[m].b])
            for ochalf in range(2):
                for k in range(NK2):
                    wt, wap = w2half[k]
                    stg = tmp32a if k % 2 == 0 else wtmp2
                    src = w2p.h[l][k * 128:(k + 1) * 128,
                                   ochalf * 512:(ochalf + 1) * 512]
                    P.do("y", lambda eo, src=src, stg=stg: eo.dma_start(
                        out=stg.h[:, :], in_=src), r=[w2p.b], w=[stg.b],
                        dma=True)
                    P.do("v", lambda eo, wap=wap, stg=stg: eo.tensor_copy(
                        wap, stg.h[:, :]), r=[stg.b], w=[wt.b])
                for gi in range(2):
                    gidx = ghalf * 2 + gi
                    qsd = dslc(gidx * 512, (gidx + 1) * 512)
                    pslices = [(PP1.h[:, 0:512], PP1s[0]),
                               (PP1.h[:, 512:1024], PP1s[1]),
                               (PP2.h[:, 0:512], PP2s[0]),
                               (PP2.h[:, 512:1024], PP2s[1])]
                    for k in range(NK2):
                        wt, wap = w2half[k]
                        for j4 in range(4):
                            pap, pb = pslices[j4]
                            P.do("t", lambda eo, k=k, j4=j4, wap=wap, pap=pap,
                                 gi=gi: eo.matmul(
                                     pap, wap[:, j4 * 128:(j4 + 1) * 128],
                                     h2t[k].h[:, gi * 512:(gi + 1) * 512],
                                     start=(k == 0), stop=False),
                                 r=[wt.b, h2t[k].b], w=[pb])
                    for j4 in range(4):
                        oc = ochalf * 4 + j4
                        pap, pb = pslices[j4]
                        P.do("t", lambda eo, oc=oc, pap=pap: eo.matmul(
                            pap, b2row.h[0:1, oc * 128:(oc + 1) * 128],
                            onesP.h[64:65, 0:512], start=False, stop=True),
                            r=[b2row.b, onesP.b], w=[pb])
                    for j4 in range(4):
                        oc = ochalf * 4 + j4
                        pap, pb = pslices[j4]
                        P.do("v", lambda eo, oc=oc, qsd=qsd, pap=pap:
                             eo.tensor_tensor(out=xn16[oc].h[:, qsd], in0=pap,
                                              in1=xn16[oc].h[:, qsd], op=ALU.add),
                             r=[pb, xn16[oc].b], w=[xn16[oc].b])
        P.barrier()

    fin_x = [(wtmp2.h[:, :], wtmp2.b), (RW[0].h[:, 0:512], RW[0].b)]
    fin_s = [(tmp16a.h[:, :], tmp16a.b), (h2t[0].h[:, 0:512], h2t[0].b)]
    fin_u = [(tmp16b.h[:, :], tmp16b.b), (gel16.h[:, :], gel16.b)]
    fin_m = [(tmp32a.h[:, :], tmp32a.b), (RW[1].h[:, 0:512], RW[1].b)]
    fin_o = [(wtmp2.h[:, :], wtmp2.b), (RW[0].h[:, 512:1024], RW[0].b)]
    it = 0
    for c in range(NCH):
        for gidx in range(NG):
            pi = it % 2
            it += 1
            xa, xb = fin_x[pi]
            sa, sb_ = fin_s[pi]
            ua, ub = fin_u[pi]
            ma, mb = fin_m[pi]
            oa, ob = fin_o[pi]
            qsd = dslc(gidx * 512, (gidx + 1) * 512)
            qs0 = slice(gidx * 512, (gidx + 1) * 512)
            P.do("y", lambda eo, c=c, qs0=qs0, xa=xa: eo.dma_start(
                out=xa, in_=xT.h[c * 128:(c + 1) * 128, qs0]),
                r=[xT.b], w=[xb], dma=True)
            P.do("y", lambda eo, c=c, qs0=qs0, sa=sa: eo.dma_start(
                out=sa, in_=s_scr.h[c * 128:(c + 1) * 128, qs0]),
                r=[s_scr.b], w=[sb_], dma=True)
            P.do("s", lambda eo, c=c, qsd=qsd, ua=ua: eo.activation(
                ua, xn16[c].h[:, qsd], ACT.Silu),
                r=[xn16[c].b], w=[ub])
            P.do("v", lambda eo, ua=ua, sa=sa, ma=ma: eo.tensor_tensor(
                out=ma, in0=ua, in1=sa, op=ALU.mult),
                r=[ub, sb_], w=[mb])
            P.do("v", lambda eo, xa=xa, ma=ma, oa=oa: eo.tensor_tensor(
                out=oa, in0=xa, in1=ma, op=ALU.add),
                r=[xb, mb], w=[ob])
            P.do("y", lambda eo, c=c, qs0=qs0, oa=oa: eo.dma_start(
                out=out_T.h[c * 128:(c + 1) * 128, qs0], in_=oa),
                r=[ob], w=[out_T.b], dma=True)
    P.barrier()

    with nc.Block() as block:
        @block.tensor
        def _(te):
            for clo in P.q["t"]:
                clo()

        @block.vector
        def _(ve):
            for clo in P.q["v"]:
                clo()

        @block.scalar
        def _(se):
            for clo in P.q["s"]:
                clo()

        @block.sync
        def _(ye):
            for clo in P.q["y"]:
                clo()

        @block.gpsimd
        def _(ge):
            for clo in P.q["g"]:
                clo()
    es.close()
    CFG['counts'] = dict(P.cnt)
    return nc


def _host_constants(S):
    DH_ = 64
    WINDOW = 512
    half_idx = np.arange(0, DH_, 2, dtype=np.float32)
    inv_freq = 1.0 / 10000.0 ** (half_idx / DH_)
    t = np.arange(S, dtype=np.float32)
    freqs = t[:, None] * inv_freq[None, :]
    freqs = np.concatenate([freqs, freqs], axis=-1)
    base = (half_idx + 0.4 * DH_) / (1.4 * DH_)
    power = (t - S // 2) / float(WINDOW // 2)
    scale = base[None, :] ** power[:, None]
    scale = np.concatenate([scale, scale], axis=-1)
    cos, sin = np.cos(freqs), np.sin(freqs)
    sgn = np.concatenate([-np.ones(32, np.float32), np.ones(32, np.float32)])
    AQ = (cos * scale).T
    BQ = (sin * scale).T * sgn[:, None]
    AK = (cos / scale).T
    BK = (sin / scale).T * sgn[:, None]
    rotT = np.stack([np.tile(a, (2, 1)) for a in (AQ, BQ, AK, BK)]).astype(np.float16)
    import ml_dtypes as _mld
    E2 = np.zeros((128, 33), _mld.float8_e4m3fn)
    E2[0:64, 0] = 1.0
    E2[64:128, 32] = 1.0
    import ml_dtypes
    kk = np.arange(128)[:, None]
    qq = np.arange(512)[None, :]
    madd = np.zeros((128, 8 * 512), np.float32)
    for iv in range(8):
        off = iv * 128 - 512
        valid = (qq >= kk + off) & (qq <= kk + off + 512)
        madd[:, iv * 512:(iv + 1) * 512] = np.where(valid, 0.0, -28000.0)
    madd8 = madd.astype(ml_dtypes.float8_e5m2)
    return dict(rotT=rotT, E2_d=E2, madd_d=madd8)


def kernel(**inputs):
    S = CFG["S"]
    D = 1024
    HIDP = 2816
    HID = 2730
    f32 = np.float32

    x = np.ascontiguousarray(np.asarray(inputs["L0_attn_x"], f32))
    B = x.shape[0]

    def g(k):
        return np.ascontiguousarray(np.asarray(inputs[k], f32))

    w1 = g("L0_ffn_w1")
    b1 = g("L0_ffn_b1")
    w2 = g("L0_ffn_w2")
    w1p = np.zeros((2, D, 2 * HIDP), f32)
    w1p[:, :, 0:HID] = w1[:, :, 0:HID]
    w1p[:, :, HIDP:HIDP + HID] = w1[:, :, HID:2 * HID]
    b1p = np.zeros((2, 2 * HIDP), f32)
    b1p[:, 0:HID] = b1[:, 0:HID]
    b1p[:, HIDP:HIDP + HID] = b1[:, HID:2 * HID]
    w2p = np.zeros((2, HIDP, D), f32)
    w2p[:, 0:HID, :] = w2

    def cols(a):
        return np.ascontiguousarray(a.reshape(-1, 128).T)

    def ptiles(a, tw):
        K, M = a.shape
        nch, nt = K // 128, M // tw
        return np.ascontiguousarray(a.reshape(nch, 128, nt, tw)
                                    .transpose(2, 1, 0, 3)
                                    .reshape(nt, 128, nch * tw))

    shared = dict(
        Wq=ptiles(g("L0_attn_Wq"), 128).reshape(-1, 1024),
        Wk=ptiles(g("L0_attn_Wk"), 128).reshape(-1, 1024),
        Wv=ptiles(g("L0_attn_Wv"), 128).reshape(-1, 1024),
        Wo=ptiles(g("L0_attn_Wo"), 128).reshape(-1, 1024),
        Wg=ptiles(g("L0_attn_Wg"), 128).reshape(-1, 1024),
        bqk_c=np.stack([cols(g("L0_attn_bq")), cols(g("L0_attn_bk"))]),
        bg_c=cols(g("L0_attn_bg")),
        bo_r=np.ascontiguousarray(g("L0_attn_bo").reshape(1, D)),
        bv_r=np.ascontiguousarray(g("L0_attn_bv").reshape(1, D)),
        Wt_r=cols(g("L0_attn_Wt").reshape(-1)),
        bt_s=np.ascontiguousarray(g("L0_attn_bt").reshape(1, 1)),
        wqkv_qk=np.concatenate([ptiles(wq_[:, 0:1024], 128)
                                for wq_ in g("L0_ffn_wqkv")]).reshape(-1, 1024),
        wqkv_v=np.concatenate([ptiles(wq_[:, 1024:1536], 64)
                               for wq_ in g("L0_ffn_wqkv")]).reshape(-1, 512),
        fwo=np.concatenate([ptiles(f_, 128)
                            for f_ in g("L0_ffn_wo")]).reshape(-1, 512),
        w1p=np.concatenate([ptiles(w1p[i], 128)
                            for i in range(2)]).reshape(-1, 1024),
        b1p_c=np.stack([cols(b1p[i]) for i in range(2)]),
        w2p=w2p,
        b2_r=np.ascontiguousarray(g("L0_ffn_b2").reshape(2, 1, D)),
        lng_c=np.stack([cols(g("L0_ffn_ln_g")[i]) for i in range(2)]),
        lnb_c=np.stack([cols(g("L0_ffn_ln_b")[i]) for i in range(2)]),
        qsc_c=np.ascontiguousarray(
            np.tile(g("L0_ffn_q_scale"), (1, 2)).reshape(2, 128, 1)),
        ksc_c=np.ascontiguousarray(
            np.tile(g("L0_ffn_k_scale"), (1, 2)).reshape(2, 128, 1)),
        **_host_constants(S),
    )
    nc = build()
    in_maps = []
    for b in range(B):
        m = dict(shared)
        m["xT"] = np.ascontiguousarray(x[b].T)
        in_maps.append(m)
    res = run_bass_kernel_spmd(nc, in_maps, core_ids=list(range(B)))
    out = np.stack([np.asarray(r["out_T"]).T for r in res.results])
    return np.ascontiguousarray(out.astype(np.float32))


# version 23 (final)
# speedup vs baseline: 1.0332x; 1.0056x over previous
import numpy as np
from contextlib import ExitStack

import concourse.bass as bass
import concourse.mybir as mybir
from concourse.bass_utils import run_bass_kernel_spmd

F32 = mybir.dt.float32
F16 = mybir.dt.float16
ALU = mybir.AluOpType
ACT = mybir.ActivationFunctionType
AX = mybir.AxisListType

CFG = dict(S=2048, NPROBE=4)


def _config(S=2048, NPROBE=4):
    CFG["S"] = S
    CFG["NPROBE"] = NPROBE


class Buf:
    __slots__ = ("w", "r")

    def __init__(self):
        self.w = None
        self.r = {}


class Tile:
    __slots__ = ("h", "b")

    def __init__(self, h):
        self.h = h
        self.b = Buf()


class Prog:
    ENGS = ("t", "v", "s", "y", "g")
    KDMA = 4

    def __init__(self, nc, es):
        self.nc = nc
        self.semids = ["t", "v", "s"] + [f"y{i}" for i in range(self.KDMA)] \
            + [f"g{i}" for i in range(self.KDMA)]
        self.sem = {sid: es.enter_context(nc.semaphore("sem_" + sid))
                    for sid in self.semids}
        self.cnt = {sid: 0 for sid in self.semids}
        self.dman = {"y": 0, "g": 0}
        self.seen = {e: {sid: 0 for sid in self.semids} for e in self.ENGS}
        self.q = {e: [] for e in self.ENGS}
        self.engobj = dict(t=nc.tensor, v=nc.vector, s=nc.scalar, y=nc.sync,
                           g=nc.gpsimd)

    def do(self, e, fn, r=(), w=(), dma=False):
        need = {}

        def add(src):
            if src is None:
                return
            f, c = src
            if e == "t" and f == "t":
                return
            if c > need.get(f, 0):
                need[f] = c

        for b in r:
            add(b.w)
        for b in w:
            add(b.w)
            for f, c in b.r.items():
                add((f, c))
        if dma:
            n = self.dman[e]
            sid = f"{e}{n % self.KDMA}"
            self.dman[e] = n + 1
            if 16 * (n // self.KDMA) > need.get(sid, 0):
                need[sid] = 16 * (n // self.KDMA)
            inc = 16
        else:
            sid = e
            inc = 1
        waits = []
        for f, c in need.items():
            if self.seen[e][f] < c:
                self.seen[e][f] = c
                waits.append((f, c))
        self.cnt[sid] += inc
        mycnt = self.cnt[sid]

        def clo(eo=self.engobj[e], waits=tuple(waits), fn=fn, sem=self.sem[sid],
                inc=inc, sems=self.sem):
            for f, c in waits:
                eo.wait_ge(sems[f], c)
            fn(eo).then_inc(sem, inc)

        self.q[e].append(clo)
        src = (sid, mycnt)
        for b in r:
            if b.r.get(sid, 0) < mycnt:
                b.r[sid] = mycnt
        for b in w:
            b.w = src
            b.r = {}
        return src

    def raw(self, e, fn):
        self.q[e].append(lambda eo=self.engobj[e], fn=fn: fn(eo))

    def barrier(self):
        snap = dict(self.cnt)
        sems = self.sem
        for e in self.ENGS:
            waits = []
            for sid in self.semids:
                if sid != e and self.seen[e][sid] < snap[sid]:
                    self.seen[e][sid] = snap[sid]
                    waits.append((sid, snap[sid]))

            def clo(eo=self.engobj[e], waits=tuple(waits)):
                for f, c in waits:
                    eo.wait_ge(sems[f], c)

            self.q[e].append(clo)


def build():
    S = CFG["S"]
    NPROBE = CFG["NPROBE"]
    D = 1024
    NCH = 8
    H = 16
    DH = 64
    KK = S // 4
    NQT = S // 128
    NG = S // 512
    QTH = NQT // 2
    HF = 8
    INNER = 512
    NW = S // 512
    HIDP = 2816
    NM1 = 2 * HIDP // 128
    NK2 = HIDP // 128
    SD = 2
    HLV = max(1, S // 1024)

    nc = bass.Bass("TRN2", target_bir_lowering=False)
    es = ExitStack()
    P = Prog(nc, es)

    def dram(name, shape, dt, kind):
        return Tile(nc.dram_tensor(name, shape, dt, kind=kind).ap())

    def sb(name, shape, dt):
        return Tile(es.enter_context(nc.sbuf_tensor(name, shape, dt)))

    def psum(name, shape, dt=F32):
        return Tile(es.enter_context(nc.psum_tensor(name, shape, dt)))

    xT = dram("xT", [D, S], F32, "ExternalInput")
    Wq = dram("Wq", [NCH * 128, D], F32, "ExternalInput")
    Wk = dram("Wk", [NCH * 128, D], F32, "ExternalInput")
    Wv = dram("Wv", [NCH * 128, D], F32, "ExternalInput")
    Wo = dram("Wo", [NCH * 128, D], F32, "ExternalInput")
    Wg = dram("Wg", [NCH * 128, D], F32, "ExternalInput")
    bqk_c = dram("bqk_c", [2, 128, NCH], F32, "ExternalInput")
    bg_c = dram("bg_c", [128, NCH], F32, "ExternalInput")
    bo_r = dram("bo_r", [1, D], F32, "ExternalInput")
    bv_r = dram("bv_r", [1, D], F32, "ExternalInput")
    Wt_r = dram("Wt_r", [128, NCH], F32, "ExternalInput")
    bt_s = dram("bt_s", [1, 1], F32, "ExternalInput")
    wqkv_qk = dram("wqkv_qk", [2 * 8 * 128, 1024], F32, "ExternalInput")
    wqkv_v = dram("wqkv_v", [2 * HF * 128, 512], F32, "ExternalInput")
    fwo = dram("fwo", [2 * 8 * 128, 512], F32, "ExternalInput")
    w1p = dram("w1p", [2 * NM1 * 128, 1024], F32, "ExternalInput")
    b1p_c = dram("b1p_c", [2, 128, NM1], F32, "ExternalInput")
    w2p = dram("w2p", [2, HIDP, D], F32, "ExternalInput")
    b2_r = dram("b2_r", [2, 1, D], F32, "ExternalInput")
    lng_c = dram("lng_c", [2, 128, NCH], F32, "ExternalInput")
    lnb_c = dram("lnb_c", [2, 128, NCH], F32, "ExternalInput")
    qsc_c = dram("qsc_c", [2, 128, 1], F32, "ExternalInput")
    ksc_c = dram("ksc_c", [2, 128, 1], F32, "ExternalInput")
    rotT = dram("rotT", [4, 128, S], F16, "ExternalInput")
    E2_d = dram("E2_d", [128, 33], mybir.dt.float8e4, "ExternalInput")
    madd_d = dram("madd_d", [128, 8 * 512], mybir.dt.float8e5, "ExternalInput")

    out_T = dram("out_T", [D, S], F32, "ExternalOutput")
    s_scr = dram("s_scr", [D, S], F16, "Internal")
    sp_scr = dram("sp_scr", [D, S], F16, "Internal")

    F8 = mybir.dt.float8e4

    def view(pack, p0, p1):
        t = Tile.__new__(Tile)
        t.h = pack.h[p0:p1, :]
        t.b = pack.b
        return t

    def cview(pack, c0, w):
        t = Tile.__new__(Tile)
        t.h = pack.h[:, c0:c0 + w]
        t.b = Buf()
        return t

    xn16 = [sb(f"xn16_{c}", [128, S + SD], F16) for c in range(NCH)]
    P8 = [sb(f"P8_{i}", [128, S + SD], F16) for i in range(8)]
    q16p = sb("q16p", [128, S + SD], F16)
    k16p = sb("k16p", [128, S + SD], F16)
    q16pB = sb("q16pB", [128, S + SD], F16)
    k16pB = sb("k16pB", [128, S + SD], F16)
    sq16 = sb("sq16", [128, S + SD], F16)
    dump8 = sb("dump8", [128, S], F8)
    s16 = P8
    h2t = [sb(f"h2t_{k}", [128, 1024], F16) for k in range(NK2)]
    wbig = [sb(f"wbig_{i}", [128, D], F16) for i in range(11)]
    wq16 = wbig[0:2]
    wk16 = wbig[2:4]
    wv16 = wbig[4:6]
    wo16 = wbig[6:8]
    wg16 = wbig[8:10]
    wqkv16 = [wbig[10], wbig[0]]
    w1ab = sb("w1ab", [128, D], F16)
    w1gb = sb("w1gb", [128, D], F16)
    fwo16 = [sb(f"fwo16_{i}", [128, INNER], F16) for i in range(2)]
    rot2 = [sb(f"rot2_{i}", [128, 512], F16) for i in range(2)]
    yT = [sb(f"yT_{k}", [128, 512], F16) for k in range(3)]
    vhat = [[sb(f"vh_{hh}_{t}", [128, DH + 1], F16) for t in range(NQT)]
            for hh in range(2)]
    E2 = sb("E2", [128, 33], mybir.dt.float8e4)
    madd = sb("madd", [128, 8 * 512], mybir.dt.float8e5)
    onesP = sb("onesP", [128, 512], F16)
    ones_c = sb("ones_c", [128, 1], F16)
    epsc = sb("epsc", [128, 1], F32)
    clampc = sb("clampc", [128, 1], F32)
    ln8c = sb("ln8c", [128, 1], F32)
    bias_q = sb("bias_q", [128, NCH], F32)
    bias_k = sb("bias_k", [128, NCH], F32)
    bias_g = sb("bias_g", [128, NCH], F32)
    b1c = sb("b1c", [128, NM1], F32)
    lngc = sb("lngc", [128, NCH], F32)
    lnbc = sb("lnbc", [128, NCH], F32)
    qscol = sb("qscol", [128, 1], F32)
    kscol = sb("kscol", [128, 1], F32)
    RW = [sb(f"RW_{i}", [128, S], F32) for i in range(2)]
    rrowA = view(RW[0], 0, 33)
    srow = view(RW[0], 0, 1)
    s2row = view(RW[1], 0, 1)
    murow = srow
    rstdrow = s2row
    R16 = [sb(f"R16_{i}", [128, S], F16) for i in range(2)]
    rrow16A = view(R16[0], 0, 33)
    rrow16 = [view(R16[0], 0, 1), view(R16[0], 32, 33)]
    murow16 = view(R16[0], 0, 1)
    rstdrow16 = view(R16[1], 0, 1)
    throw = view(R16[1], 64, 65)

    brows = sb("brows", [128, D], F16)
    borow = view(brows, 0, 1)
    bvrow = view(brows, 32, 33)
    b2row = view(brows, 64, 65)
    thb = sb("thb", [128, 512], F16)
    tcol16 = cview(thb, 0, 1)
    rd16 = view(thb, 64, 65)
    th16t = cview(thb, 8, QTH)
    wtmp2 = sb("wtmp2", [128, 512], F32)
    tmp32a = sb("tmp32a", [128, 512], F32)
    rd32 = view(tmp32a, 64, 65)
    tmp16a = sb("tmp16a", [128, 512], F16)
    tmp16b = sb("tmp16b", [128, 512], F16)
    gel16 = thb
    rdb16 = view(tmp16b, 64, 128)
    itemp = sb("itemp", [1, 1], F32)
    lnit = sb("lnit", [128, 1], F32)
    scal1 = cview(wtmp2, 0, 1)
    scal16 = cview(thb, 16, 1)
    tau0c = sb("tau0c", [128, 1], F32)
    irho0c = sb("irho0c", [128, 1], F32)
    _stnames = ("tauA", "tauB", "cA", "cB", "lo", "hi", "bt", "bd", "d1",
                "t1", "t2", "t3", "tnA", "tnB")
    _stpack = sb("stpack", [128, 14 * QTH], F32)

    st = {n: cview(_stpack, i * QTH, QTH) for i, n in enumerate(_stnames)}
    stc = {n: (cview(_stpack, _stnames.index(n) * QTH, QTH // 2),
               cview(_stpack, _stnames.index(n) * QTH + QTH // 2, QTH - QTH // 2))
           for n in ("cA", "cB")}
    _mpack = sb("mpack", [128, 2 * QTH], mybir.dt.uint8)
    msk = cview(_mpack, 0, QTH)
    mskn = cview(_mpack, QTH, QTH)
    meanT = cview(_stpack, 0, NCH)
    wtT = cview(_stpack, NCH, NCH)
    tcol = cview(_stpack, 2 * NCH, 1)

    PP1 = psum("PP1", [128, 1024])
    PP2 = psum("PP2", [128, 1024])
    PQ1 = psum("PQ1", [128, 512])
    PQ2 = psum("PQ2", [128, 512])
    PQ3 = psum("PQ3", [128, 512])
    PP1s = [Buf(), Buf()]
    PP2s = [Buf(), Buf()]

    DATA = slice(SD, SD + S)
    SH = slice(SD - 1, SD - 1 + S)

    def dslc(a, b2):
        return slice(SD + a, SD + b2)

    x16 = s16
    for c in range(NCH):
        P.do("g", lambda eo, c=c: eo.dma_start(
            out=x16[c].h[:, DATA], in_=xT.h[c * 128:(c + 1) * 128, :]),
            r=[xT.b], w=[x16[c].b], dma=True)
    for src_d, dst in ((E2_d, E2), (madd_d, madd), (bg_c, bias_g), (Wt_r, wtT)):
        P.do("y", lambda eo, src_d=src_d, dst=dst: eo.dma_start(
            out=dst.h[:, :], in_=src_d.h), r=[src_d.b], w=[dst.b], dma=True)
    for i, dst in ((0, bias_q), (1, bias_k)):
        P.do("y", lambda eo, i=i, dst=dst: eo.dma_start(
            out=dst.h[:, :], in_=bqk_c.h[i]), r=[bqk_c.b], w=[dst.b], dma=True)
    for src_d, dst in ((bo_r, borow), (bv_r, bvrow)):
        P.do("g", lambda eo, src_d=src_d, dst=dst: eo.dma_start(
            out=dst.h[:, :], in_=src_d.h), r=[src_d.b], w=[dst.b], dma=True)
    P.do("v", lambda eo: eo.memset(ones_c.h[:, :], 1.0), w=[ones_c.b])
    P.do("v", lambda eo: eo.memset(onesP.h[:, :], 1.0), w=[onesP.b])
    P.do("v", lambda eo: eo.memset(epsc.h[:, :], 1e-5), w=[epsc.b])
    P.do("v", lambda eo: eo.memset(clampc.h[:, :], 1e-18), w=[clampc.b])
    P.do("v", lambda eo: eo.memset(ln8c.h[:, :], float(np.log(8.0))), w=[ln8c.b])
    P.do("v", lambda eo: eo.memset(st["t1"].h[:, :], 0.0), w=[st["t1"].b])
    for hh in range(2):
        for t in range(NQT):
            P.do("v", lambda eo, hh=hh, t=t: eo.memset(
                vhat[hh][t].h[:, DH:DH + 1], 1.0), w=[vhat[hh][t].b])
    for c in range(NCH):
        P.do("v", lambda eo, c=c: eo.memset(xn16[c].h[:, 0:SD], 0.0), w=[xn16[c].b])

    def ln_stats(slices):
        for piece in range(NG):
            cols = [slice(a.start + piece * 512, a.start + (piece + 1) * 512)
                    for _, a in slices]
            for c in range(NCH):
                t, _ = slices[c]
                P.do("t", lambda eo, t=t, c=c, cc=cols[c]: eo.matmul(
                    PQ1.h[0:1, :], ones_c.h[:, :], t.h[:, cc],
                    start=(c == 0), stop=(c == NCH - 1)), r=[t.b, ones_c.b], w=[PQ1.b])
            P.do("s", lambda eo, piece=piece: eo.activation(
                srow.h[0:1, piece * 512:(piece + 1) * 512], PQ1.h[0:1, :],
                ACT.Identity), r=[PQ1.b], w=[srow.b])
            for c in range(NCH):
                t, _ = slices[c]
                P.do("v", lambda eo, t=t, cc=cols[c]: eo.tensor_tensor(
                    out=sq16.h[:, 0:512], in0=t.h[:, cc], in1=t.h[:, cc],
                    op=ALU.mult), r=[t.b], w=[sq16.b])
                P.do("t", lambda eo, c=c: eo.matmul(
                    PQ2.h[0:1, :], ones_c.h[:, :], sq16.h[:, 0:512],
                    start=(c == 0), stop=(c == NCH - 1)),
                    r=[sq16.b, ones_c.b], w=[PQ2.b])
            P.do("s", lambda eo, piece=piece: eo.activation(
                s2row.h[0:1, piece * 512:(piece + 1) * 512], PQ2.h[0:1, :],
                ACT.Identity), r=[PQ2.b], w=[s2row.b])

    def ln_rows():
        P.do("v", lambda eo: eo.tensor_scalar(out=murow.h[0:1, :], in0=srow.h[0:1, :],
             scalar1=1.0 / D, scalar2=None, op0=ALU.mult), r=[srow.b], w=[murow.b])
        for jh in range(S // 1024):
            hs = slice(jh * 1024, (jh + 1) * 1024)
            P.do("v", lambda eo, hs=hs: eo.tensor_tensor(
                out=PP1.h[0:1, :], in0=murow.h[0:1, hs],
                in1=murow.h[0:1, hs], op=ALU.mult), r=[murow.b], w=[PP1.b])
            P.do("v", lambda eo, hs=hs: eo.tensor_scalar(
                out=rstdrow.h[0:1, hs], in0=s2row.h[0:1, hs],
                scalar1=1.0 / D, scalar2=None, op0=ALU.mult),
                r=[s2row.b], w=[rstdrow.b])
            P.do("v", lambda eo, hs=hs: eo.tensor_tensor(
                out=rstdrow.h[0:1, hs], in0=rstdrow.h[0:1, hs],
                in1=PP1.h[0:1, :], op=ALU.subtract),
                r=[PP1.b, rstdrow.b], w=[rstdrow.b])
        P.do("s", lambda eo: eo.activation(rstdrow.h[0:1, :], rstdrow.h[0:1, :],
             ACT.Ln, bias=epsc.h[0:1, :]), r=[rstdrow.b, epsc.b], w=[rstdrow.b])
        P.do("s", lambda eo: eo.activation(rstdrow.h[0:1, :], rstdrow.h[0:1, :],
             ACT.Exp, scale=-0.5), r=[rstdrow.b], w=[rstdrow.b])
        P.do("v", lambda eo: eo.tensor_tensor(out=murow16.h[0:1, :],
             in0=murow.h[0:1, :], in1=rstdrow.h[0:1, :], op=ALU.mult),
             r=[murow.b, rstdrow.b], w=[murow16.b])
        P.do("v", lambda eo: eo.tensor_copy(rstdrow16.h[0:1, :], rstdrow.h[0:1, :]),
             r=[rstdrow.b], w=[rstdrow16.b])

    def bcast_to(pp, row16, half):
        c0 = half * 1024
        for j in range(2):
            P.do("t", lambda eo, j=j, c0=c0, pp=pp, row16=row16: eo.matmul(
                pp.h[:, j * 512:(j + 1) * 512], onesP.h[0:1, 0:128],
                row16.h[0:1, c0 + j * 512:c0 + (j + 1) * 512], start=True, stop=True),
                r=[onesP.b, row16.b], w=[pp.b])

    def ssq_rows(sqtile):
        for piece in range(NG):
            PQ = PQ2 if piece % 2 == 0 else PQ3
            P.do("t", lambda eo, piece=piece, PQ=PQ, sqtile=sqtile:
                 eo.matmul(PQ.h[0:33, :], E2.h[:, :],
                           sqtile.h[:, piece * 512:(piece + 1) * 512],
                           start=True, stop=True), r=[sqtile.b, E2.b], w=[PQ.b])
            eng = "s" if piece % 2 == 0 else "v"
            if eng == "s":
                P.do("s", lambda eo, piece=piece, PQ=PQ: eo.activation(
                    rrowA.h[:, piece * 512:(piece + 1) * 512], PQ.h[0:33, :],
                    ACT.Identity), r=[PQ.b], w=[rrowA.b])
            else:
                P.do("v", lambda eo, piece=piece, PQ=PQ: eo.tensor_copy(
                    rrowA.h[:, piece * 512:(piece + 1) * 512], PQ.h[0:33, :]),
                    r=[PQ.b], w=[rrowA.b])

    def finish_rrows(with_itemp):
        P.do("v", lambda eo: eo.tensor_scalar(
            out=rrowA.h[:, :], in0=rrowA.h[:, :], scalar1=clampc.h[0:33, :],
            scalar2=None, op0=ALU.max), r=[rrowA.b, clampc.b], w=[rrowA.b])
        P.do("s", lambda eo: eo.activation(rrowA.h[:, :], rrowA.h[:, :], ACT.Ln),
             r=[rrowA.b], w=[rrowA.b])
        if with_itemp:
            P.do("s", lambda eo: eo.activation(rrow16A.h[:, :], rrowA.h[:, :],
                 ACT.Exp, scale=-0.5, bias=lnit.h[0:33, :]),
                 r=[rrowA.b, lnit.b], w=[rrow16A.b])
        else:
            P.do("s", lambda eo: eo.activation(rrow16A.h[:, :], rrowA.h[:, :],
                 ACT.Exp, scale=-0.5), r=[rrowA.b], w=[rrow16A.b])

    def bcast_row16(pp, hh, half):
        c0 = half * 1024
        b0 = hh * 32
        for j in range(2):
            P.do("t", lambda eo, j=j, c0=c0, pp=pp, b0=b0, hh=hh: eo.matmul(
                pp.h[:, j * 512:(j + 1) * 512], onesP.h[b0:b0 + 1, 0:128],
                rrow16[hh].h[0:1, c0 + j * 512:c0 + (j + 1) * 512],
                start=True, stop=True), r=[onesP.b, rrow16A.b], w=[pp.b])

    def av_divide(dst_fn):
        P.do("s", lambda eo: eo.activation(rd32.h[0:1, :], PQ3.h[DH:DH + 1, :],
             ACT.Ln), r=[PQ3.b], w=[rd32.b])
        P.do("s", lambda eo: eo.activation(rd16.h[0:1, :], rd32.h[0:1, :],
             ACT.Exp, scale=-1.0), r=[rd32.b], w=[rd16.b])
        P.do("t", lambda eo: eo.matmul(PQ1.h[0:64, :], onesP.h[64:65, 0:64],
             rd16.h[0:1, :], start=True, stop=True), r=[onesP.b, rd16.b], w=[PQ1.b])
        P.do("s", lambda eo: eo.activation(rdb16.h[:, :], PQ1.h[0:64, :],
             ACT.Identity), r=[PQ1.b], w=[rdb16.b])
        P.do("v", lambda eo: eo.tensor_tensor(out=tmp16a.h[0:64, :],
             in0=PQ3.h[0:DH, :], in1=rdb16.h[:, :], op=ALU.mult),
             r=[PQ3.b, rdb16.b], w=[tmp16a.b])
        dst_fn()

    ln_stats([(x16[c], DATA) for c in range(NCH)])
    ln_rows()
    for half in range(HLV):
        bcast_to(PP1, rstdrow16, half)
        bcast_to(PP2, murow16, half)
        for c in range(NCH):
            for j in range(2):
                cs = dslc(half * 1024 + j * 512, half * 1024 + (j + 1) * 512)
                ps = slice(j * 512, (j + 1) * 512)
                P.do("v", lambda eo, c=c, cs=cs, ps=ps: eo.tensor_tensor(
                    out=wtmp2.h[:, :], in0=x16[c].h[:, cs], in1=PP1.h[:, ps],
                    op=ALU.mult), r=[x16[c].b, PP1.b], w=[wtmp2.b])
                P.do("v", lambda eo, c=c, cs=cs, ps=ps: eo.tensor_tensor(
                    out=xn16[c].h[:, cs], in0=wtmp2.h[:, :], in1=PP2.h[:, ps],
                    op=ALU.subtract), r=[wtmp2.b, PP2.b], w=[xn16[c].b])
    P.barrier()

    for c in range(NCH):
        P.do("s", lambda eo, c=c: eo.activation(
            sq16.h[:, 0:S], xn16[c].h[:, DATA], ACT.Identity,
            accum_out=meanT.h[:, c:c + 1]), r=[xn16[c].b], w=[sq16.b, meanT.b])
    P.do("v", lambda eo: eo.tensor_tensor(out=meanT.h[:, :], in0=meanT.h[:, :],
         in1=wtT.h[:, :], op=ALU.mult), r=[meanT.b, wtT.b], w=[meanT.b])
    P.do("v", lambda eo: eo.tensor_reduce(out=tcol.h[:, :], in_=meanT.h[:, :],
         axis=AX.X, op=ALU.add), r=[meanT.b], w=[tcol.b])
    P.do("v", lambda eo: eo.tensor_scalar(out=tcol16.h[:, :], in0=tcol.h[:, :],
         scalar1=1.0 / S, scalar2=None, op0=ALU.mult), r=[tcol.b], w=[tcol16.b])
    P.do("t", lambda eo: eo.matmul(PQ1.h[0:1, 0:1], tcol16.h[:, :], ones_c.h[:, :],
         start=True, stop=True), r=[tcol16.b, ones_c.b], w=[PQ1.b])
    P.do("y", lambda eo: eo.dma_start(out=scal1.h[0:1, :], in_=bt_s.h),
         r=[bt_s.b], w=[scal1.b], dma=True)
    P.do("s", lambda eo: eo.activation(itemp.h[0:1, :], PQ1.h[0:1, 0:1], ACT.Sigmoid,
         bias=scal1.h[0:1, :]), r=[PQ1.b, scal1.b], w=[itemp.b])
    P.do("v", lambda eo: eo.tensor_scalar(out=itemp.h[0:1, :], in0=itemp.h[0:1, :],
         scalar1=0.5, scalar2=None, op0=ALU.add), r=[itemp.b], w=[itemp.b])
    P.do("v", lambda eo: eo.reciprocal(itemp.h[0:1, :], itemp.h[0:1, :]),
         r=[itemp.b], w=[itemp.b])
    P.do("s", lambda eo: eo.activation(scal1.h[0:1, :], itemp.h[0:1, :], ACT.Ln),
         r=[itemp.b], w=[scal1.b])
    P.do("v", lambda eo: eo.tensor_copy(scal16.h[0:1, :], scal1.h[0:1, :]),
         r=[scal1.b], w=[scal16.b])
    P.do("t", lambda eo: eo.matmul(PQ1.h[:, 0:1], onesP.h[0:1, 0:128],
         scal16.h[0:1, :], start=True, stop=True), r=[onesP.b, scal16.b], w=[PQ1.b])
    P.do("v", lambda eo: eo.tensor_copy(lnit.h[:, :], PQ1.h[:, 0:1]),
         r=[PQ1.b], w=[lnit.b])
    for mul, dst in ((0.6744898 * 0.125, tau0c), (1.0 / 5207.5, irho0c)):
        P.do("v", lambda eo, mul=mul: eo.tensor_scalar(out=scal1.h[0:1, :],
             in0=itemp.h[0:1, :], scalar1=mul, scalar2=None, op0=ALU.mult),
             r=[itemp.b], w=[scal1.b])
        P.do("v", lambda eo: eo.tensor_copy(scal16.h[0:1, :], scal1.h[0:1, :]),
             r=[scal1.b], w=[scal16.b])
        P.do("t", lambda eo: eo.matmul(PQ1.h[:, 0:1], onesP.h[0:1, 0:128],
             scal16.h[0:1, :], start=True, stop=True),
             r=[onesP.b, scal16.b], w=[PQ1.b])
        P.do("v", lambda eo, dst=dst: eo.tensor_copy(dst.h[:, :], PQ1.h[:, 0:1]),
             r=[PQ1.b], w=[dst.b])
    P.barrier()

    def load_cols(Wd, dst, col0, ncol=128):
        t_ = col0 // 128
        src = Wd.h[t_ * 128:(t_ + 1) * 128, :]
        P.do("g", lambda eo, src=src, dst=dst: eo.dma_start(
            out=dst.h[:, 0:(ncol * NCH)], in_=src), r=[Wd.b], w=[dst.b], dma=True)

    def load_wpair(p, slot):
        load_cols(Wq, wq16[slot], p * 128)
        load_cols(Wk, wk16[slot], p * 128)
        load_cols(Wv, wv16[slot], p * 128)

    def vT_make(wvt, hh, bias_row, bias_off, src_tiles, src_off):
        for t in range(NQT):
            PQ = PQ1 if t % 2 == 0 else PQ2
            for c in range(NCH):
                tt, off = src_tiles[c], src_off[c]
                P.do("t", lambda eo, t=t, c=c, PQ=PQ, tt=tt, off=off, wvt=wvt, hh=hh:
                     eo.matmul(PQ.h[:, 0:64],
                               tt.h[:, off + t * 128:off + (t + 1) * 128],
                               wvt.h[:, c * 128 + hh * 64:c * 128 + hh * 64 + 64],
                               start=(c == 0), stop=False),
                     r=[tt.b, wvt.b], w=[PQ.b])
            P.do("t", lambda eo, t=t, PQ=PQ, bias_row=bias_row, bias_off=bias_off,
                 hh=hh: eo.matmul(
                     PQ.h[:, 0:64], onesP.h[32:33, 0:128],
                     bias_row.h[0:1, bias_off + hh * 64:bias_off + hh * 64 + 64],
                     start=False, stop=True),
                 r=[onesP.b, bias_row.b], w=[PQ.b])
            eng = "v" if t % 2 == 0 else "s"
            if eng == "v":
                P.do("v", lambda eo, hh=hh, t=t, PQ=PQ: eo.tensor_copy(
                    vhat[hh][t].h[:, 0:DH], PQ.h[:, 0:64]),
                    r=[PQ.b], w=[vhat[hh][t].b])
            else:
                P.do("s", lambda eo, hh=hh, t=t, PQ=PQ: eo.activation(
                    vhat[hh][t].h[:, 0:DH], PQ.h[:, 0:64],
                    ACT.Identity), r=[PQ.b], w=[vhat[hh][t].b])

    pending = [None]

    def emit_av(ctx):
        if ctx is None:
            return
        hh, hb, hd, halfb, qt, kt_ = ctx
        for gi in range(NG // 2):
            gidx = halfb * (NG // 2) + gi
            qs = dslc(gidx * 512, (gidx + 1) * 512)
            P.do("t", lambda eo, gidx=gidx: eo.matmul(PQ2.h[:, :],
                 onesP.h[64:65, 0:128],
                 throw.h[0:1, gidx * 512:(gidx + 1) * 512],
                 start=True, stop=True), r=[onesP.b, throw.b], w=[PQ2.b])
            P.do("s", lambda eo: eo.activation(thb.h[:, :], PQ2.h[:, :],
                 ACT.Identity), r=[PQ2.b], w=[thb.b])
            def qkb(kc, qs=qs, hb=hb):
                PQ = PQ1 if kc % 2 == 0 else PQ2
                P.do("t", lambda eo, kc=kc, qs=qs, PQ=PQ, hb=hb: eo.matmul(
                    PQ.h[:, :],
                    kt_.h[hb:hb + 64, SD + kc * 128:SD + (kc + 1) * 128],
                    qt.h[hb:hb + 64, qs], start=True, stop=True),
                    r=[kt_.b, qt.b], w=[PQ.b])
            qkb(0)
            for kc in range(NQT):
                PQ = PQ1 if kc % 2 == 0 else PQ2
                yt = yT[kc % 3]
                if kc + 1 < NQT:
                    qkb(kc + 1)
                P.do("s", lambda eo, PQ=PQ, yt=yt: eo.activation(
                    yt.h[:, :], PQ.h[:, :], ACT.Exp), r=[PQ.b], w=[yt.b])
                P.do("v", lambda eo, yt=yt: eo.tensor_tensor(
                    out=tmp16a.h[:, :], in0=yt.h[:, :], in1=thb.h[:, :],
                    op=ALU.is_gt), r=[yt.b, thb.b], w=[tmp16a.b])
                P.do("v", lambda eo, yt=yt: eo.tensor_tensor(
                    out=yt.h[:, :], in0=yt.h[:, :], in1=tmp16a.h[:, :],
                    op=ALU.mult), r=[yt.b, tmp16a.b], w=[yt.b])
                P.do("t", lambda eo, kc=kc, hh=hh, yt=yt: eo.matmul(
                    PQ3.h[0:DH + 1, :], vhat[hh][kc].h[:, :], yt.h[:, :],
                    start=(kc == 0), stop=(kc == NQT - 1)),
                    r=[vhat[hh][kc].b, yt.b], w=[PQ3.b])

            def store(hd=hd, gidx=gidx):
                P.do("y", lambda eo: eo.dma_start(
                    out=sp_scr.h[hd * 64:(hd + 1) * 64,
                                 gidx * 512:(gidx + 1) * 512],
                    in_=tmp16a.h[0:64, :]), r=[tmp16a.b], w=[sp_scr.b],
                    dma=True)
            av_divide(store)


    load_wpair(0, 0)
    for p in range(H // 2):
        slot = p % 2
        if p + 1 < H // 2:
            load_wpair(p + 1, 1 - slot)
        emit_av(pending[0])
        pending[0] = None
        qt = q16p if p % 2 == 0 else q16pB
        kt_ = k16p if p % 2 == 0 else k16pB
        for wtl, bias_t, dst16, is_q in ((wq16, bias_q, qt, True),
                                         (wk16, bias_k, kt_, False)):
            wt16 = wtl[slot]
            for half in range(HLV):
                PP = PP1 if half == 0 else PP2
                for c in range(NCH):
                    for j in range(2):
                        P.do("t", lambda eo, c=c, j=j, half=half, wt16=wt16, PP=PP:
                             eo.matmul(PP.h[:, j * 512:(j + 1) * 512],
                                       wt16.h[:, c * 128:(c + 1) * 128],
                                       xn16[c].h[:, SD + half * 1024 + j * 512:
                                                 SD + half * 1024 + (j + 1) * 512],
                                       start=(c == 0), stop=(c == NCH - 1)),
                             r=[wt16.b, xn16[c].b], w=[PP.b])
                cs = dslc(half * 1024, (half + 1) * 1024)
                P.do("s", lambda eo, PP=PP, cs=cs, bias_t=bias_t, dst16=dst16, p=p:
                     eo.activation(dst16.h[:, cs], PP.h[:, :], ACT.Identity,
                                   bias=bias_t.h[:, p:p + 1]),
                     r=[PP.b, bias_t.b], w=[dst16.b])
                csq = slice(half * 1024, (half + 1) * 1024)
                P.do("s", lambda eo, PP=PP, csq=csq, bias_t=bias_t, p=p:
                     eo.activation(sq16.h[:, csq], PP.h[:, :], ACT.Square,
                                   bias=bias_t.h[:, p:p + 1]),
                     r=[PP.b, bias_t.b], w=[sq16.b])
            ssq_rows(sq16)
            finish_rrows(is_q)
            for hh in range(2):
                hb = hh * 64
                for half in range(HLV):
                    PP = PP1 if half == 0 else PP2
                    bcast_row16(PP, hh, half)
                    cs = dslc(half * 1024, (half + 1) * 1024)
                    P.do("v", lambda eo, dst16=dst16, hb=hb, cs=cs, PP=PP:
                         eo.tensor_tensor(out=dst16.h[hb:hb + 64, cs],
                                          in0=dst16.h[hb:hb + 64, cs],
                                          in1=PP.h[hb:hb + 64, :], op=ALU.mult),
                         r=[dst16.b, PP.b], w=[dst16.b])
        for hh in range(2):
            vT_make(wv16[slot], hh, bvrow, p * 128,
                    [xn16[c] for c in range(NCH)], [SD] * NCH)
            hb = hh * 64
            hd = p * 2 + hh
            for halfb in range(2):
                for ti in range(QTH):
                    t = halfb * QTH + ti
                    PP = PP1 if ti % 2 == 0 else PP2
                    for hf2 in range(NG // 2):
                        for jj in range(2):
                            ks = dslc((hf2 * 2 + jj) * 512, (hf2 * 2 + jj + 1) * 512)
                            P.do("t", lambda eo, t=t, jj=jj, PP=PP, hb=hb,
                                 ks=ks, qt=qt, kt_=kt_:
                                 eo.matmul(PP.h[:, jj * 512:(jj + 1) * 512],
                                           qt.h[hb:hb + 64,
                                                SD + t * 128:SD + (t + 1) * 128],
                                           kt_.h[hb:hb + 64, ks],
                                           start=True, stop=True),
                                 r=[qt.b, kt_.b], w=[PP.b])
                        if (ti + hf2) % 2 == 0:
                            P.do("s", lambda eo, ti=ti, hf2=hf2, PP=PP:
                                 eo.activation(
                                     s16[ti].h[:, hf2 * 1024:(hf2 + 1) * 1024],
                                     PP.h[:, :], ACT.Identity),
                                 r=[PP.b], w=[s16[ti].b])
                        else:
                            P.do("v", lambda eo, ti=ti, hf2=hf2, PP=PP:
                                 eo.tensor_copy(
                                     s16[ti].h[:, hf2 * 1024:(hf2 + 1) * 1024],
                                     PP.h[:, :]), r=[PP.b], w=[s16[ti].b])
                P.do("v", lambda eo: eo.tensor_scalar(out=st["tauA"].h[:, :],
                     in0=st["t1"].h[:, :], scalar1=0.0, scalar2=tau0c.h[:, :],
                     op0=ALU.mult, op1=ALU.add), r=[st["t1"].b, tau0c.b],
                     w=[st["tauA"].b])
                P.do("v", lambda eo: eo.memset(st["lo"].h[:, :], -4.2), w=[st["lo"].b])
                P.do("v", lambda eo: eo.memset(st["hi"].h[:, :], 4.2), w=[st["hi"].b])
                P.do("v", lambda eo: eo.memset(st["bd"].h[:, :], 1e9), w=[st["bd"].b])
                P.do("v", lambda eo: eo.memset(st["bt"].h[:, :], 0.0), w=[st["bt"].b])
                P.do("v", lambda eo: eo.tensor_scalar(out=st["tnA"].h[:, :],
                     in0=st["tauA"].h[:, :], scalar1=-1.0, scalar2=None,
                     op0=ALU.mult), r=[st["tauA"].b], w=[st["tnA"].b])
                cur, prv = "A", "B"
                QV = QTH // 2
                NSUB = max(0, NPROBE - 2)
                for probe in range(NPROBE):
                    sub = probe < NSUB
                    tau_c, c_c = st["tau" + cur], st["c" + cur]
                    tau_p, c_p = st["tau" + prv], st["c" + prv]
                    tn_c, tn_p = st["tn" + cur], st["tn" + prv]
                    cbV, cbS = stc["c" + cur]
                    for ti in range(QV):
                        ap = (s16[ti].h[:, 0:S:4] if sub else s16[ti].h[:, 0:S])
                        nout = S // 4 if sub else S
                        P.do("v", lambda eo, ti=ti, tau_c=tau_c, c_c=c_c, ap=ap,
                             nout=nout: eo.tensor_scalar(out=dump8.h[:, 0:nout],
                                              in0=ap,
                                              scalar1=tau_c.h[:, ti:ti + 1],
                                              scalar2=None, op0=ALU.is_gt,
                                              op1=ALU.add,
                                              accum_out=c_c.h[:, ti:ti + 1]),
                             r=[s16[ti].b, tau_c.b], w=[dump8.b, cbV.b])
                    for ti in range(QV, QTH):
                        ap = (s16[ti].h[:, 0:S:4] if sub else s16[ti].h[:, 0:S])
                        nout = S // 4 if sub else S
                        P.do("s", lambda eo, ti=ti, tn_c=tn_c, c_c=c_c, ap=ap,
                             nout=nout: eo.activation(sq16.h[:, 0:nout], ap,
                                           ACT.Sign, bias=tn_c.h[:, ti:ti + 1],
                                           accum_out=c_c.h[:, ti:ti + 1]),
                             r=[s16[ti].b, tn_c.b], w=[sq16.b, cbS.b])
                    if sub:
                        P.do("v", lambda eo, c_c=c_c: eo.tensor_scalar(
                            out=c_c.h[:, 0:QV], in0=c_c.h[:, 0:QV],
                            scalar1=4.0, scalar2=None, op0=ALU.mult),
                            r=[cbV.b], w=[cbV.b])
                        P.do("v", lambda eo, c_c=c_c: eo.tensor_scalar(
                            out=c_c.h[:, QV:QTH], in0=c_c.h[:, QV:QTH],
                            scalar1=2.0, scalar2=float(S // 2), op0=ALU.mult,
                            op1=ALU.add), r=[cbS.b], w=[cbS.b])
                    else:
                        P.do("v", lambda eo, c_c=c_c: eo.tensor_scalar(
                            out=c_c.h[:, QV:QTH], in0=c_c.h[:, QV:QTH],
                            scalar1=0.5, scalar2=float(S // 2), op0=ALU.mult,
                            op1=ALU.add), r=[cbS.b], w=[cbS.b])
                    P.raw("v", lambda eo: eo.drain())
                    d1, t1, t2, t3 = st["d1"], st["t1"], st["t2"], st["t3"]
                    lo_, hi_, bt_, bd_ = st["lo"], st["hi"], st["bt"], st["bd"]

                    def U(fn, r, w):
                        P.do("v", fn, r=r, w=w)
                    U(lambda eo, c_c=c_c: eo.tensor_scalar(out=d1.h[:, :],
                      in0=c_c.h[:, :], scalar1=-float(KK), scalar2=None,
                      op0=ALU.add), [cbV.b, cbS.b], [d1.b])
                    U(lambda eo: eo.tensor_scalar(out=msk.h[:, :], in0=d1.h[:, :],
                      scalar1=0.0, scalar2=None, op0=ALU.is_ge), [d1.b], [msk.b])
                    U(lambda eo: eo.tensor_scalar(out=mskn.h[:, :], in0=d1.h[:, :],
                      scalar1=0.0, scalar2=None, op0=ALU.is_lt), [d1.b], [mskn.b])
                    U(lambda eo, tau_c=tau_c: eo.copy_predicated(lo_.h[:, :],
                      msk.h[:, :], tau_c.h[:, :]), [msk.b, tau_c.b, lo_.b], [lo_.b])
                    U(lambda eo, tau_c=tau_c: eo.copy_predicated(hi_.h[:, :],
                      mskn.h[:, :], tau_c.h[:, :]), [mskn.b, tau_c.b, hi_.b], [hi_.b])
                    if not sub:
                        U(lambda eo: eo.tensor_scalar(out=t2.h[:, :], in0=d1.h[:, :],
                          scalar1=-1.0, scalar2=None, op0=ALU.mult), [d1.b], [t2.b])
                        U(lambda eo: eo.tensor_tensor(out=t2.h[:, :], in0=t2.h[:, :],
                          in1=d1.h[:, :], op=ALU.max), [t2.b, d1.b], [t2.b])
                        U(lambda eo: eo.tensor_tensor(out=msk.h[:, :], in0=t2.h[:, :],
                          in1=bd_.h[:, :], op=ALU.is_le), [t2.b, bd_.b], [msk.b])
                        U(lambda eo, tau_c=tau_c: eo.copy_predicated(bt_.h[:, :],
                          msk.h[:, :], tau_c.h[:, :]), [msk.b, tau_c.b, bt_.b], [bt_.b])
                        U(lambda eo: eo.copy_predicated(bd_.h[:, :], msk.h[:, :],
                          t2.h[:, :]), [msk.b, t2.b, bd_.b], [bd_.b])
                    if probe < NPROBE - 1:
                        if probe == 0:
                            U(lambda eo: eo.tensor_scalar(out=t2.h[:, :],
                              in0=d1.h[:, :], scalar1=irho0c.h[:, :], scalar2=None,
                              op0=ALU.mult), [d1.b, irho0c.b], [t2.b])
                        else:
                            U(lambda eo, tau_c=tau_c, tau_p=tau_p: eo.tensor_tensor(
                              out=t1.h[:, :], in0=tau_c.h[:, :], in1=tau_p.h[:, :],
                              op=ALU.subtract), [tau_c.b, tau_p.b], [t1.b])
                            U(lambda eo, c_c=c_c, c_p=c_p: eo.tensor_tensor(
                              out=t2.h[:, :], in0=c_p.h[:, :], in1=c_c.h[:, :],
                              op=ALU.subtract),
                              [stc["c" + prv][0].b, stc["c" + prv][1].b, cbV.b, cbS.b],
                              [t2.b])
                            U(lambda eo: eo.reciprocal(t1.h[:, :], t1.h[:, :]),
                              [t1.b], [t1.b])
                            U(lambda eo: eo.tensor_tensor(out=t2.h[:, :],
                              in0=t2.h[:, :], in1=t1.h[:, :], op=ALU.mult),
                              [t2.b, t1.b], [t2.b])
                            U(lambda eo: eo.tensor_scalar(out=t2.h[:, :],
                              in0=t2.h[:, :], scalar1=20.0, scalar2=2.0e6,
                              op0=ALU.max, op1=ALU.min), [t2.b], [t2.b])
                            U(lambda eo: eo.reciprocal(t2.h[:, :], t2.h[:, :]),
                              [t2.b], [t2.b])
                            U(lambda eo: eo.tensor_tensor(out=t2.h[:, :],
                              in0=d1.h[:, :], in1=t2.h[:, :], op=ALU.mult),
                              [d1.b, t2.b], [t2.b])
                        U(lambda eo, tau_c=tau_c, tau_p=tau_p: eo.tensor_tensor(
                          out=tau_p.h[:, :], in0=tau_c.h[:, :], in1=t2.h[:, :],
                          op=ALU.add), [tau_c.b, t2.b], [tau_p.b])
                        U(lambda eo: eo.tensor_tensor(out=t1.h[:, :],
                          in0=hi_.h[:, :], in1=lo_.h[:, :], op=ALU.subtract),
                          [hi_.b, lo_.b], [t1.b])
                        U(lambda eo: eo.tensor_scalar(out=t1.h[:, :],
                          in0=t1.h[:, :], scalar1=0.05, scalar2=None,
                          op0=ALU.mult), [t1.b], [t1.b])
                        U(lambda eo: eo.tensor_tensor(out=t3.h[:, :],
                          in0=lo_.h[:, :], in1=t1.h[:, :], op=ALU.add),
                          [lo_.b, t1.b], [t3.b])
                        U(lambda eo, tau_p=tau_p: eo.tensor_tensor(
                          out=tau_p.h[:, :], in0=tau_p.h[:, :], in1=t3.h[:, :],
                          op=ALU.max), [tau_p.b, t3.b], [tau_p.b])
                        U(lambda eo: eo.tensor_tensor(out=t3.h[:, :],
                          in0=hi_.h[:, :], in1=t1.h[:, :], op=ALU.subtract),
                          [hi_.b, t1.b], [t3.b])
                        U(lambda eo, tau_p=tau_p: eo.tensor_tensor(
                          out=tau_p.h[:, :], in0=tau_p.h[:, :], in1=t3.h[:, :],
                          op=ALU.min), [tau_p.b, t3.b], [tau_p.b])
                        U(lambda eo, tau_p=tau_p, tn_p=tn_p: eo.tensor_scalar(
                          out=tn_p.h[:, :], in0=tau_p.h[:, :], scalar1=-1.0,
                          scalar2=None, op0=ALU.mult), [tau_p.b], [tn_p.b])
                        cur, prv = prv, cur
                P.do("s", lambda eo: eo.activation(th16t.h[:, :], st["bt"].h[:, :],
                     ACT.Exp), r=[st["bt"].b], w=[th16t.b])
                for ti in range(QTH):
                    t = halfb * QTH + ti
                    P.do("y", lambda eo, t=t, ti=ti: eo.dma_start(
                        out=throw.h[0:1, t * 128:(t + 1) * 128],
                        in_=th16t.h[:, ti:ti + 1]), r=[th16t.b], w=[throw.b],
                        dma=True)
                emit_av(pending[0])
                pending[0] = (hh, hb, hd, halfb, qt, kt_)
    emit_av(pending[0])
    pending[0] = None

    sp16 = s16
    for c in range(NCH):
        P.do("y", lambda eo, c=c: eo.dma_start(out=sp16[c].h[:, 0:S],
             in_=sp_scr.h[c * 128:(c + 1) * 128, :]), r=[sp_scr.b],
             w=[sp16[c].b], dma=True)

    def load_wog(oc, slot):
        load_cols(Wo, wo16[slot], oc * 128)
        load_cols(Wg, wg16[slot], oc * 128)

    load_wog(0, 0)
    P.barrier()

    for oc in range(NCH):
        slot = oc % 2
        if oc + 1 < NCH:
            load_wog(oc + 1, 1 - slot)
        for gidx in range(NG):
            qs0 = slice(gidx * 512, (gidx + 1) * 512)
            qsd = dslc(gidx * 512, (gidx + 1) * 512)
            PU = PQ1 if gidx % 2 == 0 else PQ2
            for c in range(NCH):
                P.do("t", lambda eo, c=c, qs0=qs0, PU=PU, slot=slot: eo.matmul(
                    PU.h[:, :], wo16[slot].h[:, c * 128:(c + 1) * 128],
                    sp16[c].h[:, qs0], start=(c == 0), stop=False),
                    r=[wo16[slot].b, sp16[c].b], w=[PU.b])
            P.do("t", lambda eo, PU=PU, oc=oc: eo.matmul(
                PU.h[:, :], borow.h[0:1, oc * 128:(oc + 1) * 128],
                onesP.h[0:1, 0:512], start=False, stop=True),
                r=[borow.b, onesP.b], w=[PU.b])
            for c in range(NCH):
                P.do("t", lambda eo, c=c, qsd=qsd, slot=slot: eo.matmul(
                    PQ3.h[:, :], wg16[slot].h[:, c * 128:(c + 1) * 128],
                    xn16[c].h[:, qsd], start=(c == 0), stop=(c == NCH - 1)),
                    r=[wg16[slot].b, xn16[c].b], w=[PQ3.b])
            P.do("s", lambda eo, oc=oc: eo.activation(
                tmp16b.h[:, :], PQ3.h[:, :], ACT.Sigmoid,
                bias=bias_g.h[:, oc:oc + 1]), r=[PQ3.b, bias_g.b], w=[tmp16b.b])
            P.do("v", lambda eo, oc=oc, qsd=qsd, PU=PU: eo.tensor_tensor(
                out=tmp32a.h[:, :], in0=PU.h[:, :], in1=xn16[oc].h[:, qsd],
                op=ALU.subtract), r=[PU.b, xn16[oc].b], w=[tmp32a.b])
            P.do("v", lambda eo: eo.tensor_tensor(out=tmp32a.h[:, :],
                 in0=tmp32a.h[:, :], in1=tmp16b.h[:, :], op=ALU.mult),
                 r=[tmp32a.b, tmp16b.b], w=[tmp32a.b])
            P.do("v", lambda eo, oc=oc, qsd=qsd: eo.tensor_tensor(
                out=tmp16a.h[:, :], in0=tmp32a.h[:, :], in1=xn16[oc].h[:, qsd],
                op=ALU.add), r=[tmp32a.b, xn16[oc].b], w=[tmp16a.b])
            P.do("y", lambda eo, oc=oc, gidx=gidx: eo.dma_start(
                out=s_scr.h[oc * 128:(oc + 1) * 128, gidx * 512:(gidx + 1) * 512],
                in_=tmp16a.h[:, :]), r=[tmp16a.b], w=[s_scr.b], dma=True)
    P.barrier()

    qc = s16[0:4]
    kc16 = s16[4:8]
    sqf = sq16
    qshf = q16p
    h1_16 = s16
    SHIFTOFF = [SD] * 4 + [SD - 1] * 4

    for l in range(2):
        for srcd, dstt in ((qsc_c, qscol), (ksc_c, kscol), (lng_c, lngc),
                           (lnb_c, lnbc), (b1p_c, b1c)):
            P.do("y", lambda eo, srcd=srcd, dstt=dstt, l=l: eo.dma_start(
                out=dstt.h[:, :], in_=srcd.h[l]), r=[srcd.b], w=[dstt.b], dma=True)
        P.do("g", lambda eo, l=l: eo.dma_start(out=b2row.h[:, :], in_=b2_r.h[l]),
             r=[b2_r.b], w=[b2row.b], dma=True)

        def load_wqkv(oc, slot, l=l):
            r0 = (l * 8 + oc) * 128
            src = wqkv_qk.h[r0:r0 + 128, :]
            P.do("g", lambda eo, src=src, slot=slot: eo.dma_start(
                out=wqkv16[slot].h[:, :], in_=src), r=[wqkv_qk.b],
                w=[wqkv16[slot].b], dma=True)

        load_wqkv(0, 0)
        for oc in range(8):
            slot = oc % 2
            if oc + 1 < 8:
                load_wqkv(oc + 1, 1 - slot)
            isq = oc < 4
            dst16 = qc[oc] if isq else kc16[oc - 4]
            for half in range(HLV):
                PP = PP1 if half == 0 else PP2
                for c in range(NCH):
                    base = SHIFTOFF[c]
                    for j in range(2):
                        P.do("t", lambda eo, c=c, j=j, half=half, slot=slot,
                             base=base, PP=PP: eo.matmul(
                                 PP.h[:, j * 512:(j + 1) * 512],
                                 wqkv16[slot].h[:, c * 128:(c + 1) * 128],
                                 xn16[c].h[:, base + half * 1024 + j * 512:
                                           base + half * 1024 + (j + 1) * 512],
                                 start=(c == 0), stop=(c == NCH - 1)),
                             r=[wqkv16[slot].b, xn16[c].b], w=[PP.b])
                cs = slice(half * 1024, (half + 1) * 1024)
                P.do("s", lambda eo, PP=PP, cs=cs, dst16=dst16: eo.activation(
                    dst16.h[:, cs], PP.h[:, :], ACT.Identity),
                    r=[PP.b], w=[dst16.b])
                P.do("s", lambda eo, PP=PP, cs=cs: eo.activation(
                    sqf.h[:, cs], PP.h[:, :], ACT.Square), r=[PP.b], w=[sqf.b])
            ssq_rows(sqf)
            finish_rrows(False)
            for hh in range(2):
                hb = hh * 64
                for half in range(HLV):
                    PP = PP1 if half == 0 else PP2
                    bcast_row16(PP, hh, half)
                    cs = slice(half * 1024, (half + 1) * 1024)
                    P.do("v", lambda eo, dst16=dst16, hb=hb, cs=cs, PP=PP:
                         eo.tensor_tensor(out=dst16.h[hb:hb + 64, cs],
                                          in0=dst16.h[hb:hb + 64, cs],
                                          in1=PP.h[hb:hb + 64, :], op=ALU.mult),
                         r=[dst16.b, PP.b], w=[dst16.b])

        for tiles, aidx, scol in ((qc, 0, qscol), (kc16, 2, kscol)):
            for qtr in range(NG):
                q0 = qtr * 512
                P.do("y", lambda eo, aidx=aidx, q0=q0: eo.dma_start(
                    out=rot2[0].h[:, :], in_=rotT.h[aidx][:, q0:q0 + 512]),
                    r=[rotT.b], w=[rot2[0].b], dma=True)
                P.do("y", lambda eo, aidx=aidx, q0=q0: eo.dma_start(
                    out=rot2[1].h[:, :], in_=rotT.h[aidx + 1][:, q0:q0 + 512]),
                    r=[rotT.b], w=[rot2[1].b], dma=True)
                for c4 in range(4):
                    tl = tiles[c4]
                    qsl = slice(q0, q0 + 512)
                    P.do("v", lambda eo, tl=tl, qsl=qsl, scol=scol: eo.tensor_scalar(
                        out=sqf.h[:, qsl], in0=tl.h[:, qsl], scalar1=scol.h[:, :],
                        scalar2=None, op0=ALU.mult), r=[tl.b, scol.b], w=[sqf.b])
                    for b0 in (0, 64):
                        P.do("y", lambda eo, b0=b0, qsl=qsl: eo.dma_start(
                            out=qshf.h[b0:b0 + 32, qsl],
                            in_=sqf.h[b0 + 32:b0 + 64, qsl]),
                            r=[sqf.b], w=[qshf.b], dma=True)
                        P.do("y", lambda eo, b0=b0, qsl=qsl: eo.dma_start(
                            out=qshf.h[b0 + 32:b0 + 64, qsl],
                            in_=sqf.h[b0:b0 + 32, qsl]),
                            r=[sqf.b], w=[qshf.b], dma=True)
                    P.do("v", lambda eo, qsl=qsl: eo.tensor_tensor(
                        out=sqf.h[:, qsl], in0=sqf.h[:, qsl], in1=rot2[0].h[:, :],
                        op=ALU.mult), r=[sqf.b, rot2[0].b], w=[sqf.b])
                    P.do("v", lambda eo, qsl=qsl: eo.tensor_tensor(
                        out=qshf.h[:, qsl], in0=qshf.h[:, qsl], in1=rot2[1].h[:, :],
                        op=ALU.mult), r=[qshf.b, rot2[1].b], w=[qshf.b])
                    P.do("v", lambda eo, tl=tl, qsl=qsl: eo.tensor_tensor(
                        out=tl.h[:, qsl], in0=sqf.h[:, qsl], in1=qshf.h[:, qsl],
                        op=ALU.add), r=[sqf.b, qshf.b], w=[tl.b])
        otile = qc
        for h in range(HF):
            hslot = h % 2
            r0 = (l * HF + h) * 128
            src = wqkv_v.h[r0:r0 + 128, :]
            P.do("g", lambda eo, src=src, hslot=hslot: eo.dma_start(
                out=fwo16[hslot].h[:, :], in_=src), r=[wqkv_v.b],
                w=[fwo16[hslot].b], dma=True)
            for t in range(NQT):
                PQ = PQ1 if t % 2 == 0 else PQ2
                for c in range(NCH):
                    base = SHIFTOFF[c]
                    P.do("t", lambda eo, t=t, c=c, PQ=PQ, base=base, hslot=hslot:
                         eo.matmul(PQ.h[:, 0:64],
                                   xn16[c].h[:, base + t * 128:base + (t + 1) * 128],
                                   fwo16[hslot].h[:, c * 64:(c + 1) * 64],
                                   start=(c == 0), stop=(c == NCH - 1)),
                         r=[xn16[c].b, fwo16[hslot].b], w=[PQ.b])
                eng = "v" if t % 2 == 0 else "s"
                if eng == "v":
                    P.do("v", lambda eo, hslot=hslot, t=t, PQ=PQ: eo.tensor_copy(
                        vhat[hslot][t].h[:, 0:DH], PQ.h[:, 0:64]),
                        r=[PQ.b], w=[vhat[hslot][t].b])
                else:
                    P.do("s", lambda eo, hslot=hslot, t=t, PQ=PQ: eo.activation(
                        vhat[hslot][t].h[:, 0:DH], PQ.h[:, 0:64],
                        ACT.Identity), r=[PQ.b], w=[vhat[hslot][t].b])
            cq = h // 2
            b0 = (h % 2) * 64
            for w in range(NW):
                kts = list(range(max(0, 4 * (w - 1)), 4 * (w + 1)))
                first, last = kts[0], kts[-1]
                def simk(i, kt, w=w, b0=b0, cq=cq):
                    PQ = PQ1 if i % 2 == 0 else PQ2
                    P.do("t", lambda eo, kt=kt, w=w, PQ=PQ, b0=b0, cq=cq: eo.matmul(
                        PQ.h[:, :], kc16[cq].h[b0:b0 + 64, kt * 128:(kt + 1) * 128],
                        qc[cq].h[b0:b0 + 64, w * 512:(w + 1) * 512],
                        start=True, stop=True), r=[kc16[cq].b, qc[cq].b], w=[PQ.b])
                simk(0, kts[0])
                for i, kt in enumerate(kts):
                    PQ = PQ1 if i % 2 == 0 else PQ2
                    yt = yT[i % 3]
                    off = (kt - 4 * w) * 128
                    if i + 1 < len(kts):
                        simk(i + 1, kts[i + 1])
                    iv = (off + 512) // 128
                    P.do("v", lambda eo, PQ=PQ, iv=iv: eo.tensor_tensor(
                        out=tmp32a.h[:, :], in0=PQ.h[:, :],
                        in1=madd.h[:, iv * 512:(iv + 1) * 512], op=ALU.add),
                        r=[PQ.b, madd.b], w=[tmp32a.b])
                    P.do("s", lambda eo, yt=yt: eo.activation(
                        yt.h[:, :], tmp32a.h[:, :], ACT.Exp, scale=8.0),
                        r=[tmp32a.b], w=[yt.b])
                    P.do("t", lambda eo, kt=kt, h=h, first=first, last=last,
                         hslot=hslot, yt=yt: eo.matmul(
                             PQ3.h[0:DH + 1, :], vhat[hslot][kt].h[:, :], yt.h[:, :],
                             start=(kt == first), stop=(kt == last)),
                         r=[vhat[hslot][kt].b, yt.b], w=[PQ3.b])

                def store(h=h, w=w):
                    P.do("y", lambda eo: eo.dma_start(
                        out=otile[h // 2].h[(h % 2) * 64:(h % 2) * 64 + 64,
                                            w * 512:(w + 1) * 512],
                        in_=tmp16a.h[0:64, :]), r=[tmp16a.b],
                        w=[otile[h // 2].b], dma=True)
                av_divide(store)
        P.barrier()
        def load_fwo(oc, slot, l=l):
            r0 = (l * 8 + oc) * 128
            src = fwo.h[r0:r0 + 128, :]
            P.do("g", lambda eo, src=src, slot=slot: eo.dma_start(
                out=fwo16[slot].h[:, :], in_=src), r=[fwo.b],
                w=[fwo16[slot].b], dma=True)

        load_fwo(0, 0)
        for oc in range(NCH):
            slot = oc % 2
            if oc + 1 < NCH:
                load_fwo(oc + 1, 1 - slot)
            for gidx in range(NG):
                qsd = dslc(gidx * 512, (gidx + 1) * 512)
                PU = PQ1 if gidx % 2 == 0 else PQ2
                for c in range(4):
                    P.do("t", lambda eo, c=c, gidx=gidx, PU=PU, slot=slot:
                         eo.matmul(PU.h[:, :],
                                   fwo16[slot].h[:, c * 128:(c + 1) * 128],
                                   otile[c].h[:, gidx * 512:(gidx + 1) * 512],
                                   start=(c == 0), stop=(c == 3)),
                         r=[fwo16[slot].b, otile[c].b], w=[PU.b])
                P.do("v", lambda eo, oc=oc, qsd=qsd, PU=PU: eo.tensor_tensor(
                    out=xn16[oc].h[:, qsd], in0=PU.h[:, :], in1=xn16[oc].h[:, qsd],
                    op=ALU.add), r=[PU.b, xn16[oc].b], w=[xn16[oc].b])
        P.barrier()
        ln_stats([(xn16[c], slice(SHIFTOFF[c], SHIFTOFF[c] + S))
                  for c in range(NCH)])
        ln_rows()
        for half in range(HLV):
            bcast_to(PP1, rstdrow16, half)
            bcast_to(PP2, murow16, half)
            for c in range(NCH):
                base = SHIFTOFF[c]
                for j in range(2):
                    csx = slice(base + half * 1024 + j * 512,
                                base + half * 1024 + (j + 1) * 512)
                    cs = slice(half * 1024 + j * 512, half * 1024 + (j + 1) * 512)
                    ps = slice(j * 512, (j + 1) * 512)
                    P.do("v", lambda eo, c=c, csx=csx, ps=ps: eo.tensor_tensor(
                        out=wtmp2.h[:, :], in0=xn16[c].h[:, csx], in1=PP1.h[:, ps],
                        op=ALU.mult), r=[xn16[c].b, PP1.b], w=[wtmp2.b])
                    P.do("v", lambda eo, ps=ps: eo.tensor_tensor(
                        out=wtmp2.h[:, :], in0=wtmp2.h[:, :], in1=PP2.h[:, ps],
                        op=ALU.subtract), r=[wtmp2.b, PP2.b], w=[wtmp2.b])
                    P.do("v", lambda eo, c=c, cs=cs: eo.tensor_scalar(
                        out=h1_16[c].h[:, cs], in0=wtmp2.h[:, :],
                        scalar1=lngc.h[:, c:c + 1], scalar2=lnbc.h[:, c:c + 1],
                        op0=ALU.mult, op1=ALU.add),
                        r=[wtmp2.b, lngc.b, lnbc.b], w=[h1_16[c].b])
        P.barrier()
        w2half = []
        for k in range(NK2):
            wt = wbig[k // 2]
            ap = wt.h[:, (k % 2) * 512:(k % 2) * 512 + 512]
            w2half.append((wt, ap))

        w1a2 = [w1ab, wbig[9]]
        w1g2 = [w1gb, wbig[10]]

        def load_w1(m, gate, buf, l=l):
            mm_ = m + NK2 if gate else m
            dst = (w1g2 if gate else w1a2)[buf]
            r0 = (l * NM1 + mm_) * 128
            src = w1p.h[r0:r0 + 128, :]
            P.do("g", lambda eo, src=src, dst=dst: eo.dma_start(
                out=dst.h[:, :], in_=src), r=[w1p.b], w=[dst.b], dma=True)

        w1order = [(gh_, m_) for gh_ in range(NG // 2) for m_ in range(NK2)]
        slots6 = [(PQ1.h[:, :], PQ1.b), (PQ2.h[:, :], PQ2.b),
                  (PP1.h[:, 0:512], PP1s[0]), (PP1.h[:, 512:1024], PP1s[1]),
                  (PP2.h[:, 0:512], PP2s[0]), (PP2.h[:, 512:1024], PP2s[1])]
        w1rot = [0]
        load_w1(w1order[0][1], False, 0)
        load_w1(w1order[0][1], True, 0)
        for ghalf in range(NG // 2):
            for m in range(NK2):
                i1 = ghalf * NK2 + m
                if i1 + 1 < len(w1order):
                    load_w1(w1order[i1 + 1][1], False, (i1 + 1) % 2)
                    load_w1(w1order[i1 + 1][1], True, (i1 + 1) % 2)
                wa, wg = w1a2[i1 % 2], w1g2[i1 % 2]
                for gi in range(2):
                    gidx = ghalf * 2 + gi
                    qs = slice(gidx * 512, (gidx + 1) * 512)
                    pa, pba = slots6[w1rot[0] % 6]
                    pg, pbg = slots6[(w1rot[0] + 1) % 6]
                    w1rot[0] += 2
                    for c in range(NCH):
                        P.do("t", lambda eo, c=c, qs=qs, pa=pa, wa=wa: eo.matmul(
                            pa, wa.h[:, c * 128:(c + 1) * 128],
                            h1_16[c].h[:, qs], start=(c == 0), stop=(c == NCH - 1)),
                            r=[wa.b, h1_16[c].b], w=[pba])
                    for c in range(NCH):
                        P.do("t", lambda eo, c=c, qs=qs, pg=pg, wg=wg: eo.matmul(
                            pg, wg.h[:, c * 128:(c + 1) * 128],
                            h1_16[c].h[:, qs], start=(c == 0), stop=(c == NCH - 1)),
                            r=[wg.b, h1_16[c].b], w=[pbg])
                    P.do("s", lambda eo, m=m, pa=pa: eo.activation(
                        tmp16b.h[:, :], pa, ACT.Identity,
                        bias=b1c.h[:, m:m + 1]), r=[pba, b1c.b], w=[tmp16b.b])
                    P.do("s", lambda eo, m=m, pg=pg: eo.activation(
                        gel16.h[:, :], pg, ACT.Gelu,
                        bias=b1c.h[:, NK2 + m:NK2 + m + 1]),
                        r=[pbg, b1c.b], w=[gel16.b])
                    P.do("v", lambda eo, m=m, gi=gi: eo.tensor_tensor(
                        out=h2t[m].h[:, gi * 512:(gi + 1) * 512],
                        in0=tmp16b.h[:, :], in1=gel16.h[:, :], op=ALU.mult),
                        r=[tmp16b.b, gel16.b], w=[h2t[m].b])
            for ochalf in range(2):
                for k in range(NK2):
                    wt, wap = w2half[k]
                    stg = tmp32a if k % 2 == 0 else wtmp2
                    src = w2p.h[l][k * 128:(k + 1) * 128,
                                   ochalf * 512:(ochalf + 1) * 512]
                    P.do("y", lambda eo, src=src, stg=stg: eo.dma_start(
                        out=stg.h[:, :], in_=src), r=[w2p.b], w=[stg.b],
                        dma=True)
                    P.do("v", lambda eo, wap=wap, stg=stg: eo.tensor_copy(
                        wap, stg.h[:, :]), r=[stg.b], w=[wt.b])
                for gi in range(2):
                    gidx = ghalf * 2 + gi
                    qsd = dslc(gidx * 512, (gidx + 1) * 512)
                    pslices = [(PP1.h[:, 0:512], PP1s[0]),
                               (PP1.h[:, 512:1024], PP1s[1]),
                               (PP2.h[:, 0:512], PP2s[0]),
                               (PP2.h[:, 512:1024], PP2s[1])]
                    for k in range(NK2):
                        wt, wap = w2half[k]
                        for j4 in range(4):
                            pap, pb = pslices[j4]
                            P.do("t", lambda eo, k=k, j4=j4, wap=wap, pap=pap,
                                 gi=gi: eo.matmul(
                                     pap, wap[:, j4 * 128:(j4 + 1) * 128],
                                     h2t[k].h[:, gi * 512:(gi + 1) * 512],
                                     start=(k == 0), stop=False),
                                 r=[wt.b, h2t[k].b], w=[pb])
                    for j4 in range(4):
                        oc = ochalf * 4 + j4
                        pap, pb = pslices[j4]
                        P.do("t", lambda eo, oc=oc, pap=pap: eo.matmul(
                            pap, b2row.h[0:1, oc * 128:(oc + 1) * 128],
                            onesP.h[64:65, 0:512], start=False, stop=True),
                            r=[b2row.b, onesP.b], w=[pb])
                    for j4 in range(4):
                        oc = ochalf * 4 + j4
                        pap, pb = pslices[j4]
                        P.do("v", lambda eo, oc=oc, qsd=qsd, pap=pap:
                             eo.tensor_tensor(out=xn16[oc].h[:, qsd], in0=pap,
                                              in1=xn16[oc].h[:, qsd], op=ALU.add),
                             r=[pb, xn16[oc].b], w=[xn16[oc].b])
        P.barrier()

    fin_x = [(wtmp2.h[:, :], wtmp2.b), (RW[0].h[:, 0:512], RW[0].b)]
    fin_s = [(tmp16a.h[:, :], tmp16a.b), (h2t[0].h[:, 0:512], h2t[0].b)]
    fin_u = [(tmp16b.h[:, :], tmp16b.b), (gel16.h[:, :], gel16.b)]
    fin_m = [(tmp32a.h[:, :], tmp32a.b), (RW[1].h[:, 0:512], RW[1].b)]
    fin_o = [(wtmp2.h[:, :], wtmp2.b), (RW[0].h[:, 512:1024], RW[0].b)]
    it = 0
    for c in range(NCH):
        for gidx in range(NG):
            pi = it % 2
            it += 1
            xa, xb = fin_x[pi]
            sa, sb_ = fin_s[pi]
            ua, ub = fin_u[pi]
            ma, mb = fin_m[pi]
            oa, ob = fin_o[pi]
            qsd = dslc(gidx * 512, (gidx + 1) * 512)
            qs0 = slice(gidx * 512, (gidx + 1) * 512)
            P.do("y", lambda eo, c=c, qs0=qs0, xa=xa: eo.dma_start(
                out=xa, in_=xT.h[c * 128:(c + 1) * 128, qs0]),
                r=[xT.b], w=[xb], dma=True)
            P.do("y", lambda eo, c=c, qs0=qs0, sa=sa: eo.dma_start(
                out=sa, in_=s_scr.h[c * 128:(c + 1) * 128, qs0]),
                r=[s_scr.b], w=[sb_], dma=True)
            P.do("s", lambda eo, c=c, qsd=qsd, ua=ua: eo.activation(
                ua, xn16[c].h[:, qsd], ACT.Silu),
                r=[xn16[c].b], w=[ub])
            P.do("v", lambda eo, ua=ua, sa=sa, ma=ma: eo.tensor_tensor(
                out=ma, in0=ua, in1=sa, op=ALU.mult),
                r=[ub, sb_], w=[mb])
            P.do("v", lambda eo, xa=xa, ma=ma, oa=oa: eo.tensor_tensor(
                out=oa, in0=xa, in1=ma, op=ALU.add),
                r=[xb, mb], w=[ob])
            P.do("y", lambda eo, c=c, qs0=qs0, oa=oa: eo.dma_start(
                out=out_T.h[c * 128:(c + 1) * 128, qs0], in_=oa),
                r=[ob], w=[out_T.b], dma=True)
    P.barrier()

    with nc.Block() as block:
        @block.tensor
        def _(te):
            for clo in P.q["t"]:
                clo()

        @block.vector
        def _(ve):
            for clo in P.q["v"]:
                clo()

        @block.scalar
        def _(se):
            for clo in P.q["s"]:
                clo()

        @block.sync
        def _(ye):
            for clo in P.q["y"]:
                clo()

        @block.gpsimd
        def _(ge):
            for clo in P.q["g"]:
                clo()
    es.close()
    CFG['counts'] = dict(P.cnt)
    return nc


def _host_constants(S):
    DH_ = 64
    WINDOW = 512
    half_idx = np.arange(0, DH_, 2, dtype=np.float32)
    inv_freq = 1.0 / 10000.0 ** (half_idx / DH_)
    t = np.arange(S, dtype=np.float32)
    freqs = t[:, None] * inv_freq[None, :]
    freqs = np.concatenate([freqs, freqs], axis=-1)
    base = (half_idx + 0.4 * DH_) / (1.4 * DH_)
    power = (t - S // 2) / float(WINDOW // 2)
    scale = base[None, :] ** power[:, None]
    scale = np.concatenate([scale, scale], axis=-1)
    cos, sin = np.cos(freqs), np.sin(freqs)
    sgn = np.concatenate([-np.ones(32, np.float32), np.ones(32, np.float32)])
    AQ = (cos * scale).T
    BQ = (sin * scale).T * sgn[:, None]
    AK = (cos / scale).T
    BK = (sin / scale).T * sgn[:, None]
    rotT = np.stack([np.tile(a, (2, 1)) for a in (AQ, BQ, AK, BK)]).astype(np.float16)
    import ml_dtypes as _mld
    E2 = np.zeros((128, 33), _mld.float8_e4m3fn)
    E2[0:64, 0] = 1.0
    E2[64:128, 32] = 1.0
    import ml_dtypes
    kk = np.arange(128)[:, None]
    qq = np.arange(512)[None, :]
    madd = np.zeros((128, 8 * 512), np.float32)
    for iv in range(8):
        off = iv * 128 - 512
        valid = (qq >= kk + off) & (qq <= kk + off + 512)
        madd[:, iv * 512:(iv + 1) * 512] = np.where(valid, 0.0, -28000.0)
    madd8 = madd.astype(ml_dtypes.float8_e5m2)
    return dict(rotT=rotT, E2_d=E2, madd_d=madd8)


def kernel(**inputs):
    S = CFG["S"]
    D = 1024
    HIDP = 2816
    HID = 2730
    f32 = np.float32

    x = np.ascontiguousarray(np.asarray(inputs["L0_attn_x"], f32))
    B = x.shape[0]

    def g(k):
        return np.ascontiguousarray(np.asarray(inputs[k], f32))

    w1 = g("L0_ffn_w1")
    b1 = g("L0_ffn_b1")
    w2 = g("L0_ffn_w2")
    w1p = np.zeros((2, D, 2 * HIDP), f32)
    w1p[:, :, 0:HID] = w1[:, :, 0:HID]
    w1p[:, :, HIDP:HIDP + HID] = w1[:, :, HID:2 * HID]
    b1p = np.zeros((2, 2 * HIDP), f32)
    b1p[:, 0:HID] = b1[:, 0:HID]
    b1p[:, HIDP:HIDP + HID] = b1[:, HID:2 * HID]
    w2p = np.zeros((2, HIDP, D), f32)
    w2p[:, 0:HID, :] = w2

    def cols(a):
        return np.ascontiguousarray(a.reshape(-1, 128).T)

    def ptiles(a, tw):
        K, M = a.shape
        nch, nt = K // 128, M // tw
        return np.ascontiguousarray(a.reshape(nch, 128, nt, tw)
                                    .transpose(2, 1, 0, 3)
                                    .reshape(nt, 128, nch * tw))

    shared = dict(
        Wq=ptiles(g("L0_attn_Wq"), 128).reshape(-1, 1024),
        Wk=ptiles(g("L0_attn_Wk"), 128).reshape(-1, 1024),
        Wv=ptiles(g("L0_attn_Wv"), 128).reshape(-1, 1024),
        Wo=ptiles(g("L0_attn_Wo"), 128).reshape(-1, 1024),
        Wg=ptiles(g("L0_attn_Wg"), 128).reshape(-1, 1024),
        bqk_c=np.stack([cols(g("L0_attn_bq")), cols(g("L0_attn_bk"))]),
        bg_c=cols(g("L0_attn_bg")),
        bo_r=np.ascontiguousarray(g("L0_attn_bo").reshape(1, D)),
        bv_r=np.ascontiguousarray(g("L0_attn_bv").reshape(1, D)),
        Wt_r=cols(g("L0_attn_Wt").reshape(-1)),
        bt_s=np.ascontiguousarray(g("L0_attn_bt").reshape(1, 1)),
        wqkv_qk=np.concatenate([ptiles(wq_[:, 0:1024], 128)
                                for wq_ in g("L0_ffn_wqkv")]).reshape(-1, 1024),
        wqkv_v=np.concatenate([ptiles(wq_[:, 1024:1536], 64)
                               for wq_ in g("L0_ffn_wqkv")]).reshape(-1, 512),
        fwo=np.concatenate([ptiles(f_, 128)
                            for f_ in g("L0_ffn_wo")]).reshape(-1, 512),
        w1p=np.concatenate([ptiles(w1p[i], 128)
                            for i in range(2)]).reshape(-1, 1024),
        b1p_c=np.stack([cols(b1p[i]) for i in range(2)]),
        w2p=w2p,
        b2_r=np.ascontiguousarray(g("L0_ffn_b2").reshape(2, 1, D)),
        lng_c=np.stack([cols(g("L0_ffn_ln_g")[i]) for i in range(2)]),
        lnb_c=np.stack([cols(g("L0_ffn_ln_b")[i]) for i in range(2)]),
        qsc_c=np.ascontiguousarray(
            np.tile(g("L0_ffn_q_scale"), (1, 2)).reshape(2, 128, 1)),
        ksc_c=np.ascontiguousarray(
            np.tile(g("L0_ffn_k_scale"), (1, 2)).reshape(2, 128, 1)),
        **_host_constants(S),
    )
    nc = build()
    in_maps = []
    for b in range(B):
        m = dict(shared)
        m["xT"] = np.ascontiguousarray(x[b].T)
        in_maps.append(m)
    res = run_bass_kernel_spmd(nc, in_maps, core_ids=list(range(B)))
    out = np.stack([np.asarray(r["out_T"]).T for r in res.results])
    return np.ascontiguousarray(out.astype(np.float32))
